# version 11
# baseline (speedup 1.0000x reference)
import math
import numpy as np
import concourse.bass as bass
import concourse.mybir as mybir
from concourse.bass_utils import run_bass_kernel_spmd

F32 = mybir.dt.float32
BF16 = mybir.dt.bfloat16
I32 = mybir.dt.int32
U8 = mybir.dt.uint8
AF = mybir.ActivationFunctionType
ALU = mybir.AluOpType
AX = mybir.AxisListType

T = 2048
HID = 1024
H, R, D = 8, 2, 128
NH = H * R
NB = T // 128
NEXP = 16
OMEGA_SCALE = 2.0 * math.pi / (512 * 16.0)
ISQD = 1.0 / math.sqrt(D)
TWO_PI = float(np.float32(2.0 * math.pi))
INV_2PI = float(np.float32(1.0 / (2.0 * math.pi)))

N_PAIRS = 8
ARENA_B = 176 * 1024


def spans(c0, c1, bank=512):
    out = []
    c = c0
    while c < c1:
        e = min(c1, (c // bank + 1) * bank)
        out.append((c, e - c))
        c = e
    return out


class Prog:
    CENG = ("tensor", "vector", "scalar", "pool")
    DENG = ("sdma", "gdma")
    ENGS = CENG + DENG
    NSEM = 6

    def __init__(self):
        self.groups = {e: [] for e in self.ENGS}
        self.count = {e: 0 for e in self.ENGS}
        self.sem_uses = {}
        self.writer = {}
        self.readers = {}
        self.vclock = {e: {} for e in self.ENGS}
        self.drained = {}
        self.barrier_snap = None

    def _token_for(self, eng):
        if eng in self.CENG:
            return (eng, eng, self.count[eng] + 1)
        j = self.count[eng] % self.NSEM
        sk = f"{eng}{j}"
        return (eng, sk, (self.sem_uses.get(sk, 0) + 1) * 16)

    def emit(self, eng, fn, reads=(), writes=(), tag="", seq=False):
        deps = {}
        own_dep = [0]
        def add(tok):
            de, sk, dv = tok
            if de == eng and de in self.CENG:
                if de == "tensor":
                    return
                own_dep[0] = max(own_dep[0], dv)
                return
            if deps.get(sk, 0) < dv:
                deps[sk] = dv
        if self.barrier_snap:
            for tok in self.barrier_snap:
                add(tok)
        for k in reads:
            w = self.writer.get(k)
            if w:
                add(w)
        for k in writes:
            w = self.writer.get(k)
            if w:
                add(w)
            for rdep in self.readers.get(k, ()):
                add(rdep)
        tok = self._token_for(eng)
        if eng in self.DENG:
            sk = tok[1]
            prev = self.sem_uses.get(sk, 0)
            if prev > 0:
                if deps.get(sk, 0) < prev * 16:
                    deps[sk] = prev * 16
            self.sem_uses[sk] = prev + 1
        vc = self.vclock[eng]
        waits = []
        for sk, dv in deps.items():
            if vc.get(sk, 0) >= dv:
                continue
            vc[sk] = dv
            waits.append((sk, dv))
        drain = False
        if own_dep[0] > self.drained.get(eng, 0):
            drain = True
            self.drained[eng] = self.count[eng]
        self.count[eng] += 1
        self.groups[eng].append((waits, fn, tag, drain, seq, tok[1], tok[2]))
        me = tok
        for k in writes:
            self.writer[k] = me
            self.readers[k] = []
        for k in reads:
            self.readers.setdefault(k, []).append(me)
        return me

    def barrier(self):
        snap = []
        for e in self.CENG:
            if self.count[e]:
                snap.append((e, e, self.count[e]))
        for e in self.DENG:
            for j in range(self.NSEM):
                sk = f"{e}{j}"
                u = self.sem_uses.get(sk, 0)
                if u:
                    snap.append((e, sk, u * 16))
        self.barrier_snap = snap


class Arena:
    def __init__(self, ap_u8, nbytes):
        self.ap = ap_u8
        self.nbytes = nbytes
        self.off = 0

    def mark(self):
        return self.off

    def reset(self):
        self.off = 0

    def release(self, m):
        self.off = m

    def alloc(self, cols, dtype, parts=128):
        esz = 2 if dtype in (BF16,) else 4
        start = (self.off + 31) // 32 * 32
        self.off = start + cols * esz
        assert self.off <= self.nbytes, f"arena overflow {self.off}"
        view = self.ap.bitcast(dtype)
        v = view[:, start // esz:start // esz + cols]
        if parts != 128:
            v = v[0:parts, :]
        return v


def build_nc(debug_taps=(), n_pairs=N_PAIRS):
    nc = bass.Bass("TRN2", target_bir_lowering=False, debug=False)

    def din(name, shape, dtype=F32):
        return nc.dram_tensor(name, list(shape), dtype, kind="ExternalInput").ap()

    def dscr(name, shape, dtype):
        kind = "ExternalOutput" if name in debug_taps else "Internal"
        return nc.dram_tensor(name, list(shape), dtype, kind=kind).ap()

    x_in = din("x", [T, HID])
    WqT = din("WqT", [HID, HID]); WkT = din("WkT", [HID, HID]); WvT = din("WvT", [HID, HID])
    WoT = din("WoT", [2 * HID, HID])
    wA1T = din("wA1T", [HID, 32]); wA2T = din("wA2T", [32, 2 * HID])
    wB1T = din("wB1T", [HID, 32]); wB2T = din("wB2T", [32, 2 * HID])
    convAp = din("convAp", [128, NB * 3]); convBp = din("convBp", [128, NB * 3])
    btwT = din("btwT", [HID, 16]); btb = din("btb", [16, 1])
    phi_in = din("phi", [2, 1]); omr_in = din("omr", [2, 1])
    colemb = din("colemb", [16, HID])
    gw1 = din("gw1", [HID, 512]); gb1p = din("gb1p", [128, 4])
    gw2 = din("gw2", [512, 16]); gb2 = din("gb2", [1, 16])
    Wcols = din("Wcols", [NEXP, HID, 256])
    bcolsp = din("bcolsp", [128, 2 * NEXP])
    projw = din("projw", [4 * HID, HID])
    projb = din("projb", [1, HID])
    lng = din("lng", [1, HID]); lnb = din("lnb", [1, HID])
    identin = din("identin", [128, 128])
    maskSin = din("maskSin", [128, 128])
    maskKin = din("maskKin", [128, 128])
    pos4in = din("pos4in", [4, T])
    selcin = din("selcin", [16, 16 * 128])
    sel4in = din("sel4in", [4, 4 * 128])
    mbigin = din("mbigin", [128, 128])
    dup24in = din("dup24in", [2, 4])
    phsh4in = din("phsh4in", [4, 1])

    out = nc.dram_tensor("out", [T, HID], F32, kind="ExternalOutput").ap()

    xn_scr = dscr("xn_scr", [T, HID], F32)
    xnT_scr = dscr("xnT_scr", [HID, T], BF16)
    qT_scr = dscr("qT_scr", [HID, T], BF16)
    kT_scr = dscr("kT_scr", [HID, T], BF16)
    v_scr = dscr("v_scr", [T, HID], BF16)
    W_scr = dscr("W_scr", [2 * HID, T], BF16)
    o_scr = dscr("o_scr", [NH * 128, T], BF16)
    attn_scr = dscr("attn_scr", [T, HID], F32)
    wc_scr = dscr("wc_scr", [4 * HID, T], BF16)
    beta_scr = dscr("beta_scr", [128, NB * 16], F32)
    wts_scr = dscr("wts_scr", [128, NB * 16], F32)
    y_scr = dscr("y_scr", [T, HID], F32)
    bc_scr = dscr("bc_scr", [3 * 128, HID], F32)

    P = Prog()

    from contextlib import ExitStack
    with ExitStack() as ST:
        def sb(name, shape, dtype):
            return ST.enter_context(nc.sbuf_tensor(name, shape, dtype))
        arena_t = sb("arena", [128, ARENA_B], U8)
        identbf = sb("identbf", [128, 128], BF16)
        maskSs = sb("maskSs", [128, 128], F32)
        maskKs = sb("maskKs", [128, 128], F32)
        onescol = sb("onescol", [128, 1], BF16)
        onesrow = sb("onesrow", [1, 128], BF16)
        selcs = sb("selcs", [16, 16 * 128], BF16)
        sel4s = sb("sel4s", [4, 4 * 128], BF16)
        mbigs = sb("mbigs", [128, 128], BF16)
        dup24s = sb("dup24s", [2, 4], F32)
        phsh4s = sb("phsh4s", [4, 1], F32)
        sm2 = sb("sm2", [2, 8], F32)
        sm4 = sb("sm4", [4, 8], F32)
        btbs = sb("btbs", [16, 1], F32)
        beta_tm = sb("beta_tm", [128, NB * 16], F32)
        betaneg_tm = sb("betaneg_tm", [128, NB * 16], F32)
        rnorm_tm = sb("rnorm_tm", [128, NB], F32)
        svar = sb("svar", [128, NB], F32)
        srstd = sb("srstd", [128, NB], F32)
        wts_tm = sb("wts_tm", [128, NB * 16], F32)
        eps_ln = sb("eps_ln", [128, 1], F32)
        eps24 = sb("eps24", [128, 1], F32)
        gb2bc = sb("gb2bc", [128, 16], F32)
        red1 = sb("red1", [128, 2], F32)
        red2 = sb("red2", [128, 2], F32)
        red3 = sb("red3", [128, 2], F32)
        t16a = sb("t16a", [128, 16], F32)
        t16b = sb("t16b", [128, 16], F32)
        t16c = sb("t16c", [128, 16], F32)
        psA = ST.enter_context(nc.psum_tensor("psA", [128, 2048], F32))
        psB = ST.enter_context(nc.psum_tensor("psB", [128, 2048], F32))
        smt = ST.enter_context(nc.semaphore("smt"))
        smv = ST.enter_context(nc.semaphore("smv"))
        sms = ST.enter_context(nc.semaphore("sms"))
        smp = ST.enter_context(nc.semaphore("smp"))
        dma_sems = {}
        for qe in ("sdma", "gdma"):
            for j in range(Prog.NSEM):
                dma_sems[f"{qe}{j}"] = ST.enter_context(nc.semaphore(f"{qe}{j}"))
        block = ST.enter_context(nc.Block())

        AR = Arena(arena_t[:, :], ARENA_B)
        sems = {"tensor": smt, "vector": smv, "scalar": sms, "pool": smp}
        sems.update(dma_sems)

        def pk(which, c0, c1):
            return tuple(f"ps{which}:{b}" for b in range(c0 // 512, (c1 + 511) // 512))

        emit = P.emit

        def dma(eng, dst, src, reads, writes, tag=""):
            emit(eng, lambda e, d=dst, s=src: [e.dma_start(out=d, in_=s)],
                 reads=reads, writes=writes, tag=tag)

        dma("gdma", identbf[:], identin[:, :], ["in"], ["identbf"])
        dma("sdma", maskSs[:], maskSin[:, :], ["in"], ["maskSs"])
        dma("sdma", maskKs[:], maskKin[:, :], ["in"], ["maskKs"])
        dma("gdma", selcs[:], selcin[:, :], ["in"], ["selcs"])
        dma("gdma", sel4s[:], sel4in[:, :], ["in"], ["sel4s"])
        dma("gdma", mbigs[:], mbigin[:, :], ["in"], ["mbigs"])
        dma("sdma", dup24s[:], dup24in[:, :], ["in"], ["dup24s"])
        dma("sdma", phsh4s[:], phsh4in[:, :], ["in"], ["phsh4s"])
        dma("sdma", btbs[:], btb[:, :], ["in"], ["btbs"])
        dma("sdma", sm2[0:2, 0:1], phi_in[:, :], ["in"], ["phi"])
        dma("sdma", sm2[0:2, 1:2], omr_in[:, :], ["in"], ["omr"])
        emit("vector", lambda e: [
            e.memset(onescol[:], 1.0), e.memset(onesrow[:], 1.0),
            e.memset(eps_ln[:], 1e-5), e.memset(eps24[:], 1e-24),
        ], writes=["ones", "eps"], tag="consts")

        xnT_sb = AR.alloc(NB * 128 * 8, BF16)
        xnT_keys = tuple(f"xnT:{i}" for i in range(NB))
        xnT_view = xnT_sb.rearrange("p (j t) -> p j t", j=8)

        m0 = AR.mark()
        xt_b = [AR.alloc(HID, F32) for _ in range(2)]
        xc_b = [AR.alloc(HID, F32) for _ in range(2)]
        xnf_b = [AR.alloc(HID, F32) for _ in range(2)]
        xnb_b = [AR.alloc(HID, BF16) for _ in range(2)]
        for i in range(NB):
            p = i % 2
            xt, xc, xnf, xnb = xt_b[p], xc_b[p], xnf_b[p], xnb_b[p]
            kx, kc, kn, knb = f"xt{p}", f"xc{p}", f"xnf{p}", f"xnb{p}"
            dma("sdma", xt, x_in[i * 128:(i + 1) * 128, :], ["in"], [kx], tag=f"ldx{i}")
            R1 = red1[:, p:p + 1]
            R2 = red2[:, p:p + 1]
            R3 = red3[:, p:p + 1]
            kr1, kr2, kr3 = f"red1{p}", f"red2{p}", f"red3{p}"
            emit("vector", [
                lambda e, xt=xt, R1=R1: [e.tensor_reduce(R1, xt, AX.X, ALU.add)],
                lambda e, R1=R1: [e.tensor_scalar(R1, R1, 1.0 / HID, None, ALU.mult)],
                lambda e, xt=xt, xc=xc, R1=R1: [e.tensor_scalar(xc, xt, R1, None, ALU.subtract)],
            ], reads=[kx], writes=[kc, kr1], tag=f"lnm{i}", seq=True)
            emit("scalar", lambda e, xc=xc, xnf=xnf: [e.activation(xnf, xc, AF.Square)],
                 reads=[kc], writes=[kn], tag=f"lnsq{i}")
            emit("vector", [
                lambda e, xnf=xnf, i=i: [e.tensor_reduce(svar[:, i:i + 1], xnf, AX.X, ALU.add)],
                lambda e, i=i: [e.tensor_scalar(svar[:, i:i + 1], svar[:, i:i + 1],
                                                1.0 / HID, None, ALU.mult)],
            ], reads=[kn], writes=[kr2, "svar"], tag=f"lnv{i}", seq=True)
            emit("scalar", [
                lambda e, i=i, R3=R3: [e.activation(R3, svar[:, i:i + 1], AF.Ln,
                                                    bias=eps_ln[:])],
                lambda e, i=i, R3=R3: [e.activation(srstd[:, i:i + 1], R3, AF.Exp,
                                                    scale=-0.5)],
            ], reads=[kr2, "eps", "svar"], writes=[kr3, "srstd"], tag=f"lnr{i}", seq=True)
            emit("vector", lambda e, xc=xc, xnf=xnf, i=i: [
                e.tensor_scalar(xnf, xc, srstd[:, i:i + 1], None, ALU.mult)],
                reads=[kc, "srstd"], writes=[kn], tag=f"lnx{i}")
            emit("scalar", lambda e, xnf=xnf, xnb=xnb: [e.activation(xnb, xnf, AF.Identity)],
                 reads=[kn], writes=[knb], tag=f"lnxb{i}")
            dma("sdma", xn_scr[i * 128:(i + 1) * 128, :], xnf, [kn], ["xn_scr"], tag=f"stxn{i}")
            which = "A" if p == 0 else "B"
            psx = psA if p == 0 else psB
            def trx(e, xnb=xnb, psx=psx):
                return [e.matmul(psx[:, j * 128:(j + 1) * 128],
                                 xnb[:, j * 128:(j + 1) * 128], identbf[:],
                                 start=True, stop=True)
                        for j in range(8)]
            emit("tensor", trx, reads=[knb, "identbf"], writes=pk(which, 0, 1024),
                 tag=f"trx{i}")
            emit("vector", lambda e, psx=psx, i=i: [
                e.tensor_copy(xnT_view[:, :, i * 128:(i + 1) * 128],
                              psx[:, 0:1024].rearrange("p (j c) -> p j c", j=8))],
                 reads=pk(which, 0, 1024), writes=[f"xnT:{i}"], tag=f"evx{i}")
        for j in range(8):
            dma("sdma", xnT_scr[j * 128:(j + 1) * 128, :],
                xnT_sb[:, j * 2048:(j + 1) * 2048], list(xnT_keys), ["xnT_scr"],
                tag=f"stxnT{j}")

        emit("vector", [
            lambda e: [e.tensor_tensor(rnorm_tm[:, :], srstd[:, :], srstd[:, :], ALU.mult)],
            lambda e: [e.tensor_tensor(rnorm_tm[:, :], rnorm_tm[:, :], svar[:, :], ALU.mult)],
            lambda e: [e.tensor_scalar(rnorm_tm[:, :], rnorm_tm[:, :], float(HID), None, ALU.mult)],
            lambda e: [e.tensor_scalar(rnorm_tm[:, :], rnorm_tm[:, :], eps24[:], None, ALU.max)],
        ], reads=["svar", "srstd", "eps"], writes=["rnorm"], tag="rnb", seq=True)
        emit("scalar", [
            lambda e: [e.activation(rnorm_tm[:, :], rnorm_tm[:, :], AF.Ln)],
            lambda e: [e.activation(rnorm_tm[:, :], rnorm_tm[:, :], AF.Exp, scale=-0.5)],
        ], reads=["rnorm"], writes=["rnorm"], tag="rnb2", seq=True)

        btwTs = AR.alloc(8 * 16, BF16)
        betaT_f = AR.alloc(T, F32, parts=16)
        betaT_bf = AR.alloc(T, BF16, parts=16)
        dma("gdma", btwTs.rearrange("p (j m) -> p j m", j=8), btwT.rearrange("(j p) m -> p j m", p=128), ["in"], ["btwTs"])
        def mmbeta(e):
            r = []
            for c in range(0, T, 512):
                for j in range(8):
                    r.append(e.matmul(psA[0:16, c:c + 512], btwTs[:, j * 16:(j + 1) * 16],
                                      xnT_sb[:, j * 2048 + c:j * 2048 + c + 512],
                                      start=(j == 0), stop=(j == 7)))
            return r
        emit("tensor", mmbeta, reads=("btwTs",) + xnT_keys, writes=pk("A", 0, 2048),
             tag="mmbeta")
        emit("scalar", lambda e: [
            e.activation(betaT_f, psA[0:16, 0:2048], AF.Sigmoid, bias=btbs[:])],
            reads=pk("A", 0, 2048) + ("btbs",), writes=["betaTf"], tag="betasig")
        emit("vector", lambda e: [
            e.tensor_scalar(betaT_bf, betaT_f, 2.0, None, ALU.mult)],
            reads=["betaTf"], writes=["betaTbf"], tag="beta2")
        for half in range(2):
            which = "A" if half == 0 else "B"
            psx = psA if half == 0 else psB
            def trb2(e, half=half, psx=psx):
                r = []
                for jj in range(8):
                    i = half * 8 + jj
                    r.append(e.matmul(psx[:, jj * 16:(jj + 1) * 16],
                                      betaT_bf[:, i * 128:(i + 1) * 128],
                                      identbf[0:16, 0:16], start=True, stop=True))
                return r
            emit("tensor", trb2, reads=["betaTbf", "identbf"], writes=pk(which, 0, 128),
                 tag=f"trbeta{half}")
            def evb(e, half=half, psx=psx):
                r = []
                for jj in range(8):
                    i = half * 8 + jj
                    r.append(e.tensor_copy(beta_tm[:, i * 16:(i + 1) * 16],
                                           psx[:, jj * 16:(jj + 1) * 16]))
                return r
            emit("vector", evb, reads=pk(which, 0, 128), writes=["beta_tm"],
                 tag=f"evbeta{half}")
        emit("vector", lambda e: [
            e.tensor_scalar(betaneg_tm[:, :], beta_tm[:, :], -1.0, None, ALU.mult)],
            reads=["beta_tm"], writes=["betaneg_tm"], tag="betaneg")
        dma("sdma", beta_scr[:, :], beta_tm[:, :], ["beta_tm"], ["beta_scr"], tag="stbeta")
        AR.release(m0)
        P.barrier()

        cbc = [AR.alloc(T, BF16) for _ in range(2)]
        sbc = [AR.alloc(T, BF16) for _ in range(2)]
        m1 = AR.mark()
        pos4 = AR.alloc(T, F32, parts=4)
        th4 = AR.alloc(T, F32, parts=4)
        sc4 = AR.alloc(T, F32, parts=4)
        si4 = AR.alloc(T, I32, parts=4)
        csrow = AR.alloc(T, BF16, parts=4)
        dma("sdma", pos4, pos4in[:, :], ["in"], ["pos4"], tag="ldpos")
        emit("scalar", lambda e: [
            e.activation(sm2[0:2, 2:3], sm2[0:2, 1:2], AF.Abs)],
            reads=["omr"], writes=["sp1"], tag="spa")
        emit("scalar", lambda e: [
            e.activation(sm2[0:2, 3:4], sm2[0:2, 2:3], AF.Exp, scale=-1.0)],
            reads=["sp1"], writes=["sp2"], tag="spe")
        emit("vector", lambda e: [
            e.tensor_scalar(sm2[0:2, 4:5], sm2[0:2, 3:4], 1.0, None, ALU.add),
            e.tensor_scalar(sm2[0:2, 5:6], sm2[0:2, 1:2], 0.0, None, ALU.max),
        ], reads=["sp2", "omr"], writes=["sp3"], tag="spv")
        emit("scalar", lambda e: [
            e.activation(sm2[0:2, 6:7], sm2[0:2, 4:5], AF.Ln)],
            reads=["sp3"], writes=["sp4"], tag="spl")
        emit("vector", [
            lambda e: [e.tensor_tensor(sm2[0:2, 7:8], sm2[0:2, 5:6], sm2[0:2, 6:7], ALU.add)],
            lambda e: [e.tensor_scalar(sm2[0:2, 7:8], sm2[0:2, 7:8], OMEGA_SCALE, None, ALU.mult)],
        ], reads=["sp4", "sp3"], writes=["omega2"], tag="spf", seq=True)
        emit("tensor", lambda e: [
            e.matmul(psA[0:4, 0:1], dup24s[:, :], sm2[0:2, 0:1], start=True, stop=True),
            e.matmul(psA[0:4, 1:2], dup24s[:, :], sm2[0:2, 7:8], start=True, stop=True),
        ], reads=["phi", "omega2", "dup24s"], writes=pk("A", 0, 2), tag="dupmm")
        emit("vector", lambda e: [
            e.tensor_tensor(sm4[0:4, 0:1], psA[0:4, 0:1], phsh4s[:, :], ALU.add),
            e.tensor_copy(sm4[0:4, 1:2], psA[0:4, 1:2]),
        ], reads=pk("A", 0, 2) + ("phsh4s",), writes=["pom4"], tag="dupev")
        emit("vector", lambda e: [
            e.tensor_scalar(th4, pos4, sm4[0:4, 1:2], sm4[0:4, 0:1], ALU.mult, ALU.add),
        ], reads=["pos4", "pom4"], writes=["th4"], tag="th4")
        emit("vector", [
            lambda e: [e.tensor_scalar(sc4, th4, INV_2PI, None, ALU.mult)],
            lambda e: [e.tensor_copy(si4, sc4)],
            lambda e: [e.tensor_copy(th4, si4)],
            lambda e: [e.tensor_tensor(sc4, sc4, th4, ALU.subtract)],
        ], reads=["th4"], writes=["sc4", "th4"], tag="rr4", seq=True)
        emit("scalar", lambda e: [
            e.activation(th4, sc4, AF.Sin, scale=TWO_PI)],
            reads=["sc4"], writes=["sin4"], tag="sin4")
        emit("vector", lambda e: [e.tensor_copy(csrow, th4)],
             reads=["sin4"], writes=["csrow"], tag="csbf")
        for r in range(2):
            for ci in range(2):
                idx = 2 * r + ci
                which = "A" if idx % 2 == 0 else "B"
                ps = psA if idx % 2 == 0 else psB
                def mmbc(e, idx=idx, ps=ps):
                    return [e.matmul(ps[:, c:c + 512], sel4s[:, idx * 128:(idx + 1) * 128],
                                     csrow[:, c:c + 512], start=True, stop=True)
                            for c in range(0, T, 512)]
                emit("tensor", mmbc, reads=["csrow", "sel4s"], writes=pk(which, 0, 2048),
                     tag=f"bccs{idx}")
                dst = cbc[r] if ci == 0 else sbc[r]
                kcs = f"cs{r}{ci}"
                emit("vector", lambda e, dst=dst, ps=ps: [e.tensor_copy(dst, ps[:, 0:2048])],
                     reads=pk(which, 0, 2048), writes=[kcs], tag=f"evcs{idx}")
        AR.release(m1)
        P.barrier()

        m1 = AR.mark()
        wA1Ts = AR.alloc(8 * 32, BF16); wB1Ts = AR.alloc(8 * 32, BF16)
        wA2Ts = AR.alloc(2 * HID, BF16, parts=32); wB2Ts = AR.alloc(2 * HID, BF16, parts=32)
        convAs = AR.alloc(NB * 3, F32); convBs = AR.alloc(NB * 3, F32)
        dma("gdma", wA1Ts.rearrange("p (j m) -> p j m", j=8), wA1T.rearrange("(j p) m -> p j m", p=128), ["in"], ["wA1Ts"])
        dma("gdma", wB1Ts.rearrange("p (j m) -> p j m", j=8), wB1T.rearrange("(j p) m -> p j m", p=128), ["in"], ["wB1Ts"])
        dma("gdma", wA2Ts, wA2T[:, :], ["in"], ["wA2Ts"])
        dma("gdma", wB2Ts, wB2T[:, :], ["in"], ["wB2Ts"])
        dma("sdma", convAs, convAp[:, :], ["in"], ["convAs"])
        dma("sdma", convBs, convBp[:, :], ["in"], ["convBs"])
        uTA = AR.alloc(T, BF16, parts=32)
        uTB = AR.alloc(T, BF16, parts=32)
        wn_b = [AR.alloc(T, BF16) for _ in range(2)]
        sq_b = [AR.alloc(T, BF16) for _ in range(2)]
        cv_b = [AR.alloc(T, F32) for _ in range(2)]
        cv2_b = [AR.alloc(T, F32) for _ in range(2)]
        xa_b = [AR.alloc(T, BF16) for _ in range(2)]
        xb_b = [AR.alloc(T, BF16) for _ in range(2)]
        wfin_b = [AR.alloc(T, BF16) for _ in range(2)]
        rinvS = AR.alloc(T, F32, parts=1)
        rrowS = AR.alloc(T, BF16, parts=1)
        rbcA = AR.alloc(T, F32); rbcB = AR.alloc(T, F32)

        def mmu(e, w1s, uT):
            r = []
            for c in range(0, T, 512):
                for j in range(8):
                    r.append(e.matmul(psA[0:32, c:c + 512], w1s[:, j * 32:(j + 1) * 32],
                                      xnT_sb[:, j * 2048 + c:j * 2048 + c + 512],
                                      start=(j == 0), stop=(j == 7)))
            return r
        for (pn, w1s, w2s, k1, k2, uT) in (("A", wA1Ts, wA2Ts, "wA1Ts", "wA2Ts", uTA),
                                           ("B", wB1Ts, wB2Ts, "wB1Ts", "wB2Ts", uTB)):
            emit("tensor", lambda e, w1s=w1s, uT=uT: mmu(e, w1s, uT),
                 reads=(k1,) + xnT_keys, writes=pk("A", 0, 2048), tag=f"mmu{pn}")
            emit("vector", lambda e, uT=uT: [e.tensor_copy(uT, psA[0:32, 0:2048])],
                 reads=pk("A", 0, 2048), writes=[f"uT{pn}"], tag=f"evu{pn}")
            for j in range(NB):
                p = j % 2
                def mmw(e, j=j, w2s=w2s, uT=uT):
                    return [e.matmul(psA[:, c:c + 512], w2s[:, j * 128:(j + 1) * 128],
                                     uT[:, c:c + 512], start=True, stop=True)
                            for c in range(0, T, 512)]
                emit("tensor", mmw, reads=[k2, f"uT{pn}"], writes=pk("A", 0, 2048),
                     tag=f"p1m{pn}{j}")
                emit("scalar", lambda e, sq=sq_b[p]: [
                    e.activation(sq, psA[:, 0:2048], AF.Square)],
                    reads=pk("A", 0, 2048), writes=[f"sq{p}"], tag=f"p1s{pn}{j}")
                def mmss(e, sq=sq_b[p], j=j):
                    return [e.matmul(psB[0:1, c:c + 512], onescol[:], sq[:, c:c + 512],
                                     start=(j == 0), stop=(j == NB - 1))
                            for c in range(0, T, 512)]
                emit("tensor", mmss, reads=[f"sq{p}", "ones"], writes=pk("B", 0, 2048),
                     tag=f"p1ss{pn}{j}")
            rinv = rinvS
            rrow = rrowS
            rbc = rbcA if pn == "A" else rbcB
            emit("scalar", [
                lambda e, rinv=rinv: [e.activation(rinv, psB[0:1, 0:2048], AF.Ln)],
                lambda e, rinv=rinv: [e.activation(rinv, rinv, AF.Exp, scale=-0.5)],
            ], reads=pk("B", 0, 2048), writes=[f"rinv{pn}"], tag=f"ri{pn}", seq=True)
            emit("vector", lambda e, rinv=rinv, rrow=rrow: [e.tensor_copy(rrow, rinv)],
                 reads=[f"rinv{pn}"], writes=[f"rrow{pn}"], tag=f"rr{pn}")
            def mmrb(e, rrow=rrow):
                return [e.matmul(psB[:, c:c + 512], onesrow[0:1, :], rrow[:, c:c + 512],
                                 start=True, stop=True) for c in range(0, T, 512)]
            emit("tensor", mmrb, reads=[f"rrow{pn}", "ones"], writes=pk("B", 0, 2048),
                 tag=f"rbm{pn}")
            emit("vector", lambda e, rbc=rbc: [e.tensor_copy(rbc, psB[:, 0:2048])],
                 reads=pk("B", 0, 2048), writes=[f"rbc{pn}"], tag=f"rbe{pn}")
            emit("vector", lambda e, uT=uT, rbc=rbc: [
                e.tensor_tensor(uT, uT, rbc[0:32, :], ALU.mult)],
                reads=[f"uT{pn}", f"rbc{pn}"], writes=[f"uT{pn}"], tag=f"usc{pn}")

        for j in range(NB):
            p = j % 2
            r_ = j % 2
            for (pn, w2s, k2, uT, convs, kconv, xres, kxr, rbc, krbc) in (
                    ("A", wA2Ts, "wA2Ts", uTA, convAs, "convAs", xa_b[p], f"xa{p}", rbcA, "rbcA"),
                    ("B", wB2Ts, "wB2Ts", uTB, convBs, "convBs", xb_b[p], f"xb{p}", rbcB, "rbcB")):
                wn, cv, cv2 = wn_b[p], cv_b[p], cv2_b[p]
                def mmw2(e, j=j, w2s=w2s, uT=uT):
                    return [e.matmul(psA[:, c:c + 512], w2s[:, j * 128:(j + 1) * 128],
                                     uT[:, c:c + 512], start=True, stop=True)
                            for c in range(0, T, 512)]
                emit("tensor", mmw2, reads=[k2, f"uT{pn}"], writes=pk("A", 0, 2048),
                     tag=f"p2m{pn}{j}")
                emit("scalar", lambda e, wn=wn: [
                    e.activation(wn, psA[:, 0:2048], AF.Identity)],
                    reads=pk("A", 0, 2048), writes=[f"wn{p}"], tag=f"p2n{pn}{j}")
                emit("scalar", lambda e, wn=wn, cv=cv, j=j, convs=convs: [
                    e.activation(cv, wn, AF.Identity, scale=convs[:, j * 3 + 2:j * 3 + 3])],
                    reads=[f"wn{p}", kconv], writes=[f"cv{p}"], tag=f"cva{pn}{j}")
                emit("vector", lambda e, wn=wn, cv=cv, cv2=cv2, j=j, convs=convs: [
                    e.scalar_tensor_tensor(cv2[:, 1:T], wn[:, 0:T - 1],
                                           convs[:, j * 3 + 1:j * 3 + 2], cv[:, 1:T],
                                           ALU.mult, ALU.add),
                    e.tensor_copy(cv2[:, 0:1], cv[:, 0:1]),
                ], reads=[f"wn{p}", f"cv{p}", kconv], writes=[f"cv2{p}"], tag=f"cvb{pn}{j}")
                emit("vector", lambda e, wn=wn, cv=cv, cv2=cv2, j=j, convs=convs: [
                    e.scalar_tensor_tensor(cv[:, 2:T], wn[:, 0:T - 2],
                                           convs[:, j * 3 + 0:j * 3 + 1], cv2[:, 2:T],
                                           ALU.mult, ALU.add),
                    e.tensor_copy(cv[:, 0:2], cv2[:, 0:2]),
                ], reads=[f"wn{p}", f"cv2{p}", kconv], writes=[f"cv{p}"], tag=f"cvc{pn}{j}")
                emit("scalar", lambda e, cv=cv, xres=xres: [
                    e.activation(xres, cv, AF.Silu)],
                    reads=[f"cv{p}"], writes=[kxr], tag=f"sil{pn}{j}")
            wf = wfin_b[p]
            emit("vector", lambda e, p=p, r_=r_, wf=wf: [
                e.tensor_tensor(cv_b[p], xa_b[p], cbc[r_], ALU.mult),
                e.tensor_tensor(cv2_b[p], xb_b[p], sbc[r_], ALU.mult),
            ], reads=[f"xa{p}", f"xb{p}", f"cs{r_}0", f"cs{r_}1"],
                writes=[f"cv{p}", f"cv2{p}"], tag=f"wc1{j}")
            emit("vector", lambda e, p=p, wf=wf: [
                e.tensor_tensor(wf, cv_b[p], cv2_b[p], ALU.add)],
                reads=[f"cv{p}", f"cv2{p}"], writes=[f"wf{p}"], tag=f"wc2{j}")
            dma("sdma", W_scr[j * 128:(j + 1) * 128, :], wf, [f"wf{p}"], ["W_scr"],
                tag=f"stW{j}")
        AR.release(m1)
        P.barrier()

        m1 = AR.mark()
        wbig = AR.alloc(8 * HID, BF16)
        stg_b = [AR.alloc(T, BF16) for _ in range(2)]
        for (nameW, Wap, scr, tmaj) in (("q", WqT, qT_scr, False),
                                        ("k", WkT, kT_scr, False),
                                        ("v", WvT, v_scr, True)):
            dma("gdma", wbig.rearrange("p (j m) -> p j m", j=8),
                Wap.rearrange("(j p) m -> p j m", p=128),
                ["in", "wbig"], ["wbig"], tag=f"ldW{nameW}")
            if not tmaj:
                for mch in range(8):
                    p = mch % 2
                    which = "A" if p == 0 else "B"
                    ps = psA if p == 0 else psB
                    def mmq(e, mch=mch, ps=ps):
                        r = []
                        for c in range(0, T, 512):
                            for j in range(8):
                                r.append(e.matmul(
                                    ps[:, c:c + 512],
                                    wbig[:, j * 1024 + mch * 128:j * 1024 + (mch + 1) * 128],
                                    xnT_sb[:, j * 2048 + c:j * 2048 + c + 512],
                                    start=(j == 0), stop=(j == 7)))
                        return r
                    emit("tensor", mmq, reads=("wbig",) + xnT_keys,
                         writes=pk(which, 0, 2048), tag=f"mm{nameW}{mch}")
                    emit("vector", lambda e, ps=ps, st=stg_b[p]: [
                        e.tensor_copy(st, ps[:, 0:2048])],
                        reads=pk(which, 0, 2048), writes=[f"stg{p}"], tag=f"ev{nameW}{mch}")
                    dma("sdma", scr[mch * 128:(mch + 1) * 128, :], stg_b[p],
                        [f"stg{p}"], [f"{nameW}_scr"], tag=f"st{nameW}{mch}")
            else:
                for i in range(NB):
                    p = i % 2
                    which = "A" if p == 0 else "B"
                    ps = psA if p == 0 else psB
                    def mmv(e, i=i, ps=ps):
                        r = []
                        for c in range(0, HID, 512):
                            for j in range(8):
                                r.append(e.matmul(
                                    ps[:, c:c + 512],
                                    xnT_sb[:, j * 2048 + i * 128:j * 2048 + (i + 1) * 128],
                                    wbig[:, j * 1024 + c:j * 1024 + c + 512],
                                    start=(j == 0), stop=(j == 7)))
                        return r
                    emit("tensor", mmv, reads=("wbig",) + xnT_keys,
                         writes=pk(which, 0, 1024), tag=f"mmv{i}")
                    emit("vector", lambda e, ps=ps, st=stg_b[p]: [
                        e.tensor_copy(st[:, 0:1024], ps[:, 0:1024])],
                        reads=pk(which, 0, 1024), writes=[f"stg{p}"], tag=f"evv{i}")
                    dma("sdma", scr[i * 128:(i + 1) * 128, :], stg_b[p][:, 0:1024],
                        [f"stg{p}"], ["v_scr"], tag=f"stv{i}")
        AR.release(m1)
        P.barrier()

        m1 = AR.mark()
        gw1s = AR.alloc(4 * HID, BF16)
        gb1s = AR.alloc(4, F32)
        hT_t = [AR.alloc(T, BF16) for _ in range(4)]
        gw2s = AR.alloc(4 * 16, BF16)
        ceu = AR.alloc(HID, F32, parts=16)
        ceub = AR.alloc(HID, BF16, parts=16)
        cesq = AR.alloc(HID, F32, parts=16)
        ceuT = AR.alloc(8 * 16, BF16)
        dma("gdma", gw1s.rearrange("p (j m) -> p j m", j=8), gw1.rearrange("(j p) m -> p j m", p=128), ["in"], ["gw1s"])
        dma("sdma", gb1s, gb1p[:, :], ["in"], ["gb1s"])
        dma("gdma", gw2s.rearrange("p (j m) -> p j m", j=4), gw2.rearrange("(j p) m -> p j m", p=128), ["in"], ["gw2s"])
        dma("sdma", ceu, colemb[:, :], ["in"], ["ceu"], tag="ldce")
        dma("sdma", gb2bc[0:1, :], gb2[0:1, :], ["in"], ["gb2r"], tag="ldgb2")
        emit("vector", lambda e: [e.tensor_copy(t16a[0:1, 0:16].bitcast(BF16)[0:1, 0:16]
                                                if False else t16b[0:1, 0:16], gb2bc[0:1, 0:16])],
             reads=["gb2r"], writes=["gb2row"], tag="gb2cp") if False else None
        gb2rowb = AR.alloc(16, BF16, parts=1)
        emit("vector", lambda e: [e.tensor_copy(gb2rowb, gb2bc[0:1, 0:16])],
             reads=["gb2r"], writes=["gb2rowb"], tag="gb2cp")
        emit("tensor", lambda e: [e.matmul(psA[:, 0:16], onesrow[0:1, :], gb2rowb,
                                           start=True, stop=True)],
             reads=["gb2rowb", "ones"], writes=pk("A", 0, 16), tag="gb2mm")
        emit("vector", lambda e: [e.tensor_copy(gb2bc[:, :], psA[:, 0:16])],
             reads=pk("A", 0, 16), writes=["gb2bc"], tag="gb2ev")
        emit("scalar", lambda e: [e.activation(cesq, ceu, AF.Square)],
             reads=["ceu"], writes=["cesq"], tag="cesq")
        emit("vector", [
            lambda e: [e.tensor_reduce(t16a[0:16, 0:1], cesq, AX.X, ALU.add)],
            lambda e: [e.tensor_scalar(t16a[0:16, 0:1], t16a[0:16, 0:1], eps24[0:16, :], None, ALU.max)],
        ], reads=["cesq", "eps"], writes=["cen2"], tag="cered", seq=True)
        emit("scalar", [
            lambda e: [e.activation(t16a[0:16, 0:1], t16a[0:16, 0:1], AF.Ln)],
            lambda e: [e.activation(t16a[0:16, 0:1], t16a[0:16, 0:1], AF.Exp, scale=-0.5)],
        ], reads=["cen2"], writes=["cen2"], tag="ceri", seq=True)
        emit("vector", lambda e: [
            e.tensor_scalar(ceub, ceu, t16a[0:16, 0:1], None, ALU.mult)],
            reads=["ceu", "cen2"], writes=["ceub"], tag="ceun")
        def cetr(e):
            return [e.matmul(psB[:, j * 16:(j + 1) * 16],
                             ceub[:, j * 128:(j + 1) * 128], identbf[0:16, 0:16],
                             start=True, stop=True)
                    for j in range(8)]
        emit("tensor", cetr, reads=["ceub", "identbf"], writes=pk("B", 0, 128), tag="cetr")
        emit("vector", lambda e: [e.tensor_copy(ceuT, psB[:, 0:128])],
             reads=pk("B", 0, 128), writes=["ceuT"], tag="ceuTev")
        for mch in range(4):
            which = "A" if mch % 2 == 0 else "B"
            ps = psA if mch % 2 == 0 else psB
            def mmh(e, mch=mch, ps=ps):
                r = []
                for c in range(0, T, 512):
                    for j in range(8):
                        r.append(e.matmul(ps[:, c:c + 512],
                                          gw1s[:, j * 512 + mch * 128:j * 512 + (mch + 1) * 128],
                                          xnT_sb[:, j * 2048 + c:j * 2048 + c + 512],
                                          start=(j == 0), stop=(j == 7)))
                return r
            emit("tensor", mmh, reads=("gw1s",) + xnT_keys, writes=pk(which, 0, 2048),
                 tag=f"mmh{mch}")
            emit("scalar", lambda e, ps=ps, mch=mch: [
                e.activation(hT_t[mch], ps[:, 0:2048], AF.Gelu, bias=gb1s[:, mch:mch + 1])],
                reads=pk(which, 0, 2048) + ("gb1s",), writes=[f"hT{mch}"], tag=f"gelh{mch}")
        for i in range(NB):
            which = "A" if i % 2 == 0 else "B"
            ps = psA if i % 2 == 0 else psB
            def mml(e, i=i, ps=ps):
                r = []
                for j in range(8):
                    r.append(e.matmul(ps[:, 0:16],
                                      xnT_sb[:, j * 2048 + i * 128:j * 2048 + (i + 1) * 128],
                                      ceuT[:, j * 16:(j + 1) * 16],
                                      start=(j == 0), stop=(j == 7)))
                for j in range(4):
                    r.append(e.matmul(ps[:, 16:32], hT_t[j][:, i * 128:(i + 1) * 128],
                                      gw2s[:, j * 16:(j + 1) * 16],
                                      start=(j == 0), stop=(j == 3)))
                return r
            emit("tensor", mml,
                 reads=tuple(f"hT{m}" for m in range(4)) + ("ceuT", "gw2s") + xnT_keys,
                 writes=pk(which, 0, 32), tag=f"mml{i}")
            emit("vector", lambda e, ps=ps: [
                e.tensor_tensor(t16a[:, 0:16], ps[:, 16:32], gb2bc[:, 0:16], ALU.add)],
                reads=pk(which, 0, 32) + ("gb2bc",), writes=["t16a"], tag=f"lga{i}")
            emit("scalar", lambda e: [
                e.activation(t16a[:, 0:16], t16a[:, 0:16], AF.Sigmoid)],
                reads=["t16a"], writes=["t16a"], tag=f"lgs{i}")
            emit("vector", [
                lambda e, ps=ps, i=i: [e.scalar_tensor_tensor(
                    t16c[:, 0:16], ps[:, 0:16], rnorm_tm[:, i:i + 1],
                    t16a[:, 0:16], ALU.mult, ALU.add)],
                lambda e: [e.tensor_reduce(red1[:, 0:1], t16c[:, 0:16], AX.X, ALU.max)],
                lambda e: [e.tensor_scalar(t16b[:, 0:16], t16c[:, 0:16], red1[:, 0:1], None, ALU.is_equal)],
            ], reads=pk(which, 0, 32) + ("t16a", "rnorm"),
                writes=["lgt", "msk", "red10"], tag=f"lgm{i}", seq=True)
            emit("scalar", lambda e: [
                e.activation(t16c[:, 0:16], t16c[:, 0:16], AF.Exp)],
                reads=["lgt"], writes=["lgt"], tag=f"lge{i}")
            emit("vector", [
                lambda e: [e.tensor_reduce(red2[:, 0:1], t16c[:, 0:16], AX.X, ALU.add)],
                lambda e: [e.reciprocal(red2[:, 0:1], red2[:, 0:1])],
                lambda e: [e.tensor_scalar(t16c[:, 0:16], t16c[:, 0:16], red2[:, 0:1], None, ALU.mult)],
                lambda e, i=i: [e.tensor_tensor(wts_tm[:, i * 16:(i + 1) * 16], t16c[:, 0:16],
                                                t16b[:, 0:16], ALU.mult)],
            ], reads=["lgt", "msk"], writes=["wts_tm", "red20"], tag=f"lgw{i}", seq=True)
        dma("sdma", wts_scr[:, :], wts_tm[:, :], ["wts_tm"], ["wts_scr"], tag="stwts")
        AR.release(m1)
        P.barrier()

        AR.reset()
        qs = AR.alloc(T, BF16); ks = AR.alloc(T, BF16); vs = AR.alloc(T, BF16)
        ws_p = [AR.alloc(T, BF16) for _ in range(2)]
        wN_p = [AR.alloc(T, BF16) for _ in range(2)]
        Ninv_p = [AR.alloc(NB * 128, BF16) for _ in range(2)]
        Lb_p = [AR.alloc(512, BF16) for _ in range(2)]
        Lt_p = [AR.alloc(512, BF16) for _ in range(2)]
        Xt_p = [[AR.alloc(512, BF16) for _ in range(2)] for _ in range(2)]
        En_p = [[AR.alloc(512, BF16) for _ in range(2)] for _ in range(2)]
        Et_p = [[AR.alloc(512, BF16) for _ in range(2)] for _ in range(2)]
        Vsb_p = [AR.alloc(T, BF16) for _ in range(2)]
        Zsb_p = [AR.alloc(T, BF16) for _ in range(2)]
        Csb_p = [[AR.alloc(T, BF16) for _ in range(2)] for _ in range(2)]
        Mt_p = [AR.alloc(128, BF16) for _ in range(2)]
        PT_p = [[AR.alloc((NB - s) * 128, BF16) for s in range(NB)] for _ in range(2)]
        csrowf = AR.alloc(T, F32, parts=1)
        crecipb = AR.alloc(T, BF16, parts=1)
        orbc = AR.alloc(T, BF16)
        obf = AR.alloc(T, BF16)

        def head_pair(h):
            dma("gdma", qs, qT_scr[h * 128:(h + 1) * 128, :], ["q_scr"], ["qs"], tag=f"ldq{h}")
            dma("gdma", ks, kT_scr[h * 128:(h + 1) * 128, :], ["k_scr"], ["ks"], tag=f"ldk{h}")
            dma("sdma", vs,
                v_scr.rearrange("(s p) (hh d) -> p s hh d", p=128, d=128)[:, :, h, :],
                ["v_scr"], ["vs"], tag=f"ldv{h}")
            for r in range(2):
                n = 2 * h + r
                dma("gdma", ws_p[r], W_scr[n * 128:(n + 1) * 128, :], ["W_scr"],
                    [f"ws{r}"], tag=f"ldw{n}")

            ctx = []
            for r in range(2):
                ctx.append(dict(
                    n=2 * h + r, ws=ws_p[r], kw=f"ws{r}", wN=wN_p[r], kwN=f"wN{r}",
                    which=("A" if r == 0 else "B"), psx=(psA if r == 0 else psB),
                    Lb=Lb_p[r], Lt=Lt_p[r], Xt=Xt_p[r], En=En_p[r], Et=Et_p[r],
                    kL=f"L{r}", kLt=f"Lt{r}",
                    kXt=[f"Xt{r}0", f"Xt{r}1"], kEn=[f"En{r}0", f"En{r}1"],
                    kEt=[f"Et{r}0", f"Et{r}1"]))
            for half in range(2):
                for r in range(2):
                    d = ctx[r]
                    def trw(e, half=half, ws=d["ws"], psx=d["psx"]):
                        return [e.matmul(psx[:, jj * 128:(jj + 1) * 128],
                                         ws[:, (half * 8 + jj) * 128:(half * 8 + jj + 1) * 128],
                                         identbf[:], start=True, stop=True)
                                for jj in range(8)]
                    emit("tensor", trw, reads=[d["kw"], "identbf"],
                         writes=pk(d["which"], 0, 1024), tag=f"trw{d['n']}{half}")
                for r in range(2):
                    d = ctx[r]
                    emit("scalar", lambda e, half=half, wN=d["wN"], psx=d["psx"]: [
                        e.activation(wN[:, half * 1024:(half + 1) * 1024], psx[:, 0:1024],
                                     AF.Identity)],
                        reads=pk(d["which"], 0, 1024), writes=[d["kwN"]],
                        tag=f"evw{d['n']}{half}")
            for g in range(4):
                for r in range(2):
                    d = ctx[r]
                    def mmG(e, g=g, ws=d["ws"], psx=d["psx"]):
                        return [e.matmul(psx[:, 1024 + jj * 128:1024 + (jj + 1) * 128],
                                         ws[:, (g * 4 + jj) * 128:(g * 4 + jj + 1) * 128],
                                         ws[:, (g * 4 + jj) * 128:(g * 4 + jj + 1) * 128],
                                         start=True, stop=True) for jj in range(4)]
                    emit("tensor", mmG, reads=[d["kw"]], writes=pk(d["which"], 1024, 1536),
                         tag=f"G{d['n']}{g}")
                for r in range(2):
                    d = ctx[r]
                    def evL(e, g=g, n=d["n"], psx=d["psx"], Lb=d["Lb"]):
                        return [e.scalar_tensor_tensor(
                            Lb[:, jj * 128:(jj + 1) * 128],
                            psx[:, 1024 + jj * 128:1024 + (jj + 1) * 128],
                            beta_tm[:, (g * 4 + jj) * 16 + n:(g * 4 + jj) * 16 + n + 1],
                            maskSs[:], ALU.mult, ALU.mult) for jj in range(4)]
                    emit("vector", evL, reads=pk(d["which"], 1024, 1536) + ("beta_tm", "maskSs"),
                         writes=[d["kL"]], tag=f"evL{d['n']}{g}")
                for r in range(2):
                    d = ctx[r]
                    def trL(e, Lb=d["Lb"], psx=d["psx"]):
                        return [e.matmul(psx[:, 1536 + jj * 128:1536 + (jj + 1) * 128],
                                         Lb[:, jj * 128:(jj + 1) * 128], identbf[:],
                                         start=True, stop=True)
                                for jj in range(4)]
                    emit("tensor", trL, reads=[d["kL"], "identbf"],
                         writes=pk(d["which"], 1536, 2048), tag=f"trL{d['n']}{g}")
                for r in range(2):
                    d = ctx[r]
                    emit("vector", lambda e, Lt=d["Lt"], psx=d["psx"]: [
                        e.tensor_copy(Lt, psx[:, 1536:2048])],
                        reads=pk(d["which"], 1536, 2048), writes=[d["kLt"]],
                        tag=f"evLt{d['n']}{g}")
                for r in range(2):
                    d = ctx[r]
                    emit("vector", lambda e, Lt=d["Lt"], Xt=d["Xt"]: [
                        e.tensor_tensor(Xt[0][:, jj * 128:(jj + 1) * 128], identbf[:],
                                        Lt[:, jj * 128:(jj + 1) * 128], ALU.subtract)
                        for jj in range(4)],
                        reads=[d["kLt"], "identbf"], writes=[d["kXt"][0]],
                        tag=f"x0{d['n']}{g}")
                for r in range(2):
                    d = ctx[r]
                    def e0(e, Lb=d["Lb"], Lt=d["Lt"], psx=d["psx"]):
                        r2 = []
                        for jj in range(4):
                            s = slice(jj * 128, (jj + 1) * 128)
                            r2.append(e.matmul(psx[:, 1024 + jj * 128:1024 + (jj + 1) * 128],
                                               Lt[:, s], Lb[:, s], start=True, stop=True))
                            r2.append(e.matmul(psx[:, 1536 + jj * 128:1536 + (jj + 1) * 128],
                                               Lb[:, s], Lt[:, s], start=True, stop=True))
                        return r2
                    emit("tensor", e0, reads=[d["kL"], d["kLt"]],
                         writes=pk(d["which"], 1024, 2048), tag=f"e0{d['n']}{g}")
                for r in range(2):
                    d = ctx[r]
                    emit("vector", lambda e, En=d["En"], Et=d["Et"], psx=d["psx"]: [
                        e.tensor_copy(En[0], psx[:, 1024:1536]),
                        e.tensor_copy(Et[0], psx[:, 1536:2048]),
                    ], reads=pk(d["which"], 1024, 2048), writes=[d["kEn"][0], d["kEt"][0]],
                        tag=f"eve0{d['n']}{g}")
                for it in range(6):
                    pa, pb = it % 2, 1 - it % 2
                    for r in range(2):
                        d = ctx[r]
                        def xup(e, En=d["En"], Xt=d["Xt"], pa=pa, psx=d["psx"]):
                            return [e.matmul(psx[:, 1024 + jj * 128:1024 + (jj + 1) * 128],
                                             En[pa][:, jj * 128:(jj + 1) * 128],
                                             Xt[pa][:, jj * 128:(jj + 1) * 128],
                                             start=True, stop=True) for jj in range(4)]
                        emit("tensor", xup, reads=[d["kEn"][pa], d["kXt"][pa]],
                             writes=pk(d["which"], 1024, 1536), tag=f"xu{d['n']}{g}{it}")
                    for r in range(2):
                        d = ctx[r]
                        emit("vector", lambda e, Xt=d["Xt"], pa=pa, pb=pb, psx=d["psx"]: [
                            e.tensor_tensor(Xt[pb], psx[:, 1024:1536], Xt[pa], ALU.add)],
                            reads=pk(d["which"], 1024, 1536) + (d["kXt"][pa],),
                            writes=[d["kXt"][pb]], tag=f"xa{d['n']}{g}{it}")
                    if it < 5:
                        for r in range(2):
                            d = ctx[r]
                            def esq(e, En=d["En"], Et=d["Et"], pa=pa, psx=d["psx"]):
                                return [e.matmul(psx[:, 1536 + jj * 128:1536 + (jj + 1) * 128],
                                                 Et[pa][:, jj * 128:(jj + 1) * 128],
                                                 En[pa][:, jj * 128:(jj + 1) * 128],
                                                 start=True, stop=True) for jj in range(4)]
                            emit("tensor", esq, reads=[d["kEt"][pa], d["kEn"][pa]],
                                 writes=pk(d["which"], 1536, 2048), tag=f"en{d['n']}{g}{it}")
                        for r in range(2):
                            d = ctx[r]
                            emit("scalar", lambda e, En=d["En"], pb=pb, psx=d["psx"]: [
                                e.activation(En[pb], psx[:, 1536:2048], AF.Identity)],
                                reads=pk(d["which"], 1536, 2048), writes=[d["kEn"][pb]],
                                tag=f"evn{d['n']}{g}{it}")
                        for r in range(2):
                            d = ctx[r]
                            def etq(e, En=d["En"], Et=d["Et"], pa=pa, psx=d["psx"]):
                                return [e.matmul(psx[:, 1536 + jj * 128:1536 + (jj + 1) * 128],
                                                 En[pa][:, jj * 128:(jj + 1) * 128],
                                                 Et[pa][:, jj * 128:(jj + 1) * 128],
                                                 start=True, stop=True) for jj in range(4)]
                            emit("tensor", etq, reads=[d["kEn"][pa], d["kEt"][pa]],
                                 writes=pk(d["which"], 1536, 2048), tag=f"et{d['n']}{g}{it}")
                        for r in range(2):
                            d = ctx[r]
                            emit("vector", lambda e, Et=d["Et"], pb=pb, psx=d["psx"]: [
                                e.tensor_copy(Et[pb], psx[:, 1536:2048])],
                                reads=pk(d["which"], 1536, 2048), writes=[d["kEt"][pb]],
                                tag=f"evt{d['n']}{g}{it}")
                for r in range(2):
                    d = ctx[r]
                    def trN(e, Xt=d["Xt"], psx=d["psx"]):
                        return [e.matmul(psx[:, 1024 + jj * 128:1024 + (jj + 1) * 128],
                                         Xt[0][:, jj * 128:(jj + 1) * 128], identbf[:],
                                         start=True, stop=True)
                                for jj in range(4)]
                    emit("tensor", trN, reads=[d["kXt"][0], "identbf"],
                         writes=pk(d["which"], 1024, 1536), tag=f"trN{d['n']}{g}")
                for r in range(2):
                    d = ctx[r]
                    emit("vector", lambda e, g=g, r=r, psx=d["psx"]: [
                        e.tensor_copy(Ninv_p[r][:, g * 512:(g + 1) * 512],
                                      psx[:, 1024:1536])],
                        reads=pk(d["which"], 1024, 1536), writes=[f"Ninv{r}:{g}"],
                        tag=f"evN{d['n']}{g}")

            for i in range(NB - 1, -1, -1):
                c0 = i * 128
                cpar = (NB - 1 - i) % 2
                pr = []
                for r in range(2):
                    pr.append(dict(
                        n=2 * h + r, which=("A" if r == 0 else "B"),
                        ps=(psA if r == 0 else psB), ws=ws_p[r], kw=f"ws{r}",
                        wN=wN_p[r], kwN=f"wN{r}", Vsb=Vsb_p[r], Zsb=Zsb_p[r],
                        Ccur=Csb_p[r][cpar], Cnxt=Csb_p[r][1 - cpar],
                        kC=f"C{r}{cpar}", kCn=f"C{r}{1 - cpar}"))
                for r in range(2):
                    d = pr[r]
                    def mmV(e, i=i, c0=c0, ws=d["ws"], Ccur=d["Ccur"], ps=d["ps"]):
                        rr = []
                        for (c, w) in spans(c0, T):
                            rr.append(e.matmul(ps[:, c:c + w], ws[:, c0:c0 + 128],
                                               qs[:, c:c + w], start=True,
                                               stop=(i == NB - 1 or c + w <= c0 + 128)))
                            if i < NB - 1 and c + w > c0 + 128:
                                cc = max(c, c0 + 128)
                                rr.append(e.matmul(ps[:, cc:c + w], ws[:, c0:c0 + 128],
                                                   Ccur[:, cc:c + w], start=False, stop=True))
                        return rr
                    emit("tensor", mmV, reads=[d["kw"], "qs", d["kC"]],
                         writes=pk(d["which"], c0, T), tag=f"V{d['n']}_{i}")
                for r in range(2):
                    d = pr[r]
                    which, ps, Vsb, n = d["which"], d["ps"], d["Vsb"], d["n"]
                    if True:
                        B = min(T, ((c0 + 128 + 511) // 512) * 512)
                        emit("vector", lambda e, Vsb=Vsb, c0=c0, ps=ps, B=B: [
                            e.scalar_tensor_tensor(Vsb[:, c0:c0 + 128], ps[:, c0:c0 + 128],
                                                   1.0, maskKs[:], ALU.mult, ALU.mult)] +
                            ([e.tensor_copy(Vsb[:, c0 + 128:B], ps[:, c0 + 128:B])]
                             if c0 + 128 < B else []),
                            reads=pk(which, c0, B) + ("maskKs",),
                            writes=[f"Vsbd{r}"], tag=f"evVd{n}_{i}")
                        if B < T:
                            emit("scalar", lambda e, Vsb=Vsb, ps=ps, B=B: [
                                e.activation(Vsb[:, B:T], ps[:, B:T], AF.Identity)],
                                reads=pk(which, B, T), writes=[f"Vsb{r}"],
                                tag=f"evV{n}_{i}")
                for r in range(2):
                    d = pr[r]
                    vkeys = [f"Vsb{r}", f"Vsbd{r}"]
                    def mmZ(e, i=i, c0=c0, Vsb=d["Vsb"], ps=d["ps"], r=r):
                        return [e.matmul(ps[:, c:c + w],
                                         Ninv_p[r][:, i * 128:(i + 1) * 128],
                                         Vsb[:, c:c + w], start=True, stop=True)
                                for (c, w) in spans(c0, T)]
                    emit("tensor", mmZ, reads=vkeys + [f"Ninv{r}:{i // 4}"],
                         writes=pk(d["which"], c0, T), tag=f"Z{d['n']}_{i}")
                for r in range(2):
                    d = pr[r]
                    which, ps, Zsb, n = d["which"], d["ps"], d["Zsb"], d["n"]
                    if r == 0:
                        emit("scalar", lambda e, Zsb=Zsb, c0=c0, i=i, n=n, ps=ps: [
                            e.activation(Zsb[:, c0:T], ps[:, c0:T], AF.Identity,
                                         scale=betaneg_tm[:, i * 16 + n:i * 16 + n + 1])],
                            reads=pk(which, c0, T) + ("betaneg_tm",), writes=[f"Zsb{r}"],
                            tag=f"evZ{n}_{i}")
                    else:
                        emit("vector", lambda e, Zsb=Zsb, c0=c0, i=i, n=n, ps=ps: [
                            e.tensor_scalar(Zsb[:, c0:T], ps[:, c0:T],
                                            betaneg_tm[:, i * 16 + n:i * 16 + n + 1],
                                            None, ALU.mult)],
                            reads=pk(which, c0, T) + ("betaneg_tm",), writes=[f"Zsb{r}"],
                            tag=f"evZ{n}_{i}")
                for r in range(2):
                    d = pr[r]
                    def mmM(e, c0=c0, ws=d["ws"], ps=d["ps"]):
                        return [e.matmul(ps[:, c0:c0 + 128], ws[:, c0:c0 + 128],
                                         ks[:, c0:c0 + 128], start=True, stop=True)]
                    emit("tensor", mmM, reads=[d["kw"], "ks"],
                         writes=pk(d["which"], c0, c0 + 128), tag=f"M{d['n']}_{i}")
                for r in range(2):
                    d = pr[r]
                    emit("vector", lambda e, r=r, c0=c0, ps=d["ps"]: [
                        e.tensor_tensor(Mt_p[r], ps[:, c0:c0 + 128], maskSs[:], ALU.mult)],
                        reads=pk(d["which"], c0, c0 + 128) + ("maskSs",), writes=[f"Mt{r}"],
                        tag=f"evM{d['n']}_{i}")
                for r in range(2):
                    d = pr[r]
                    def mmS(e, i=i, c0=c0, Zsb=d["Zsb"], Ccur=d["Ccur"], r=r, ps=d["ps"]):
                        rr = []
                        first = True
                        for (c, w) in spans(c0, T):
                            rr.append(e.matmul(ps[:, c:c + w], ks[:, c0:c0 + 128],
                                               qs[:, c:c + w], start=True, stop=False))
                            if first:
                                rr.append(e.matmul(ps[:, c0:c0 + 128], mbigs[:],
                                                   identbf[:], start=False, stop=False))
                                first = False
                            haveC = (i < NB - 1 and c + w > c0 + 128)
                            rr.append(e.matmul(ps[:, c:c + w], Mt_p[r],
                                               Zsb[:, c:c + w], start=False,
                                               stop=(not haveC)))
                            if haveC:
                                cc = max(c, c0 + 128)
                                rr.append(e.matmul(ps[:, cc:c + w], ks[:, c0:c0 + 128],
                                                   Ccur[:, cc:c + w], start=False, stop=True))
                        return rr
                    emit("tensor", mmS, reads=["ks", "qs", f"Mt{r}", f"Zsb{r}", d["kC"],
                                               "mbigs", "identbf"],
                         writes=pk(d["which"], c0, T), tag=f"S{d['n']}_{i}")
                for r in range(2):
                    d = pr[r]
                    PTt = PT_p[r][i]
                    emit("scalar", lambda e, PTt=PTt, c0=c0, ps=d["ps"]: [
                        e.activation(PTt[:, 0:T - c0], ps[:, c0:T], AF.Exp, scale=ISQD)],
                        reads=pk(d["which"], c0, T), writes=[f"PT{r}:{i}"],
                        tag=f"exp{d['n']}_{i}")

                for r in range(2):
                    d = pr[r]
                    def mmC(e, c0=c0, wN=d["wN"], Zsb=d["Zsb"], ps=d["ps"]):
                        return [e.matmul(ps[:, c:c + w], wN[:, c0:c0 + 128],
                                         Zsb[:, c:c + w], start=True, stop=True)
                                for (c, w) in spans(c0, T)]
                    emit("tensor", mmC, reads=[d["kwN"], f"Zsb{r}"],
                         writes=pk(d["which"], c0, T), tag=f"Cm{d['n']}_{i}")
                for r in range(2):
                    d = pr[r]
                    which, ps = d["which"], d["ps"]
                    Cnxt, Ccur, kC, kCn = d["Cnxt"], d["Ccur"], d["kC"], d["kCn"]
                    if i == NB - 1:
                        emit("vector", lambda e, Cnxt=Cnxt, c0=c0, ps=ps: [
                            e.tensor_copy(Cnxt[:, c0:T], ps[:, c0:T])],
                            reads=pk(which, c0, T), writes=[kCn], tag=f"Cc{d['n']}_{i}")
                    else:
                        emit("vector", lambda e, Cnxt=Cnxt, Ccur=Ccur, c0=c0, ps=ps: [
                            e.tensor_copy(Cnxt[:, c0:c0 + 128], ps[:, c0:c0 + 128]),
                            e.tensor_tensor(Cnxt[:, c0 + 128:T], ps[:, c0 + 128:T],
                                            Ccur[:, c0 + 128:T], ALU.add)],
                            reads=pk(which, c0, T) + (kC,), writes=[kCn],
                            tag=f"Cc{d['n']}_{i}")

            for r in range(2):
                n = 2 * h + r
                ptkeys = [f"PT{r}:{s}" for s in range(NB)]
                def mmo(e, r=r):
                    rr = []
                    for s in range(NB):
                        c0s = s * 128
                        for (c, w) in spans(c0s, T):
                            slast = min(4 * (c // 512) + 3, NB - 1)
                            rr.append(e.matmul(psA[:, c:c + w], vs[:, s * 128:(s + 1) * 128],
                                               PT_p[r][s][:, c - c0s:c - c0s + w],
                                               start=(s == 0), stop=(s == slast)))
                    return rr
                emit("tensor", mmo, reads=["vs"] + ptkeys, writes=pk("A", 0, 2048),
                     tag=f"o{n}")
                def mmcs(e, r=r):
                    rr = []
                    for s in range(NB):
                        c0s = s * 128
                        for (c, w) in spans(c0s, T):
                            slast = min(4 * (c // 512) + 3, NB - 1)
                            rr.append(e.matmul(psB[0:1, c:c + w], onescol[:],
                                               PT_p[r][s][:, c - c0s:c - c0s + w],
                                               start=(s == 0), stop=(s == slast)))
                    return rr
                emit("tensor", mmcs, reads=["ones"] + ptkeys, writes=pk("B", 0, 2048),
                     tag=f"cs{n}")
                emit("vector", [
                    lambda e: [e.reciprocal(csrowf, psB[0:1, 0:2048])],
                    lambda e: [e.tensor_copy(crecipb, csrowf)],
                ], reads=pk("B", 0, 2048), writes=["crecip"], tag=f"orc{n}", seq=True)
                def mmrb2(e):
                    return [e.matmul(psB[:, c:c + 512], onesrow[0:1, :],
                                     crecipb[:, c:c + 512], start=True, stop=True)
                            for c in range(0, T, 512)]
                emit("tensor", mmrb2, reads=["crecip", "ones"], writes=pk("B", 0, 2048),
                     tag=f"orb{n}")
                emit("vector", lambda e: [e.tensor_copy(orbc, psB[:, 0:2048])],
                     reads=pk("B", 0, 2048), writes=["orbc"], tag=f"orbe{n}")
                emit("vector", lambda e: [
                    e.tensor_tensor(obf, psA[:, 0:2048], orbc, ALU.mult)],
                    reads=pk("A", 0, 2048) + ("orbc",), writes=["obf"], tag=f"onrm{n}")
                dma("sdma", o_scr[n * 128:(n + 1) * 128, :], obf, ["obf"], ["o_scr"],
                    tag=f"sto{n}")

        for h in range(n_pairs):
            head_pair(h)
        P.barrier()

        AR.reset()
        xnT2 = AR.alloc(NB * 128 * 8, BF16)
        xnT2_keys = tuple(f"xnT2:{j}" for j in range(8))
        for j in range(8):
            dma("gdma", xnT2[:, j * 2048:(j + 1) * 2048], xnT_scr[j * 128:(j + 1) * 128, :],
                ["xnT_scr"], [f"xnT2:{j}"], tag=f"ldxnT{j}")
        wcw_b = [AR.alloc(2 * HID, BF16) for _ in range(2)]
        bcs = AR.alloc(2 * NEXP, F32)
        wb_b = [AR.alloc(T, BF16) for _ in range(2)]
        gout_b = [AR.alloc(T, BF16) for _ in range(2)]
        wcout_b = [AR.alloc(T, BF16) for _ in range(4)]
        wtsTr = AR.alloc(T, BF16, parts=16)
        wtb = AR.alloc(NB * 16, BF16)
        dma("sdma", bcs, bcolsp[:, :], ["in"], ["bcs"], tag="ldbcols")
        emit("vector", lambda e: [e.tensor_copy(wtb, wts_tm[:, :])],
             reads=["wts_tm"], writes=["wtb"], tag="wtsbf")
        for half in range(2):
            which = "A" if half == 0 else "B"
            psx = psA if half == 0 else psB
            def trw3(e, half=half, psx=psx):
                return [e.matmul(psx[0:16, jj * 128:(jj + 1) * 128],
                                 wtb[:, (half * 8 + jj) * 16:(half * 8 + jj + 1) * 16],
                                 identbf[:], start=True, stop=True) for jj in range(8)]
            emit("tensor", trw3, reads=["wtb", "identbf"], writes=pk(which, 0, 1024),
                 tag=f"trwts{half}")
            emit("vector", lambda e, half=half, psx=psx: [
                e.tensor_copy(wtsTr[:, half * 1024:(half + 1) * 1024], psx[0:16, 0:1024])],
                reads=pk(which, 0, 1024), writes=["wtsTr"], tag=f"evwts{half}")
        for c in range(NEXP):
            p = c % 2
            wcw = wcw_b[p]
            dma("gdma", wcw.rearrange("p (j m) -> p j m", j=8),
                Wcols[c].rearrange("(j p) m -> p j m", p=128),
                ["in"], [f"wcw{p}"], tag=f"ldwc{c}")
            which = "A" if p == 0 else "B"
            ps = psA if p == 0 else psB
            def mmwb(e, c=c, ps=ps):
                return [e.matmul(ps[:, cc:cc + 512], selcs[:, c * 128:(c + 1) * 128],
                                 wtsTr[:, cc:cc + 512], start=True, stop=True)
                        for cc in range(0, T, 512)]
            emit("tensor", mmwb, reads=["wtsTr", "selcs"], writes=pk(which, 0, 2048),
                 tag=f"mmwb{c}")
            emit("vector", lambda e, wb=wb_b[p], ps=ps: [e.tensor_copy(wb, ps[:, 0:2048])],
                 reads=pk(which, 0, 2048), writes=[f"wb{p}"], tag=f"evwb{c}")
            for nt in range(2):
                which2 = "B" if p == 0 else "A"
                ps2 = psB if p == 0 else psA
                def mmcols(e, wcw=wcw, nt=nt, ps2=ps2):
                    r = []
                    for cc in range(0, T, 512):
                        for j in range(8):
                            r.append(e.matmul(
                                ps2[:, cc:cc + 512],
                                wcw[:, j * 256 + nt * 128:j * 256 + (nt + 1) * 128],
                                xnT2[:, j * 2048 + cc:j * 2048 + cc + 512],
                                start=(j == 0), stop=(j == 7)))
                    return r
                emit("tensor", mmcols, reads=(f"wcw{p}",) + xnT2_keys,
                     writes=pk(which2, 0, 2048), tag=f"mmcol{c}_{nt}")
                gout = gout_b[nt]
                emit("scalar", lambda e, gout=gout, c=c, nt=nt, ps2=ps2: [
                    e.activation(gout, ps2[:, 0:2048], AF.Gelu,
                                 bias=bcs[:, c * 2 + nt:c * 2 + nt + 1])],
                    reads=pk(which2, 0, 2048) + ("bcs",), writes=[f"gout{nt}"],
                    tag=f"gel{c}_{nt}")
                wco = wcout_b[2 * p + nt]
                emit("vector", lambda e, gout=gout, wco=wco, wb=wb_b[p]: [
                    e.tensor_tensor(wco, gout, wb, ALU.mult)],
                    reads=[f"gout{nt}", f"wb{p}"], writes=[f"wco{2 * p + nt}"],
                    tag=f"wm{c}_{nt}")
                dma("sdma", wc_scr[(c * 2 + nt) * 128:(c * 2 + nt + 1) * 128, :], wco,
                    [f"wco{2 * p + nt}"], ["wc_scr"], tag=f"stwc{c}_{nt}")
        P.barrier()

        AR.reset()
        pws = AR.alloc(32 * HID, BF16)
        wos = AR.alloc(16 * HID, BF16)
        og_b = [AR.alloc(16 * 128, BF16) for _ in range(2)]
        ao_b = [AR.alloc(HID, BF16) for _ in range(2)]
        pwkeys = tuple(f"pws{j}" for j in range(4))
        dma("gdma", wos.rearrange("p (j m) -> p j m", j=16),
            WoT.rearrange("(j p) m -> p j m", p=128), ["in"], ["wos"], tag="ldWo")
        for j in range(4):
            dma("gdma", pws[:, j * 8192:(j + 1) * 8192].rearrange("p (jj m) -> p jj m", jj=8),
                projw.rearrange("(jj p) m -> p jj m", p=128)[:, j * 8:(j + 1) * 8, :],
                ["in"], [f"pws{j}"], tag=f"ldpw{j}")
        lngbc = AR.alloc(HID, F32); lnbbc = AR.alloc(HID, F32); pbbc = AR.alloc(HID, F32)
        mrow = AR.mark()
        rowf = AR.alloc(HID, F32, parts=1)
        rowh = AR.alloc(HID, BF16, parts=1)
        rowl = AR.alloc(HID, BF16, parts=1)
        rows = AR.alloc(HID, F32, parts=1)
        AR.release(mrow)
        wcg_b = [AR.alloc(32 * 128, BF16) for _ in range(2)]
        xng_b = [AR.alloc(HID, F32) for _ in range(2)]
        xg_b = [AR.alloc(HID, F32) for _ in range(2)]
        yb_b = [AR.alloc(HID, F32) for _ in range(2)]
        yn_b = [AR.alloc(HID, F32) for _ in range(2)]
        ob_b = [AR.alloc(HID, F32) for _ in range(2)]

        def bc_row(srcap, kin, dst, kout, whichps):
            ps = psA if whichps == "A" else psB
            dma("sdma", rowf, srcap, ["in", "rowf"], ["rowf"], tag=f"ld{kout}")
            emit("vector", [
                lambda e: [e.tensor_copy(rowh, rowf)],
                lambda e: [e.tensor_copy(rows, rowh)],
                lambda e: [e.tensor_tensor(rows, rowf, rows, ALU.subtract)],
                lambda e: [e.tensor_copy(rowl, rows)],
            ], reads=["rowf"], writes=["rowhl", "rowf"], tag=f"hl{kout}", seq=True)
            def mm(e):
                r = []
                for cc in range(0, HID, 512):
                    r.append(e.matmul(ps[:, cc:cc + 512], onesrow[0:1, :],
                                      rowh[:, cc:cc + 512], start=True, stop=False))
                    r.append(e.matmul(ps[:, cc:cc + 512], onesrow[0:1, :],
                                      rowl[:, cc:cc + 512], start=False, stop=True))
                return r
            emit("tensor", mm, reads=["rowhl", "ones"], writes=pk(whichps, 0, 1024),
                 tag=f"bc{kout}")
            emit("vector", lambda e, dst=dst: [e.tensor_copy(dst, ps[:, 0:1024])],
                 reads=pk(whichps, 0, 1024), writes=[kout], tag=f"ev{kout}")

        bc_row(projb[0:1, :], "projbr", pbbc, "pbbc", "A")
        bc_row(lng[0:1, :], "lngr", lngbc, "lngbc", "B")
        bc_row(lnb[0:1, :], "lnbr", lnbbc, "lnbbc", "A")
        P.barrier()
        if "bc_scr" in debug_taps:
            dma("sdma", bc_scr[0:128, :], pbbc, ["pbbc"], ["bc_scr"], tag="dbg_bc0")
            dma("sdma", bc_scr[128:256, :], lngbc, ["lngbc"], ["bc_scr"], tag="dbg_bc1")
            dma("sdma", bc_scr[256:384, :], lnbbc, ["lnbbc"], ["bc_scr"], tag="dbg_bc2")

        for i in range(NB):
            p = i % 2
            which = "A" if p == 0 else "B"
            ps = psA if p == 0 else psB
            wcg, xng, xg = wcg_b[p], xng_b[p], xg_b[p]
            yb, yn, ob = yb_b[p], yn_b[p], ob_b[p]
            dma("sdma", wcg,
                wc_scr.rearrange("(kk p) t -> p kk t", p=128)[:, :, i * 128:(i + 1) * 128],
                ["wc_scr"], [f"wcg{p}"], tag=f"ldwcg{i}")
            dma("sdma", xng, xn_scr[i * 128:(i + 1) * 128, :], ["xn_scr"], [f"xng{p}"],
                tag=f"ldxng{i}")
            dma("gdma", xg, x_in[i * 128:(i + 1) * 128, :], ["in"], [f"xg{p}"],
                tag=f"ldxg{i}")
            og = og_b[p]; ao = ao_b[p]
            dma("sdma", og,
                o_scr.rearrange("(nn pp) t -> pp nn t", pp=128)[:, :, i * 128:(i + 1) * 128],
                ["o_scr"], [f"og{p}"], tag=f"ldog{i}")
            def mmy(e, wcg=wcg, ps=ps):
                r = []
                for cc in range(0, HID, 512):
                    for j in range(32):
                        r.append(e.matmul(ps[:, cc:cc + 512], wcg[:, j * 128:(j + 1) * 128],
                                          pws[:, j * 1024 + cc:j * 1024 + cc + 512],
                                          start=(j == 0), stop=(j == 31)))
                return r
            emit("tensor", mmy, reads=(f"wcg{p}",) + pwkeys, writes=pk(which, 0, 1024),
                 tag=f"mmy{i}")
            def mmwo(e, og=og, ps=ps):
                r = []
                for cc in range(0, HID, 512):
                    for j in range(NH):
                        r.append(e.matmul(ps[:, 1024 + cc:1024 + cc + 512],
                                          og[:, j * 128:(j + 1) * 128],
                                          wos[:, j * 1024 + cc:j * 1024 + cc + 512],
                                          start=(j == 0), stop=(j == NH - 1)))
                return r
            emit("tensor", mmwo, reads=[f"og{p}", "wos"], writes=pk(which, 1024, 2048),
                 tag=f"mmwo{i}")
            emit("scalar", lambda e, ao=ao, ps=ps: [
                e.activation(ao, ps[:, 1024:2048], AF.Identity)],
                reads=pk(which, 1024, 2048), writes=[f"ao{p}"], tag=f"evwo{i}")
            emit("vector", [
                lambda e, yb=yb, ps=ps: [e.scalar_tensor_tensor(
                    yb, ps[:, 0:1024], 1.0, pbbc, ALU.mult, ALU.add)],
                lambda e, yb=yb, xng=xng: [e.tensor_tensor(yb, yb, xng, ALU.add)],
            ], reads=pk(which, 0, 1024) + (f"xng{p}", "pbbc"), writes=[f"yb{p}"],
                tag=f"yb{i}", seq=True)
            if "y_scr" in debug_taps:
                dma("sdma", y_scr[i * 128:(i + 1) * 128, :], yb, [f"yb{p}"], ["y_scr"],
                    tag=f"dbg_y{i}")
            R1 = red1[:, p:p + 1]
            R2 = red2[:, p:p + 1]
            kr1, kr2 = f"red1{p}", f"red2{p}"
            emit("vector", [
                lambda e, yb=yb, R1=R1: [e.tensor_reduce(R1, yb, AX.X, ALU.add)],
                lambda e, R1=R1: [e.tensor_scalar(R1, R1, 1.0 / HID, None, ALU.mult)],
                lambda e, yb=yb, yn=yn, R1=R1: [e.tensor_scalar(yn, yb, R1, None, ALU.subtract)],
            ], reads=[f"yb{p}"], writes=[f"yn{p}", kr1], tag=f"ylm{i}", seq=True)
            emit("scalar", lambda e, yn=yn, ob=ob: [e.activation(ob, yn, AF.Square)],
                 reads=[f"yn{p}"], writes=[f"ob{p}"], tag=f"yls{i}")
            emit("vector", [
                lambda e, ob=ob, R2=R2: [e.tensor_reduce(R2, ob, AX.X, ALU.add)],
                lambda e, R2=R2: [e.tensor_scalar(R2, R2, 1.0 / HID, None, ALU.mult)],
            ], reads=[f"ob{p}"], writes=[kr2], tag=f"ylv{i}", seq=True)
            emit("scalar", [
                lambda e, R2=R2: [e.activation(R2, R2, AF.Ln, bias=eps_ln[:])],
                lambda e, R2=R2: [e.activation(R2, R2, AF.Exp, scale=-0.5)],
            ], reads=[kr2, "eps"], writes=[kr2], tag=f"ylr{i}", seq=True)
            emit("vector", [
                lambda e, yn=yn, R2=R2: [e.tensor_scalar(yn, yn, R2, None, ALU.mult)],
                lambda e, yn=yn: [e.tensor_tensor(yn, yn, lngbc, ALU.mult)],
                lambda e, yn=yn, ob=ob: [e.tensor_tensor(ob, yn, lnbbc, ALU.add)],
                lambda e, ob=ob, xg=xg: [e.tensor_tensor(ob, ob, xg, ALU.add)],
                lambda e, ob=ob, ao=ao: [e.tensor_tensor(ob, ob, ao, ALU.add)],
            ], reads=[f"yn{p}", kr2, "lngbc", "lnbbc", f"xg{p}", f"ao{p}"],
                writes=[f"ob{p}", f"yn{p}"], tag=f"yf{i}", seq=True)
            dma("sdma", out[i * 128:(i + 1) * 128, :], ob, [f"ob{p}"], ["out"],
                tag=f"stout{i}")

        def body(ename):
            def run(eng):
                for waits, fn, tag, drain, seq, sk, tv in P.groups[ename]:
                    for dk, val in waits:
                        eng.wait_ge(sems[dk], val)
                    if drain:
                        eng.drain()
                    if seq:
                        instrs = None
                        for ix, sub in enumerate(fn):
                            if ix:
                                eng.drain()
                            instrs = sub(eng)
                    else:
                        instrs = fn(eng)
                    assert instrs, f"empty group {tag} on {ename}"
                    if ename in ("tensor", "vector", "scalar", "pool"):
                        instrs[-1].then_inc(sems[sk], 1)
                    else:
                        instrs[-1].then_inc(sems[sk], 16)
                if ename == "sdma":
                    for qe in ("sdma", "gdma"):
                        for j in range(Prog.NSEM):
                            k2 = f"{qe}{j}"
                            u = P.sem_uses.get(k2, 0)
                            if u:
                                eng.wait_ge(sems[k2], u * 16)
            return run

        block.tensor(body("tensor"))
        block.vector(body("vector"))
        block.scalar(body("scalar"))
        block.gpsimd(body("gdma"))
        block.sync(body("sdma"))

    return nc


def host_inputs(inp):
    f = np.ascontiguousarray
    d = {}
    d["WqT"] = f(inp["L0_attn_Wq"].T)
    d["WkT"] = f(inp["L0_attn_Wk"].T)
    d["WvT"] = f(inp["L0_attn_Wv"].T)
    d["WoT"] = f(inp["L0_attn_Wo"].T)
    d["wA1T"] = f(inp["L0_attn_wA1"].T)
    d["wA2T"] = f(inp["L0_attn_wA2"].T)
    d["wB1T"] = f(inp["L0_attn_wB1"].T)
    d["wB2T"] = f(inp["L0_attn_wB2"].T)
    d["convAp"] = f(inp["L0_attn_convA"].reshape(16, 128, 3).transpose(1, 0, 2).reshape(128, 48))
    d["convBp"] = f(inp["L0_attn_convB"].reshape(16, 128, 3).transpose(1, 0, 2).reshape(128, 48))
    d["btwT"] = f(inp["L0_attn_bt_w"].T)
    d["btb"] = f(inp["L0_attn_bt_b"].reshape(16, 1))
    d["phi"] = f(inp["L0_attn_phi"].reshape(2, 1))
    d["omr"] = f(inp["L0_attn_omega_raw"].reshape(2, 1))
    d["colemb"] = f(inp["L0_ffn_col_emb"])
    d["gw1"] = f(inp["L0_ffn_gate_w1"])
    d["gb1p"] = f(inp["L0_ffn_gate_b1"].reshape(4, 128).T)
    d["gw2"] = f(inp["L0_ffn_gate_w2"])
    d["gb2"] = f(inp["L0_ffn_gate_b2"].reshape(1, 16))
    d["Wcols"] = f(inp["L0_ffn_W_cols"])
    d["bcolsp"] = f(inp["L0_ffn_b_cols"].reshape(16, 2, 128).transpose(2, 0, 1).reshape(128, 32))
    d["projw"] = f(inp["L0_ffn_proj_w"])
    d["projb"] = f(inp["L0_ffn_proj_b"].reshape(1, HID))
    d["lng"] = f(inp["L0_ffn_ln_g"].reshape(1, HID))
    d["lnb"] = f(inp["L0_ffn_ln_b"].reshape(1, HID))
    d["identin"] = np.eye(128, dtype=np.float32)
    d["maskSin"] = np.tril(np.ones((128, 128), np.float32), -1)
    d["maskKin"] = np.triu(np.ones((128, 128), np.float32))
    d["pos4in"] = np.broadcast_to(np.arange(T, dtype=np.float32), (4, T)).copy()
    sel = np.zeros((16, 16 * 128), np.float32)
    for c in range(16):
        sel[c, c * 128:(c + 1) * 128] = 1.0
    d["selcin"] = sel
    sel4 = np.zeros((4, 4 * 128), np.float32)
    for c in range(4):
        sel4[c, c * 128:(c + 1) * 128] = 1.0
    d["sel4in"] = sel4
    d["dup24in"] = np.array([[1, 1, 0, 0], [0, 0, 1, 1]], np.float32)
    mb = np.zeros((128, 128), np.float32)
    mb[np.triu_indices(128, 1)] = -1e6
    d["mbigin"] = mb
    hp = float(np.float32(0.5 * math.pi))
    d["phsh4in"] = np.array([[hp], [0.0], [hp], [0.0]], np.float32)
    return d


def kernel(**inputs):
    shared = host_inputs(inputs)
    x = np.ascontiguousarray(inputs["L0_attn_hidden_states"].astype(np.float32))
    nc = build_nc()
    in_maps = []
    for b in range(8):
        m = dict(shared)
        m["x"] = np.ascontiguousarray(x[b])
        in_maps.append(m)
    res = run_bass_kernel_spmd(nc, in_maps, core_ids=list(range(8)))
    out = np.stack([res.results[b]["out"] for b in range(8)], axis=0)
    return out.astype(np.float32)




# version 12
# speedup vs baseline: 1.0331x; 1.0331x over previous
import math
import numpy as np
import concourse.bass as bass
import concourse.mybir as mybir
from concourse.bass_utils import run_bass_kernel_spmd

F32 = mybir.dt.float32
BF16 = mybir.dt.bfloat16
I32 = mybir.dt.int32
U8 = mybir.dt.uint8
AF = mybir.ActivationFunctionType
ALU = mybir.AluOpType
AX = mybir.AxisListType

T = 2048
HID = 1024
H, R, D = 8, 2, 128
NH = H * R
NB = T // 128
NEXP = 16
OMEGA_SCALE = 2.0 * math.pi / (512 * 16.0)
ISQD = 1.0 / math.sqrt(D)
TWO_PI = float(np.float32(2.0 * math.pi))
INV_2PI = float(np.float32(1.0 / (2.0 * math.pi)))

N_PAIRS = 8
ARENA_B = 177 * 1024


def spans(c0, c1, bank=512):
    out = []
    c = c0
    while c < c1:
        e = min(c1, (c // bank + 1) * bank)
        out.append((c, e - c))
        c = e
    return out


class Prog:
    CENG = ("tensor", "vector", "scalar", "pool")
    DENG = ("sdma", "gdma")
    ENGS = CENG + DENG
    NSEM = 6

    def __init__(self):
        self.groups = {e: [] for e in self.ENGS}
        self.count = {e: 0 for e in self.ENGS}
        self.sem_uses = {}
        self.writer = {}
        self.readers = {}
        self.vclock = {e: {} for e in self.ENGS}
        self.drained = {}
        self.barrier_snap = None

    def _token_for(self, eng):
        if eng in self.CENG:
            return (eng, eng, self.count[eng] + 1)
        j = self.count[eng] % self.NSEM
        sk = f"{eng}{j}"
        return (eng, sk, (self.sem_uses.get(sk, 0) + 1) * 16)

    def emit(self, eng, fn, reads=(), writes=(), tag="", seq=False):
        deps = {}
        own_dep = [0]
        def add(tok):
            de, sk, dv = tok
            if de == eng and de in self.CENG:
                if de == "tensor":
                    return
                own_dep[0] = max(own_dep[0], dv)
                return
            if deps.get(sk, 0) < dv:
                deps[sk] = dv
        if self.barrier_snap:
            for tok in self.barrier_snap:
                add(tok)
        for k in reads:
            w = self.writer.get(k)
            if w:
                add(w)
        for k in writes:
            w = self.writer.get(k)
            if w:
                add(w)
            for rdep in self.readers.get(k, ()):
                add(rdep)
        tok = self._token_for(eng)
        if eng in self.DENG:
            sk = tok[1]
            prev = self.sem_uses.get(sk, 0)
            if prev > 0:
                if deps.get(sk, 0) < prev * 16:
                    deps[sk] = prev * 16
            self.sem_uses[sk] = prev + 1
        vc = self.vclock[eng]
        waits = []
        for sk, dv in deps.items():
            if vc.get(sk, 0) >= dv:
                continue
            vc[sk] = dv
            waits.append((sk, dv))
        drain = False
        if own_dep[0] > self.drained.get(eng, 0):
            drain = True
            self.drained[eng] = self.count[eng]
        self.count[eng] += 1
        self.groups[eng].append((waits, fn, tag, drain, seq, tok[1], tok[2]))
        me = tok
        for k in writes:
            self.writer[k] = me
            self.readers[k] = []
        for k in reads:
            self.readers.setdefault(k, []).append(me)
        return me

    def barrier(self):
        snap = []
        for e in self.CENG:
            if self.count[e]:
                snap.append((e, e, self.count[e]))
        for e in self.DENG:
            for j in range(self.NSEM):
                sk = f"{e}{j}"
                u = self.sem_uses.get(sk, 0)
                if u:
                    snap.append((e, sk, u * 16))
        self.barrier_snap = snap


class Arena:
    def __init__(self, ap_u8, nbytes):
        self.ap = ap_u8
        self.nbytes = nbytes
        self.off = 0

    def mark(self):
        return self.off

    def reset(self):
        self.off = 0

    def release(self, m):
        self.off = m

    def alloc(self, cols, dtype, parts=128):
        esz = 2 if dtype in (BF16,) else 4
        start = (self.off + 31) // 32 * 32
        self.off = start + cols * esz
        assert self.off <= self.nbytes, f"arena overflow {self.off}"
        view = self.ap.bitcast(dtype)
        v = view[:, start // esz:start // esz + cols]
        if parts != 128:
            v = v[0:parts, :]
        return v


def build_nc(debug_taps=(), n_pairs=N_PAIRS):
    nc = bass.Bass("TRN2", target_bir_lowering=False, debug=False)

    def din(name, shape, dtype=F32):
        return nc.dram_tensor(name, list(shape), dtype, kind="ExternalInput").ap()

    def dscr(name, shape, dtype):
        kind = "ExternalOutput" if name in debug_taps else "Internal"
        return nc.dram_tensor(name, list(shape), dtype, kind=kind).ap()

    x_in = din("x", [T, HID])
    WqT = din("WqT", [HID, HID]); WkT = din("WkT", [HID, HID]); WvT = din("WvT", [HID, HID])
    WoT = din("WoT", [2 * HID, HID])
    wA1T = din("wA1T", [HID, 32]); wA2T = din("wA2T", [32, 2 * HID])
    wB1T = din("wB1T", [HID, 32]); wB2T = din("wB2T", [32, 2 * HID])
    convAp = din("convAp", [128, NB * 3]); convBp = din("convBp", [128, NB * 3])
    btwT = din("btwT", [HID, 16]); btb = din("btb", [16, 1])
    phi_in = din("phi", [2, 1]); omr_in = din("omr", [2, 1])
    colemb = din("colemb", [16, HID])
    gw1 = din("gw1", [HID, 512]); gb1p = din("gb1p", [128, 4])
    gw2 = din("gw2", [512, 16]); gb2 = din("gb2", [1, 16])
    Wcols = din("Wcols", [NEXP, HID, 256])
    bcolsp = din("bcolsp", [128, 2 * NEXP])
    projw = din("projw", [4 * HID, HID])
    projb = din("projb", [1, HID])
    lng = din("lng", [1, HID]); lnb = din("lnb", [1, HID])
    identin = din("identin", [128, 128])
    maskSin = din("maskSin", [128, 128])
    maskKin = din("maskKin", [128, 128])
    pos4in = din("pos4in", [4, T])
    selcin = din("selcin", [16, 16 * 128])
    sel4in = din("sel4in", [4, 4 * 128])
    mbigin = din("mbigin", [128, 128])
    dup24in = din("dup24in", [2, 4])
    phsh4in = din("phsh4in", [4, 1])

    out = nc.dram_tensor("out", [T, HID], F32, kind="ExternalOutput").ap()

    xn_scr = dscr("xn_scr", [T, HID], F32)
    xnT_scr = dscr("xnT_scr", [HID, T], BF16)
    qT_scr = dscr("qT_scr", [HID, T], BF16)
    kT_scr = dscr("kT_scr", [HID, T], BF16)
    v_scr = dscr("v_scr", [T, HID], BF16)
    W_scr = dscr("W_scr", [2 * HID, T], BF16)
    o_scr = dscr("o_scr", [NH * 128, T], BF16)
    attn_scr = dscr("attn_scr", [T, HID], F32)
    wc_scr = dscr("wc_scr", [4 * HID, T], BF16)
    beta_scr = dscr("beta_scr", [128, NB * 16], F32)
    wts_scr = dscr("wts_scr", [128, NB * 16], F32)
    y_scr = dscr("y_scr", [T, HID], F32)
    bc_scr = dscr("bc_scr", [3 * 128, HID], F32)

    P = Prog()

    from contextlib import ExitStack
    with ExitStack() as ST:
        def sb(name, shape, dtype):
            return ST.enter_context(nc.sbuf_tensor(name, shape, dtype))
        arena_t = sb("arena", [128, ARENA_B], U8)
        identbf = sb("identbf", [128, 128], BF16)
        maskSs = sb("maskSs", [128, 128], F32)
        maskKs = sb("maskKs", [128, 128], F32)
        onescol = sb("onescol", [128, 1], BF16)
        onesrow = sb("onesrow", [1, 128], BF16)
        selcs = sb("selcs", [16, 16 * 128], BF16)
        sel4s = sb("sel4s", [4, 4 * 128], BF16)
        mbigs = sb("mbigs", [128, 128], BF16)
        dup24s = sb("dup24s", [2, 4], F32)
        phsh4s = sb("phsh4s", [4, 1], F32)
        sm2 = sb("sm2", [2, 8], F32)
        sm4 = sb("sm4", [4, 8], F32)
        btbs = sb("btbs", [16, 1], F32)
        beta_tm = sb("beta_tm", [128, NB * 16], F32)
        betaneg_tm = sb("betaneg_tm", [128, NB * 16], F32)
        rnorm_tm = sb("rnorm_tm", [128, NB], F32)
        svar = sb("svar", [128, NB], F32)
        srstd = sb("srstd", [128, NB], F32)
        wts_tm = sb("wts_tm", [128, NB * 16], F32)
        eps_ln = sb("eps_ln", [128, 1], F32)
        eps24 = sb("eps24", [128, 1], F32)
        gb2bc = sb("gb2bc", [128, 16], F32)
        red1 = sb("red1", [128, 2], F32)
        red2 = sb("red2", [128, 2], F32)
        red3 = sb("red3", [128, 2], F32)
        t16a = sb("t16a", [128, 16], F32)
        t16b = sb("t16b", [128, 16], F32)
        t16c = sb("t16c", [128, 16], F32)
        psA = ST.enter_context(nc.psum_tensor("psA", [128, 2048], F32))
        psB = ST.enter_context(nc.psum_tensor("psB", [128, 2048], F32))
        smt = ST.enter_context(nc.semaphore("smt"))
        smv = ST.enter_context(nc.semaphore("smv"))
        sms = ST.enter_context(nc.semaphore("sms"))
        smp = ST.enter_context(nc.semaphore("smp"))
        dma_sems = {}
        for qe in ("sdma", "gdma"):
            for j in range(Prog.NSEM):
                dma_sems[f"{qe}{j}"] = ST.enter_context(nc.semaphore(f"{qe}{j}"))
        block = ST.enter_context(nc.Block())

        AR = Arena(arena_t[:, :], ARENA_B)
        sems = {"tensor": smt, "vector": smv, "scalar": sms, "pool": smp}
        sems.update(dma_sems)

        def pk(which, c0, c1):
            return tuple(f"ps{which}:{b}" for b in range(c0 // 512, (c1 + 511) // 512))

        emit = P.emit

        def dma(eng, dst, src, reads, writes, tag=""):
            emit(eng, lambda e, d=dst, s=src: [e.dma_start(out=d, in_=s)],
                 reads=reads, writes=writes, tag=tag)

        dma("gdma", identbf[:], identin[:, :], ["in"], ["identbf"])
        dma("sdma", maskSs[:], maskSin[:, :], ["in"], ["maskSs"])
        dma("sdma", maskKs[:], maskKin[:, :], ["in"], ["maskKs"])
        dma("gdma", selcs[:], selcin[:, :], ["in"], ["selcs"])
        dma("gdma", sel4s[:], sel4in[:, :], ["in"], ["sel4s"])
        dma("gdma", mbigs[:], mbigin[:, :], ["in"], ["mbigs"])
        dma("sdma", dup24s[:], dup24in[:, :], ["in"], ["dup24s"])
        dma("sdma", phsh4s[:], phsh4in[:, :], ["in"], ["phsh4s"])
        dma("sdma", btbs[:], btb[:, :], ["in"], ["btbs"])
        dma("sdma", sm2[0:2, 0:1], phi_in[:, :], ["in"], ["phi"])
        dma("sdma", sm2[0:2, 1:2], omr_in[:, :], ["in"], ["omr"])
        emit("vector", lambda e: [
            e.memset(onescol[:], 1.0), e.memset(onesrow[:], 1.0),
            e.memset(eps_ln[:], 1e-5), e.memset(eps24[:], 1e-24),
        ], writes=["ones", "eps"], tag="consts")

        xnT_sb = AR.alloc(NB * 128 * 8, BF16)
        xnT_keys = tuple(f"xnT:{i}" for i in range(NB))
        xnT_view = xnT_sb.rearrange("p (j t) -> p j t", j=8)

        m0 = AR.mark()
        xt_b = [AR.alloc(HID, F32) for _ in range(2)]
        xc_b = [AR.alloc(HID, F32) for _ in range(2)]
        xnf_b = [AR.alloc(HID, F32) for _ in range(2)]
        xnb_b = [AR.alloc(HID, BF16) for _ in range(2)]
        for i in range(NB):
            p = i % 2
            xt, xc, xnf, xnb = xt_b[p], xc_b[p], xnf_b[p], xnb_b[p]
            kx, kc, kn, knb = f"xt{p}", f"xc{p}", f"xnf{p}", f"xnb{p}"
            dma("sdma", xt, x_in[i * 128:(i + 1) * 128, :], ["in"], [kx], tag=f"ldx{i}")
            R1 = red1[:, p:p + 1]
            R2 = red2[:, p:p + 1]
            R3 = red3[:, p:p + 1]
            kr1, kr2, kr3 = f"red1{p}", f"red2{p}", f"red3{p}"
            emit("vector", [
                lambda e, xt=xt, R1=R1: [e.tensor_reduce(R1, xt, AX.X, ALU.add)],
                lambda e, R1=R1: [e.tensor_scalar(R1, R1, 1.0 / HID, None, ALU.mult)],
                lambda e, xt=xt, xc=xc, R1=R1: [e.tensor_scalar(xc, xt, R1, None, ALU.subtract)],
            ], reads=[kx], writes=[kc, kr1], tag=f"lnm{i}", seq=True)
            emit("scalar", lambda e, xc=xc, xnf=xnf: [e.activation(xnf, xc, AF.Square)],
                 reads=[kc], writes=[kn], tag=f"lnsq{i}")
            emit("vector", [
                lambda e, xnf=xnf, i=i: [e.tensor_reduce(svar[:, i:i + 1], xnf, AX.X, ALU.add)],
                lambda e, i=i: [e.tensor_scalar(svar[:, i:i + 1], svar[:, i:i + 1],
                                                1.0 / HID, None, ALU.mult)],
            ], reads=[kn], writes=[kr2, "svar"], tag=f"lnv{i}", seq=True)
            emit("scalar", [
                lambda e, i=i, R3=R3: [e.activation(R3, svar[:, i:i + 1], AF.Ln,
                                                    bias=eps_ln[:])],
                lambda e, i=i, R3=R3: [e.activation(srstd[:, i:i + 1], R3, AF.Exp,
                                                    scale=-0.5)],
            ], reads=[kr2, "eps", "svar"], writes=[kr3, "srstd"], tag=f"lnr{i}", seq=True)
            emit("vector", lambda e, xc=xc, xnf=xnf, i=i: [
                e.tensor_scalar(xnf, xc, srstd[:, i:i + 1], None, ALU.mult)],
                reads=[kc, "srstd"], writes=[kn], tag=f"lnx{i}")
            emit("scalar", lambda e, xnf=xnf, xnb=xnb: [e.activation(xnb, xnf, AF.Identity)],
                 reads=[kn], writes=[knb], tag=f"lnxb{i}")
            dma("sdma", xn_scr[i * 128:(i + 1) * 128, :], xnf, [kn], ["xn_scr"], tag=f"stxn{i}")
            which = "A" if p == 0 else "B"
            psx = psA if p == 0 else psB
            def trx(e, xnb=xnb, psx=psx):
                return [e.matmul(psx[:, j * 128:(j + 1) * 128],
                                 xnb[:, j * 128:(j + 1) * 128], identbf[:],
                                 start=True, stop=True)
                        for j in range(8)]
            emit("tensor", trx, reads=[knb, "identbf"], writes=pk(which, 0, 1024),
                 tag=f"trx{i}")
            emit("vector", lambda e, psx=psx, i=i: [
                e.tensor_copy(xnT_view[:, :, i * 128:(i + 1) * 128],
                              psx[:, 0:1024].rearrange("p (j c) -> p j c", j=8))],
                 reads=pk(which, 0, 1024), writes=[f"xnT:{i}"], tag=f"evx{i}")
        for j in range(8):
            dma("sdma", xnT_scr[j * 128:(j + 1) * 128, :],
                xnT_sb[:, j * 2048:(j + 1) * 2048], list(xnT_keys), ["xnT_scr"],
                tag=f"stxnT{j}")

        emit("vector", [
            lambda e: [e.tensor_tensor(rnorm_tm[:, :], srstd[:, :], srstd[:, :], ALU.mult)],
            lambda e: [e.tensor_tensor(rnorm_tm[:, :], rnorm_tm[:, :], svar[:, :], ALU.mult)],
            lambda e: [e.tensor_scalar(rnorm_tm[:, :], rnorm_tm[:, :], float(HID), None, ALU.mult)],
            lambda e: [e.tensor_scalar(rnorm_tm[:, :], rnorm_tm[:, :], eps24[:], None, ALU.max)],
        ], reads=["svar", "srstd", "eps"], writes=["rnorm"], tag="rnb", seq=True)
        emit("scalar", [
            lambda e: [e.activation(rnorm_tm[:, :], rnorm_tm[:, :], AF.Ln)],
            lambda e: [e.activation(rnorm_tm[:, :], rnorm_tm[:, :], AF.Exp, scale=-0.5)],
        ], reads=["rnorm"], writes=["rnorm"], tag="rnb2", seq=True)

        btwTs = AR.alloc(8 * 16, BF16)
        betaT_f = AR.alloc(T, F32, parts=16)
        betaT_bf = AR.alloc(T, BF16, parts=16)
        dma("gdma", btwTs.rearrange("p (j m) -> p j m", j=8), btwT.rearrange("(j p) m -> p j m", p=128), ["in"], ["btwTs"])
        def mmbeta(e):
            r = []
            for c in range(0, T, 512):
                for j in range(8):
                    r.append(e.matmul(psA[0:16, c:c + 512], btwTs[:, j * 16:(j + 1) * 16],
                                      xnT_sb[:, j * 2048 + c:j * 2048 + c + 512],
                                      start=(j == 0), stop=(j == 7)))
            return r
        emit("tensor", mmbeta, reads=("btwTs",) + xnT_keys, writes=pk("A", 0, 2048),
             tag="mmbeta")
        emit("scalar", lambda e: [
            e.activation(betaT_f, psA[0:16, 0:2048], AF.Sigmoid, bias=btbs[:])],
            reads=pk("A", 0, 2048) + ("btbs",), writes=["betaTf"], tag="betasig")
        emit("vector", lambda e: [
            e.tensor_scalar(betaT_bf, betaT_f, 2.0, None, ALU.mult)],
            reads=["betaTf"], writes=["betaTbf"], tag="beta2")
        for half in range(2):
            which = "A" if half == 0 else "B"
            psx = psA if half == 0 else psB
            def trb2(e, half=half, psx=psx):
                r = []
                for jj in range(8):
                    i = half * 8 + jj
                    r.append(e.matmul(psx[:, jj * 16:(jj + 1) * 16],
                                      betaT_bf[:, i * 128:(i + 1) * 128],
                                      identbf[0:16, 0:16], start=True, stop=True))
                return r
            emit("tensor", trb2, reads=["betaTbf", "identbf"], writes=pk(which, 0, 128),
                 tag=f"trbeta{half}")
            def evb(e, half=half, psx=psx):
                r = []
                for jj in range(8):
                    i = half * 8 + jj
                    r.append(e.tensor_copy(beta_tm[:, i * 16:(i + 1) * 16],
                                           psx[:, jj * 16:(jj + 1) * 16]))
                return r
            emit("vector", evb, reads=pk(which, 0, 128), writes=["beta_tm"],
                 tag=f"evbeta{half}")
        emit("vector", lambda e: [
            e.tensor_scalar(betaneg_tm[:, :], beta_tm[:, :], -1.0, None, ALU.mult)],
            reads=["beta_tm"], writes=["betaneg_tm"], tag="betaneg")
        dma("sdma", beta_scr[:, :], beta_tm[:, :], ["beta_tm"], ["beta_scr"], tag="stbeta")
        AR.release(m0)
        P.barrier()

        cbc = [AR.alloc(T, BF16) for _ in range(2)]
        sbc = [AR.alloc(T, BF16) for _ in range(2)]
        m1 = AR.mark()
        pos4 = AR.alloc(T, F32, parts=4)
        th4 = AR.alloc(T, F32, parts=4)
        sc4 = AR.alloc(T, F32, parts=4)
        si4 = AR.alloc(T, I32, parts=4)
        csrow = AR.alloc(T, BF16, parts=4)
        dma("sdma", pos4, pos4in[:, :], ["in"], ["pos4"], tag="ldpos")
        emit("scalar", lambda e: [
            e.activation(sm2[0:2, 2:3], sm2[0:2, 1:2], AF.Abs)],
            reads=["omr"], writes=["sp1"], tag="spa")
        emit("scalar", lambda e: [
            e.activation(sm2[0:2, 3:4], sm2[0:2, 2:3], AF.Exp, scale=-1.0)],
            reads=["sp1"], writes=["sp2"], tag="spe")
        emit("vector", lambda e: [
            e.tensor_scalar(sm2[0:2, 4:5], sm2[0:2, 3:4], 1.0, None, ALU.add),
            e.tensor_scalar(sm2[0:2, 5:6], sm2[0:2, 1:2], 0.0, None, ALU.max),
        ], reads=["sp2", "omr"], writes=["sp3"], tag="spv")
        emit("scalar", lambda e: [
            e.activation(sm2[0:2, 6:7], sm2[0:2, 4:5], AF.Ln)],
            reads=["sp3"], writes=["sp4"], tag="spl")
        emit("vector", [
            lambda e: [e.tensor_tensor(sm2[0:2, 7:8], sm2[0:2, 5:6], sm2[0:2, 6:7], ALU.add)],
            lambda e: [e.tensor_scalar(sm2[0:2, 7:8], sm2[0:2, 7:8], OMEGA_SCALE, None, ALU.mult)],
        ], reads=["sp4", "sp3"], writes=["omega2"], tag="spf", seq=True)
        emit("tensor", lambda e: [
            e.matmul(psA[0:4, 0:1], dup24s[:, :], sm2[0:2, 0:1], start=True, stop=True),
            e.matmul(psA[0:4, 1:2], dup24s[:, :], sm2[0:2, 7:8], start=True, stop=True),
        ], reads=["phi", "omega2", "dup24s"], writes=pk("A", 0, 2), tag="dupmm")
        emit("vector", lambda e: [
            e.tensor_tensor(sm4[0:4, 0:1], psA[0:4, 0:1], phsh4s[:, :], ALU.add),
            e.tensor_copy(sm4[0:4, 1:2], psA[0:4, 1:2]),
        ], reads=pk("A", 0, 2) + ("phsh4s",), writes=["pom4"], tag="dupev")
        emit("vector", lambda e: [
            e.tensor_scalar(th4, pos4, sm4[0:4, 1:2], sm4[0:4, 0:1], ALU.mult, ALU.add),
        ], reads=["pos4", "pom4"], writes=["th4"], tag="th4")
        emit("vector", [
            lambda e: [e.tensor_scalar(sc4, th4, INV_2PI, None, ALU.mult)],
            lambda e: [e.tensor_copy(si4, sc4)],
            lambda e: [e.tensor_copy(th4, si4)],
            lambda e: [e.tensor_tensor(sc4, sc4, th4, ALU.subtract)],
        ], reads=["th4"], writes=["sc4", "th4"], tag="rr4", seq=True)
        emit("scalar", lambda e: [
            e.activation(th4, sc4, AF.Sin, scale=TWO_PI)],
            reads=["sc4"], writes=["sin4"], tag="sin4")
        emit("vector", lambda e: [e.tensor_copy(csrow, th4)],
             reads=["sin4"], writes=["csrow"], tag="csbf")
        for r in range(2):
            for ci in range(2):
                idx = 2 * r + ci
                which = "A" if idx % 2 == 0 else "B"
                ps = psA if idx % 2 == 0 else psB
                def mmbc(e, idx=idx, ps=ps):
                    return [e.matmul(ps[:, c:c + 512], sel4s[:, idx * 128:(idx + 1) * 128],
                                     csrow[:, c:c + 512], start=True, stop=True)
                            for c in range(0, T, 512)]
                emit("tensor", mmbc, reads=["csrow", "sel4s"], writes=pk(which, 0, 2048),
                     tag=f"bccs{idx}")
                dst = cbc[r] if ci == 0 else sbc[r]
                kcs = f"cs{r}{ci}"
                emit("vector", lambda e, dst=dst, ps=ps: [e.tensor_copy(dst, ps[:, 0:2048])],
                     reads=pk(which, 0, 2048), writes=[kcs], tag=f"evcs{idx}")
        AR.release(m1)
        P.barrier()

        m1 = AR.mark()
        wA1Ts = AR.alloc(8 * 32, BF16); wB1Ts = AR.alloc(8 * 32, BF16)
        wA2Ts = AR.alloc(2 * HID, BF16, parts=32); wB2Ts = AR.alloc(2 * HID, BF16, parts=32)
        convAs = AR.alloc(NB * 3, F32); convBs = AR.alloc(NB * 3, F32)
        dma("gdma", wA1Ts.rearrange("p (j m) -> p j m", j=8), wA1T.rearrange("(j p) m -> p j m", p=128), ["in"], ["wA1Ts"])
        dma("gdma", wB1Ts.rearrange("p (j m) -> p j m", j=8), wB1T.rearrange("(j p) m -> p j m", p=128), ["in"], ["wB1Ts"])
        dma("gdma", wA2Ts, wA2T[:, :], ["in"], ["wA2Ts"])
        dma("gdma", wB2Ts, wB2T[:, :], ["in"], ["wB2Ts"])
        dma("sdma", convAs, convAp[:, :], ["in"], ["convAs"])
        dma("sdma", convBs, convBp[:, :], ["in"], ["convBs"])
        uTA = AR.alloc(T, BF16, parts=32)
        uTB = AR.alloc(T, BF16, parts=32)
        wn_b = [AR.alloc(T, BF16) for _ in range(2)]
        sq_b = [AR.alloc(T, BF16) for _ in range(2)]
        cv_b = [AR.alloc(T, F32) for _ in range(2)]
        cv2_b = [AR.alloc(T, F32) for _ in range(2)]
        xa_b = [AR.alloc(T, BF16) for _ in range(2)]
        xb_b = [AR.alloc(T, BF16) for _ in range(2)]
        wfin_b = [AR.alloc(T, BF16) for _ in range(2)]
        rinvS = AR.alloc(T, F32, parts=1)
        rrowS = AR.alloc(T, BF16, parts=1)
        rbcA = AR.alloc(T, F32); rbcB = AR.alloc(T, F32)

        def mmu(e, w1s, uT):
            r = []
            for c in range(0, T, 512):
                for j in range(8):
                    r.append(e.matmul(psA[0:32, c:c + 512], w1s[:, j * 32:(j + 1) * 32],
                                      xnT_sb[:, j * 2048 + c:j * 2048 + c + 512],
                                      start=(j == 0), stop=(j == 7)))
            return r
        for (pn, w1s, w2s, k1, k2, uT) in (("A", wA1Ts, wA2Ts, "wA1Ts", "wA2Ts", uTA),
                                           ("B", wB1Ts, wB2Ts, "wB1Ts", "wB2Ts", uTB)):
            emit("tensor", lambda e, w1s=w1s, uT=uT: mmu(e, w1s, uT),
                 reads=(k1,) + xnT_keys, writes=pk("A", 0, 2048), tag=f"mmu{pn}")
            emit("vector", lambda e, uT=uT: [e.tensor_copy(uT, psA[0:32, 0:2048])],
                 reads=pk("A", 0, 2048), writes=[f"uT{pn}"], tag=f"evu{pn}")
            for j in range(NB):
                p = j % 2
                def mmw(e, j=j, w2s=w2s, uT=uT):
                    return [e.matmul(psA[:, c:c + 512], w2s[:, j * 128:(j + 1) * 128],
                                     uT[:, c:c + 512], start=True, stop=True)
                            for c in range(0, T, 512)]
                emit("tensor", mmw, reads=[k2, f"uT{pn}"], writes=pk("A", 0, 2048),
                     tag=f"p1m{pn}{j}")
                emit("scalar", lambda e, sq=sq_b[p]: [
                    e.activation(sq, psA[:, 0:2048], AF.Square)],
                    reads=pk("A", 0, 2048), writes=[f"sq{p}"], tag=f"p1s{pn}{j}")
                def mmss(e, sq=sq_b[p], j=j):
                    return [e.matmul(psB[0:1, c:c + 512], onescol[:], sq[:, c:c + 512],
                                     start=(j == 0), stop=(j == NB - 1))
                            for c in range(0, T, 512)]
                emit("tensor", mmss, reads=[f"sq{p}", "ones"], writes=pk("B", 0, 2048),
                     tag=f"p1ss{pn}{j}")
            rinv = rinvS
            rrow = rrowS
            rbc = rbcA if pn == "A" else rbcB
            emit("scalar", [
                lambda e, rinv=rinv: [e.activation(rinv, psB[0:1, 0:2048], AF.Ln)],
                lambda e, rinv=rinv: [e.activation(rinv, rinv, AF.Exp, scale=-0.5)],
            ], reads=pk("B", 0, 2048), writes=[f"rinv{pn}"], tag=f"ri{pn}", seq=True)
            emit("vector", lambda e, rinv=rinv, rrow=rrow: [e.tensor_copy(rrow, rinv)],
                 reads=[f"rinv{pn}"], writes=[f"rrow{pn}"], tag=f"rr{pn}")
            def mmrb(e, rrow=rrow):
                return [e.matmul(psB[:, c:c + 512], onesrow[0:1, :], rrow[:, c:c + 512],
                                 start=True, stop=True) for c in range(0, T, 512)]
            emit("tensor", mmrb, reads=[f"rrow{pn}", "ones"], writes=pk("B", 0, 2048),
                 tag=f"rbm{pn}")
            emit("vector", lambda e, rbc=rbc: [e.tensor_copy(rbc, psB[:, 0:2048])],
                 reads=pk("B", 0, 2048), writes=[f"rbc{pn}"], tag=f"rbe{pn}")
            emit("vector", lambda e, uT=uT, rbc=rbc: [
                e.tensor_tensor(uT, uT, rbc[0:32, :], ALU.mult)],
                reads=[f"uT{pn}", f"rbc{pn}"], writes=[f"uT{pn}"], tag=f"usc{pn}")

        for j in range(NB):
            p = j % 2
            r_ = j % 2
            for (pn, w2s, k2, uT, convs, kconv, xres, kxr, rbc, krbc) in (
                    ("A", wA2Ts, "wA2Ts", uTA, convAs, "convAs", xa_b[p], f"xa{p}", rbcA, "rbcA"),
                    ("B", wB2Ts, "wB2Ts", uTB, convBs, "convBs", xb_b[p], f"xb{p}", rbcB, "rbcB")):
                wn, cv, cv2 = wn_b[p], cv_b[p], cv2_b[p]
                def mmw2(e, j=j, w2s=w2s, uT=uT):
                    return [e.matmul(psA[:, c:c + 512], w2s[:, j * 128:(j + 1) * 128],
                                     uT[:, c:c + 512], start=True, stop=True)
                            for c in range(0, T, 512)]
                emit("tensor", mmw2, reads=[k2, f"uT{pn}"], writes=pk("A", 0, 2048),
                     tag=f"p2m{pn}{j}")
                emit("scalar", lambda e, wn=wn: [
                    e.activation(wn, psA[:, 0:2048], AF.Identity)],
                    reads=pk("A", 0, 2048), writes=[f"wn{p}"], tag=f"p2n{pn}{j}")
                emit("scalar", lambda e, wn=wn, cv=cv, j=j, convs=convs: [
                    e.activation(cv, wn, AF.Identity, scale=convs[:, j * 3 + 2:j * 3 + 3])],
                    reads=[f"wn{p}", kconv], writes=[f"cv{p}"], tag=f"cva{pn}{j}")
                emit("vector", lambda e, wn=wn, cv=cv, cv2=cv2, j=j, convs=convs: [
                    e.scalar_tensor_tensor(cv2[:, 1:T], wn[:, 0:T - 1],
                                           convs[:, j * 3 + 1:j * 3 + 2], cv[:, 1:T],
                                           ALU.mult, ALU.add),
                    e.tensor_copy(cv2[:, 0:1], cv[:, 0:1]),
                ], reads=[f"wn{p}", f"cv{p}", kconv], writes=[f"cv2{p}"], tag=f"cvb{pn}{j}")
                emit("vector", lambda e, wn=wn, cv=cv, cv2=cv2, j=j, convs=convs: [
                    e.scalar_tensor_tensor(cv[:, 2:T], wn[:, 0:T - 2],
                                           convs[:, j * 3 + 0:j * 3 + 1], cv2[:, 2:T],
                                           ALU.mult, ALU.add),
                    e.tensor_copy(cv[:, 0:2], cv2[:, 0:2]),
                ], reads=[f"wn{p}", f"cv2{p}", kconv], writes=[f"cv{p}"], tag=f"cvc{pn}{j}")
                emit("scalar", lambda e, cv=cv, xres=xres: [
                    e.activation(xres, cv, AF.Silu)],
                    reads=[f"cv{p}"], writes=[kxr], tag=f"sil{pn}{j}")
            wf = wfin_b[p]
            emit("vector", lambda e, p=p, r_=r_, wf=wf: [
                e.tensor_tensor(cv_b[p], xa_b[p], cbc[r_], ALU.mult),
                e.tensor_tensor(cv2_b[p], xb_b[p], sbc[r_], ALU.mult),
            ], reads=[f"xa{p}", f"xb{p}", f"cs{r_}0", f"cs{r_}1"],
                writes=[f"cv{p}", f"cv2{p}"], tag=f"wc1{j}")
            emit("vector", lambda e, p=p, wf=wf: [
                e.tensor_tensor(wf, cv_b[p], cv2_b[p], ALU.add)],
                reads=[f"cv{p}", f"cv2{p}"], writes=[f"wf{p}"], tag=f"wc2{j}")
            dma("sdma", W_scr[j * 128:(j + 1) * 128, :], wf, [f"wf{p}"], ["W_scr"],
                tag=f"stW{j}")
        AR.release(m1)
        P.barrier()

        m1 = AR.mark()
        wbig = AR.alloc(8 * HID, BF16)
        stg_b = [AR.alloc(T, BF16) for _ in range(2)]
        for (nameW, Wap, scr, tmaj) in (("q", WqT, qT_scr, False),
                                        ("k", WkT, kT_scr, False),
                                        ("v", WvT, v_scr, True)):
            dma("gdma", wbig.rearrange("p (j m) -> p j m", j=8),
                Wap.rearrange("(j p) m -> p j m", p=128),
                ["in", "wbig"], ["wbig"], tag=f"ldW{nameW}")
            if not tmaj:
                for mch in range(8):
                    p = mch % 2
                    which = "A" if p == 0 else "B"
                    ps = psA if p == 0 else psB
                    def mmq(e, mch=mch, ps=ps):
                        r = []
                        for c in range(0, T, 512):
                            for j in range(8):
                                r.append(e.matmul(
                                    ps[:, c:c + 512],
                                    wbig[:, j * 1024 + mch * 128:j * 1024 + (mch + 1) * 128],
                                    xnT_sb[:, j * 2048 + c:j * 2048 + c + 512],
                                    start=(j == 0), stop=(j == 7)))
                        return r
                    emit("tensor", mmq, reads=("wbig",) + xnT_keys,
                         writes=pk(which, 0, 2048), tag=f"mm{nameW}{mch}")
                    emit("vector", lambda e, ps=ps, st=stg_b[p]: [
                        e.tensor_copy(st, ps[:, 0:2048])],
                        reads=pk(which, 0, 2048), writes=[f"stg{p}"], tag=f"ev{nameW}{mch}")
                    dma("sdma", scr[mch * 128:(mch + 1) * 128, :], stg_b[p],
                        [f"stg{p}"], [f"{nameW}_scr"], tag=f"st{nameW}{mch}")
            else:
                for i in range(NB):
                    p = i % 2
                    which = "A" if p == 0 else "B"
                    ps = psA if p == 0 else psB
                    def mmv(e, i=i, ps=ps):
                        r = []
                        for c in range(0, HID, 512):
                            for j in range(8):
                                r.append(e.matmul(
                                    ps[:, c:c + 512],
                                    xnT_sb[:, j * 2048 + i * 128:j * 2048 + (i + 1) * 128],
                                    wbig[:, j * 1024 + c:j * 1024 + c + 512],
                                    start=(j == 0), stop=(j == 7)))
                        return r
                    emit("tensor", mmv, reads=("wbig",) + xnT_keys,
                         writes=pk(which, 0, 1024), tag=f"mmv{i}")
                    emit("vector", lambda e, ps=ps, st=stg_b[p]: [
                        e.tensor_copy(st[:, 0:1024], ps[:, 0:1024])],
                        reads=pk(which, 0, 1024), writes=[f"stg{p}"], tag=f"evv{i}")
                    dma("sdma", scr[i * 128:(i + 1) * 128, :], stg_b[p][:, 0:1024],
                        [f"stg{p}"], ["v_scr"], tag=f"stv{i}")
        AR.release(m1)
        P.barrier()

        m1 = AR.mark()
        gw1s = AR.alloc(4 * HID, BF16)
        gb1s = AR.alloc(4, F32)
        hT_t = [AR.alloc(T, BF16) for _ in range(4)]
        gw2s = AR.alloc(4 * 16, BF16)
        ceu = AR.alloc(HID, F32, parts=16)
        ceub = AR.alloc(HID, BF16, parts=16)
        cesq = AR.alloc(HID, F32, parts=16)
        ceuT = AR.alloc(8 * 16, BF16)
        dma("gdma", gw1s.rearrange("p (j m) -> p j m", j=8), gw1.rearrange("(j p) m -> p j m", p=128), ["in"], ["gw1s"])
        dma("sdma", gb1s, gb1p[:, :], ["in"], ["gb1s"])
        dma("gdma", gw2s.rearrange("p (j m) -> p j m", j=4), gw2.rearrange("(j p) m -> p j m", p=128), ["in"], ["gw2s"])
        dma("sdma", ceu, colemb[:, :], ["in"], ["ceu"], tag="ldce")
        dma("sdma", gb2bc[0:1, :], gb2[0:1, :], ["in"], ["gb2r"], tag="ldgb2")
        emit("vector", lambda e: [e.tensor_copy(t16a[0:1, 0:16].bitcast(BF16)[0:1, 0:16]
                                                if False else t16b[0:1, 0:16], gb2bc[0:1, 0:16])],
             reads=["gb2r"], writes=["gb2row"], tag="gb2cp") if False else None
        gb2rowb = AR.alloc(16, BF16, parts=1)
        emit("vector", lambda e: [e.tensor_copy(gb2rowb, gb2bc[0:1, 0:16])],
             reads=["gb2r"], writes=["gb2rowb"], tag="gb2cp")
        emit("tensor", lambda e: [e.matmul(psA[:, 0:16], onesrow[0:1, :], gb2rowb,
                                           start=True, stop=True)],
             reads=["gb2rowb", "ones"], writes=pk("A", 0, 16), tag="gb2mm")
        emit("vector", lambda e: [e.tensor_copy(gb2bc[:, :], psA[:, 0:16])],
             reads=pk("A", 0, 16), writes=["gb2bc"], tag="gb2ev")
        emit("scalar", lambda e: [e.activation(cesq, ceu, AF.Square)],
             reads=["ceu"], writes=["cesq"], tag="cesq")
        emit("vector", [
            lambda e: [e.tensor_reduce(t16a[0:16, 0:1], cesq, AX.X, ALU.add)],
            lambda e: [e.tensor_scalar(t16a[0:16, 0:1], t16a[0:16, 0:1], eps24[0:16, :], None, ALU.max)],
        ], reads=["cesq", "eps"], writes=["cen2"], tag="cered", seq=True)
        emit("scalar", [
            lambda e: [e.activation(t16a[0:16, 0:1], t16a[0:16, 0:1], AF.Ln)],
            lambda e: [e.activation(t16a[0:16, 0:1], t16a[0:16, 0:1], AF.Exp, scale=-0.5)],
        ], reads=["cen2"], writes=["cen2"], tag="ceri", seq=True)
        emit("vector", lambda e: [
            e.tensor_scalar(ceub, ceu, t16a[0:16, 0:1], None, ALU.mult)],
            reads=["ceu", "cen2"], writes=["ceub"], tag="ceun")
        def cetr(e):
            return [e.matmul(psB[:, j * 16:(j + 1) * 16],
                             ceub[:, j * 128:(j + 1) * 128], identbf[0:16, 0:16],
                             start=True, stop=True)
                    for j in range(8)]
        emit("tensor", cetr, reads=["ceub", "identbf"], writes=pk("B", 0, 128), tag="cetr")
        emit("vector", lambda e: [e.tensor_copy(ceuT, psB[:, 0:128])],
             reads=pk("B", 0, 128), writes=["ceuT"], tag="ceuTev")
        for mch in range(4):
            which = "A" if mch % 2 == 0 else "B"
            ps = psA if mch % 2 == 0 else psB
            def mmh(e, mch=mch, ps=ps):
                r = []
                for c in range(0, T, 512):
                    for j in range(8):
                        r.append(e.matmul(ps[:, c:c + 512],
                                          gw1s[:, j * 512 + mch * 128:j * 512 + (mch + 1) * 128],
                                          xnT_sb[:, j * 2048 + c:j * 2048 + c + 512],
                                          start=(j == 0), stop=(j == 7)))
                return r
            emit("tensor", mmh, reads=("gw1s",) + xnT_keys, writes=pk(which, 0, 2048),
                 tag=f"mmh{mch}")
            emit("scalar", lambda e, ps=ps, mch=mch: [
                e.activation(hT_t[mch], ps[:, 0:2048], AF.Gelu, bias=gb1s[:, mch:mch + 1])],
                reads=pk(which, 0, 2048) + ("gb1s",), writes=[f"hT{mch}"], tag=f"gelh{mch}")
        for i in range(NB):
            which = "A" if i % 2 == 0 else "B"
            ps = psA if i % 2 == 0 else psB
            def mml(e, i=i, ps=ps):
                r = []
                for j in range(8):
                    r.append(e.matmul(ps[:, 0:16],
                                      xnT_sb[:, j * 2048 + i * 128:j * 2048 + (i + 1) * 128],
                                      ceuT[:, j * 16:(j + 1) * 16],
                                      start=(j == 0), stop=(j == 7)))
                for j in range(4):
                    r.append(e.matmul(ps[:, 16:32], hT_t[j][:, i * 128:(i + 1) * 128],
                                      gw2s[:, j * 16:(j + 1) * 16],
                                      start=(j == 0), stop=(j == 3)))
                return r
            emit("tensor", mml,
                 reads=tuple(f"hT{m}" for m in range(4)) + ("ceuT", "gw2s") + xnT_keys,
                 writes=pk(which, 0, 32), tag=f"mml{i}")
            emit("vector", lambda e, ps=ps: [
                e.tensor_tensor(t16a[:, 0:16], ps[:, 16:32], gb2bc[:, 0:16], ALU.add)],
                reads=pk(which, 0, 32) + ("gb2bc",), writes=["t16a"], tag=f"lga{i}")
            emit("scalar", lambda e: [
                e.activation(t16a[:, 0:16], t16a[:, 0:16], AF.Sigmoid)],
                reads=["t16a"], writes=["t16a"], tag=f"lgs{i}")
            emit("vector", [
                lambda e, ps=ps, i=i: [e.scalar_tensor_tensor(
                    t16c[:, 0:16], ps[:, 0:16], rnorm_tm[:, i:i + 1],
                    t16a[:, 0:16], ALU.mult, ALU.add)],
                lambda e: [e.tensor_reduce(red1[:, 0:1], t16c[:, 0:16], AX.X, ALU.max)],
                lambda e: [e.tensor_scalar(t16b[:, 0:16], t16c[:, 0:16], red1[:, 0:1], None, ALU.is_equal)],
            ], reads=pk(which, 0, 32) + ("t16a", "rnorm"),
                writes=["lgt", "msk", "red10"], tag=f"lgm{i}", seq=True)
            emit("scalar", lambda e: [
                e.activation(t16c[:, 0:16], t16c[:, 0:16], AF.Exp)],
                reads=["lgt"], writes=["lgt"], tag=f"lge{i}")
            emit("vector", [
                lambda e: [e.tensor_reduce(red2[:, 0:1], t16c[:, 0:16], AX.X, ALU.add)],
                lambda e: [e.reciprocal(red2[:, 0:1], red2[:, 0:1])],
                lambda e: [e.tensor_scalar(t16c[:, 0:16], t16c[:, 0:16], red2[:, 0:1], None, ALU.mult)],
                lambda e, i=i: [e.tensor_tensor(wts_tm[:, i * 16:(i + 1) * 16], t16c[:, 0:16],
                                                t16b[:, 0:16], ALU.mult)],
            ], reads=["lgt", "msk"], writes=["wts_tm", "red20"], tag=f"lgw{i}", seq=True)
        dma("sdma", wts_scr[:, :], wts_tm[:, :], ["wts_tm"], ["wts_scr"], tag="stwts")
        AR.release(m1)
        P.barrier()

        AR.reset()
        qs_b = [AR.alloc(T, BF16) for _ in range(2)]
        ks_b = [AR.alloc(T, BF16) for _ in range(2)]
        vs_b = [AR.alloc(T, BF16) for _ in range(2)]
        ws_pb = [[AR.alloc(T, BF16) for _ in range(2)] for _ in range(2)]
        wN_p = [AR.alloc(T, BF16) for _ in range(2)]
        Ninv_p = [AR.alloc(NB * 128, BF16) for _ in range(2)]
        Lb_p = [AR.alloc(512, BF16) for _ in range(2)]
        Lt_p = [AR.alloc(512, BF16) for _ in range(2)]
        Xt_p = [[AR.alloc(512, BF16) for _ in range(2)] for _ in range(2)]
        En_p = [[AR.alloc(512, BF16) for _ in range(2)] for _ in range(2)]
        Et_p = [[AR.alloc(512, BF16) for _ in range(2)] for _ in range(2)]
        Vsb_p = [AR.alloc(T, BF16) for _ in range(2)]
        Zsb_p = [AR.alloc(T, BF16) for _ in range(2)]
        Csb_p = [AR.alloc(T, BF16) for _ in range(2)]
        Mt_p = [AR.alloc(128, BF16) for _ in range(2)]
        PT_p = [[AR.alloc((NB - s) * 128, BF16) for s in range(NB)] for _ in range(2)]
        crecipb = AR.alloc(T, BF16, parts=1)
        orbc = AR.alloc(T, BF16)
        obf = AR.alloc(T, BF16)

        def head_pair(h):
            hp = h % 2
            qs, ks, vs = qs_b[hp], ks_b[hp], vs_b[hp]
            ws_p = ws_pb[hp]
            kqs, kks, kvs = f"qs{hp}", f"ks{hp}", f"vs{hp}"
            dma("gdma", qs, qT_scr[h * 128:(h + 1) * 128, :], ["q_scr"], [kqs], tag=f"ldq{h}")
            dma("gdma", ks, kT_scr[h * 128:(h + 1) * 128, :], ["k_scr"], [kks], tag=f"ldk{h}")
            dma("sdma", vs,
                v_scr.rearrange("(s p) (hh d) -> p s hh d", p=128, d=128)[:, :, h, :],
                ["v_scr"], [kvs], tag=f"ldv{h}")
            for r in range(2):
                n = 2 * h + r
                dma("gdma", ws_p[r], W_scr[n * 128:(n + 1) * 128, :], ["W_scr"],
                    [f"ws{r}{hp}"], tag=f"ldw{n}")

            ctx = []
            for r in range(2):
                ctx.append(dict(
                    n=2 * h + r, ws=ws_p[r], kw=f"ws{r}{hp}", wN=wN_p[r], kwN=f"wN{r}",
                    which=("A" if r == 0 else "B"), psx=(psA if r == 0 else psB),
                    Lb=Lb_p[r], Lt=Lt_p[r], Xt=Xt_p[r], En=En_p[r], Et=Et_p[r],
                    kL=f"L{r}", kLt=f"Lt{r}",
                    kXt=[f"Xt{r}0", f"Xt{r}1"], kEn=[f"En{r}0", f"En{r}1"],
                    kEt=[f"Et{r}0", f"Et{r}1"]))
            for half in range(2):
                for r in range(2):
                    d = ctx[r]
                    def trw(e, half=half, ws=d["ws"], psx=d["psx"]):
                        return [e.matmul(psx[:, jj * 128:(jj + 1) * 128],
                                         ws[:, (half * 8 + jj) * 128:(half * 8 + jj + 1) * 128],
                                         identbf[:], start=True, stop=True)
                                for jj in range(8)]
                    emit("tensor", trw, reads=[d["kw"], "identbf"],
                         writes=pk(d["which"], 0, 1024), tag=f"trw{d['n']}{half}")
                for r in range(2):
                    d = ctx[r]
                    emit("scalar", lambda e, half=half, wN=d["wN"], psx=d["psx"]: [
                        e.activation(wN[:, half * 1024:(half + 1) * 1024], psx[:, 0:1024],
                                     AF.Identity)],
                        reads=pk(d["which"], 0, 1024), writes=[d["kwN"]],
                        tag=f"evw{d['n']}{half}")
            for g in range(4):
                for r in range(2):
                    d = ctx[r]
                    def mmG(e, g=g, ws=d["ws"], psx=d["psx"]):
                        return [e.matmul(psx[:, 1024 + jj * 128:1024 + (jj + 1) * 128],
                                         ws[:, (g * 4 + jj) * 128:(g * 4 + jj + 1) * 128],
                                         ws[:, (g * 4 + jj) * 128:(g * 4 + jj + 1) * 128],
                                         start=True, stop=True) for jj in range(4)]
                    emit("tensor", mmG, reads=[d["kw"]], writes=pk(d["which"], 1024, 1536),
                         tag=f"G{d['n']}{g}")
                for r in range(2):
                    d = ctx[r]
                    def evL(e, g=g, n=d["n"], psx=d["psx"], Lb=d["Lb"]):
                        return [e.scalar_tensor_tensor(
                            Lb[:, jj * 128:(jj + 1) * 128],
                            psx[:, 1024 + jj * 128:1024 + (jj + 1) * 128],
                            beta_tm[:, (g * 4 + jj) * 16 + n:(g * 4 + jj) * 16 + n + 1],
                            maskSs[:], ALU.mult, ALU.mult) for jj in range(4)]
                    emit("vector", evL, reads=pk(d["which"], 1024, 1536) + ("beta_tm", "maskSs"),
                         writes=[d["kL"]], tag=f"evL{d['n']}{g}")
                for r in range(2):
                    d = ctx[r]
                    def trL(e, Lb=d["Lb"], psx=d["psx"]):
                        return [e.matmul(psx[:, 1536 + jj * 128:1536 + (jj + 1) * 128],
                                         Lb[:, jj * 128:(jj + 1) * 128], identbf[:],
                                         start=True, stop=True)
                                for jj in range(4)]
                    emit("tensor", trL, reads=[d["kL"], "identbf"],
                         writes=pk(d["which"], 1536, 2048), tag=f"trL{d['n']}{g}")
                for r in range(2):
                    d = ctx[r]
                    emit("vector", lambda e, Lt=d["Lt"], psx=d["psx"]: [
                        e.tensor_copy(Lt, psx[:, 1536:2048])],
                        reads=pk(d["which"], 1536, 2048), writes=[d["kLt"]],
                        tag=f"evLt{d['n']}{g}")
                for r in range(2):
                    d = ctx[r]
                    emit("vector", lambda e, Lt=d["Lt"], Xt=d["Xt"]: [
                        e.tensor_tensor(Xt[0][:, jj * 128:(jj + 1) * 128], identbf[:],
                                        Lt[:, jj * 128:(jj + 1) * 128], ALU.subtract)
                        for jj in range(4)],
                        reads=[d["kLt"], "identbf"], writes=[d["kXt"][0]],
                        tag=f"x0{d['n']}{g}")
                for r in range(2):
                    d = ctx[r]
                    def e0(e, Lb=d["Lb"], Lt=d["Lt"], psx=d["psx"]):
                        r2 = []
                        for jj in range(4):
                            s = slice(jj * 128, (jj + 1) * 128)
                            r2.append(e.matmul(psx[:, 1024 + jj * 128:1024 + (jj + 1) * 128],
                                               Lt[:, s], Lb[:, s], start=True, stop=True))
                            r2.append(e.matmul(psx[:, 1536 + jj * 128:1536 + (jj + 1) * 128],
                                               Lb[:, s], Lt[:, s], start=True, stop=True))
                        return r2
                    emit("tensor", e0, reads=[d["kL"], d["kLt"]],
                         writes=pk(d["which"], 1024, 2048), tag=f"e0{d['n']}{g}")
                for r in range(2):
                    d = ctx[r]
                    emit("vector", lambda e, En=d["En"], Et=d["Et"], psx=d["psx"]: [
                        e.tensor_copy(En[0], psx[:, 1024:1536]),
                        e.tensor_copy(Et[0], psx[:, 1536:2048]),
                    ], reads=pk(d["which"], 1024, 2048), writes=[d["kEn"][0], d["kEt"][0]],
                        tag=f"eve0{d['n']}{g}")
                for it in range(6):
                    pa, pb = it % 2, 1 - it % 2
                    for r in range(2):
                        d = ctx[r]
                        def xup(e, En=d["En"], Xt=d["Xt"], pa=pa, psx=d["psx"]):
                            return [e.matmul(psx[:, 1024 + jj * 128:1024 + (jj + 1) * 128],
                                             En[pa][:, jj * 128:(jj + 1) * 128],
                                             Xt[pa][:, jj * 128:(jj + 1) * 128],
                                             start=True, stop=True) for jj in range(4)]
                        emit("tensor", xup, reads=[d["kEn"][pa], d["kXt"][pa]],
                             writes=pk(d["which"], 1024, 1536), tag=f"xu{d['n']}{g}{it}")
                    for r in range(2):
                        d = ctx[r]
                        emit("vector", lambda e, Xt=d["Xt"], pa=pa, pb=pb, psx=d["psx"]: [
                            e.tensor_tensor(Xt[pb], psx[:, 1024:1536], Xt[pa], ALU.add)],
                            reads=pk(d["which"], 1024, 1536) + (d["kXt"][pa],),
                            writes=[d["kXt"][pb]], tag=f"xa{d['n']}{g}{it}")
                    if it < 5:
                        for r in range(2):
                            d = ctx[r]
                            def esq(e, En=d["En"], Et=d["Et"], pa=pa, psx=d["psx"]):
                                return [e.matmul(psx[:, 1536 + jj * 128:1536 + (jj + 1) * 128],
                                                 Et[pa][:, jj * 128:(jj + 1) * 128],
                                                 En[pa][:, jj * 128:(jj + 1) * 128],
                                                 start=True, stop=True) for jj in range(4)]
                            emit("tensor", esq, reads=[d["kEt"][pa], d["kEn"][pa]],
                                 writes=pk(d["which"], 1536, 2048), tag=f"en{d['n']}{g}{it}")
                        for r in range(2):
                            d = ctx[r]
                            emit("scalar", lambda e, En=d["En"], pb=pb, psx=d["psx"]: [
                                e.activation(En[pb], psx[:, 1536:2048], AF.Identity)],
                                reads=pk(d["which"], 1536, 2048), writes=[d["kEn"][pb]],
                                tag=f"evn{d['n']}{g}{it}")
                        for r in range(2):
                            d = ctx[r]
                            def etq(e, En=d["En"], Et=d["Et"], pa=pa, psx=d["psx"]):
                                return [e.matmul(psx[:, 1536 + jj * 128:1536 + (jj + 1) * 128],
                                                 En[pa][:, jj * 128:(jj + 1) * 128],
                                                 Et[pa][:, jj * 128:(jj + 1) * 128],
                                                 start=True, stop=True) for jj in range(4)]
                            emit("tensor", etq, reads=[d["kEn"][pa], d["kEt"][pa]],
                                 writes=pk(d["which"], 1536, 2048), tag=f"et{d['n']}{g}{it}")
                        for r in range(2):
                            d = ctx[r]
                            emit("vector", lambda e, Et=d["Et"], pb=pb, psx=d["psx"]: [
                                e.tensor_copy(Et[pb], psx[:, 1536:2048])],
                                reads=pk(d["which"], 1536, 2048), writes=[d["kEt"][pb]],
                                tag=f"evt{d['n']}{g}{it}")
                for r in range(2):
                    d = ctx[r]
                    def trN(e, Xt=d["Xt"], psx=d["psx"]):
                        return [e.matmul(psx[:, 1024 + jj * 128:1024 + (jj + 1) * 128],
                                         Xt[0][:, jj * 128:(jj + 1) * 128], identbf[:],
                                         start=True, stop=True)
                                for jj in range(4)]
                    emit("tensor", trN, reads=[d["kXt"][0], "identbf"],
                         writes=pk(d["which"], 1024, 1536), tag=f"trN{d['n']}{g}")
                for r in range(2):
                    d = ctx[r]
                    emit("vector", lambda e, g=g, r=r, psx=d["psx"]: [
                        e.tensor_copy(Ninv_p[r][:, g * 512:(g + 1) * 512],
                                      psx[:, 1024:1536])],
                        reads=pk(d["which"], 1024, 1536), writes=[f"Ninv{r}:{g}"],
                        tag=f"evN{d['n']}{g}")

            for i in range(NB - 1, -1, -1):
                c0 = i * 128
                pr = []
                for r in range(2):
                    pr.append(dict(
                        n=2 * h + r, which=("A" if r == 0 else "B"),
                        ps=(psA if r == 0 else psB), ws=ws_p[r], kw=f"ws{r}{hp}",
                        wN=wN_p[r], kwN=f"wN{r}", Vsb=Vsb_p[r], Zsb=Zsb_p[r],
                        Ccur=Csb_p[r], Cnxt=Csb_p[r],
                        kC=f"C{r}", kCn=f"C{r}"))
                for r in range(2):
                    d = pr[r]
                    def mmV(e, i=i, c0=c0, ws=d["ws"], Ccur=d["Ccur"], ps=d["ps"]):
                        rr = []
                        for (c, w) in spans(c0, T):
                            rr.append(e.matmul(ps[:, c:c + w], ws[:, c0:c0 + 128],
                                               qs[:, c:c + w], start=True,
                                               stop=(i == NB - 1 or c + w <= c0 + 128)))
                            if i < NB - 1 and c + w > c0 + 128:
                                cc = max(c, c0 + 128)
                                rr.append(e.matmul(ps[:, cc:c + w], ws[:, c0:c0 + 128],
                                                   Ccur[:, cc:c + w], start=False, stop=True))
                        return rr
                    emit("tensor", mmV, reads=[d["kw"], kqs, d["kC"]],
                         writes=pk(d["which"], c0, T), tag=f"V{d['n']}_{i}")
                for r in range(2):
                    d = pr[r]
                    which, ps, Vsb, n = d["which"], d["ps"], d["Vsb"], d["n"]
                    if True:
                        B = min(T, ((c0 + 128 + 511) // 512) * 512)
                        emit("vector", lambda e, Vsb=Vsb, c0=c0, ps=ps, B=B: [
                            e.scalar_tensor_tensor(Vsb[:, c0:c0 + 128], ps[:, c0:c0 + 128],
                                                   1.0, maskKs[:], ALU.mult, ALU.mult)] +
                            ([e.tensor_copy(Vsb[:, c0 + 128:B], ps[:, c0 + 128:B])]
                             if c0 + 128 < B else []),
                            reads=pk(which, c0, B) + ("maskKs",),
                            writes=[f"Vsbd{r}"], tag=f"evVd{n}_{i}")
                        if B < T:
                            emit("scalar", lambda e, Vsb=Vsb, ps=ps, B=B: [
                                e.activation(Vsb[:, B:T], ps[:, B:T], AF.Identity)],
                                reads=pk(which, B, T), writes=[f"Vsb{r}"],
                                tag=f"evV{n}_{i}")
                for r in range(2):
                    d = pr[r]
                    vkeys = [f"Vsb{r}", f"Vsbd{r}"]
                    def mmZ(e, i=i, c0=c0, Vsb=d["Vsb"], ps=d["ps"], r=r):
                        return [e.matmul(ps[:, c:c + w],
                                         Ninv_p[r][:, i * 128:(i + 1) * 128],
                                         Vsb[:, c:c + w], start=True, stop=True)
                                for (c, w) in spans(c0, T)]
                    emit("tensor", mmZ, reads=vkeys + [f"Ninv{r}:{i // 4}"],
                         writes=pk(d["which"], c0, T), tag=f"Z{d['n']}_{i}")
                for r in range(2):
                    d = pr[r]
                    which, ps, Zsb, n = d["which"], d["ps"], d["Zsb"], d["n"]
                    if r == 0:
                        emit("scalar", lambda e, Zsb=Zsb, c0=c0, i=i, n=n, ps=ps: [
                            e.activation(Zsb[:, c0:T], ps[:, c0:T], AF.Identity,
                                         scale=betaneg_tm[:, i * 16 + n:i * 16 + n + 1])],
                            reads=pk(which, c0, T) + ("betaneg_tm",), writes=[f"Zsb{r}"],
                            tag=f"evZ{n}_{i}")
                    else:
                        emit("vector", lambda e, Zsb=Zsb, c0=c0, i=i, n=n, ps=ps: [
                            e.tensor_scalar(Zsb[:, c0:T], ps[:, c0:T],
                                            betaneg_tm[:, i * 16 + n:i * 16 + n + 1],
                                            None, ALU.mult)],
                            reads=pk(which, c0, T) + ("betaneg_tm",), writes=[f"Zsb{r}"],
                            tag=f"evZ{n}_{i}")
                for r in range(2):
                    d = pr[r]
                    def mmM(e, c0=c0, ws=d["ws"], ps=d["ps"]):
                        return [e.matmul(ps[:, c0:c0 + 128], ws[:, c0:c0 + 128],
                                         ks[:, c0:c0 + 128], start=True, stop=True)]
                    emit("tensor", mmM, reads=[d["kw"], kks],
                         writes=pk(d["which"], c0, c0 + 128), tag=f"M{d['n']}_{i}")
                for r in range(2):
                    d = pr[r]
                    emit("vector", lambda e, r=r, c0=c0, ps=d["ps"]: [
                        e.tensor_tensor(Mt_p[r], ps[:, c0:c0 + 128], maskSs[:], ALU.mult)],
                        reads=pk(d["which"], c0, c0 + 128) + ("maskSs",), writes=[f"Mt{r}"],
                        tag=f"evM{d['n']}_{i}")
                for r in range(2):
                    d = pr[r]
                    def mmS(e, i=i, c0=c0, Zsb=d["Zsb"], Ccur=d["Ccur"], r=r, ps=d["ps"]):
                        rr = []
                        first = True
                        for (c, w) in spans(c0, T):
                            rr.append(e.matmul(ps[:, c:c + w], ks[:, c0:c0 + 128],
                                               qs[:, c:c + w], start=True, stop=False))
                            if first:
                                rr.append(e.matmul(ps[:, c0:c0 + 128], mbigs[:],
                                                   identbf[:], start=False, stop=False))
                                first = False
                            haveC = (i < NB - 1 and c + w > c0 + 128)
                            rr.append(e.matmul(ps[:, c:c + w], Mt_p[r],
                                               Zsb[:, c:c + w], start=False,
                                               stop=(not haveC)))
                            if haveC:
                                cc = max(c, c0 + 128)
                                rr.append(e.matmul(ps[:, cc:c + w], ks[:, c0:c0 + 128],
                                                   Ccur[:, cc:c + w], start=False, stop=True))
                        return rr
                    emit("tensor", mmS, reads=[kks, kqs, f"Mt{r}", f"Zsb{r}", d["kC"],
                                               "mbigs", "identbf"],
                         writes=pk(d["which"], c0, T), tag=f"S{d['n']}_{i}")
                for r in range(2):
                    d = pr[r]
                    PTt = PT_p[r][i]
                    emit("scalar", lambda e, PTt=PTt, c0=c0, ps=d["ps"]: [
                        e.activation(PTt[:, 0:T - c0], ps[:, c0:T], AF.Exp, scale=ISQD)],
                        reads=pk(d["which"], c0, T), writes=[f"PT{r}:{i}"],
                        tag=f"exp{d['n']}_{i}")

                for r in range(2):
                    d = pr[r]
                    def mmC(e, c0=c0, wN=d["wN"], Zsb=d["Zsb"], ps=d["ps"]):
                        return [e.matmul(ps[:, c:c + w], wN[:, c0:c0 + 128],
                                         Zsb[:, c:c + w], start=True, stop=True)
                                for (c, w) in spans(c0, T)]
                    emit("tensor", mmC, reads=[d["kwN"], f"Zsb{r}"],
                         writes=pk(d["which"], c0, T), tag=f"Cm{d['n']}_{i}")
                for r in range(2):
                    d = pr[r]
                    which, ps = d["which"], d["ps"]
                    Cnxt, Ccur, kC, kCn = d["Cnxt"], d["Ccur"], d["kC"], d["kCn"]
                    if i == NB - 1:
                        emit("vector", lambda e, Cnxt=Cnxt, c0=c0, ps=ps: [
                            e.tensor_copy(Cnxt[:, c0:T], ps[:, c0:T])],
                            reads=pk(which, c0, T), writes=[kCn], tag=f"Cc{d['n']}_{i}")
                    else:
                        emit("vector", lambda e, Cnxt=Cnxt, Ccur=Ccur, c0=c0, ps=ps: [
                            e.tensor_copy(Cnxt[:, c0:c0 + 128], ps[:, c0:c0 + 128]),
                            e.tensor_tensor(Cnxt[:, c0 + 128:T], ps[:, c0 + 128:T],
                                            Ccur[:, c0 + 128:T], ALU.add)],
                            reads=pk(which, c0, T) + (kC,), writes=[kCn],
                            tag=f"Cc{d['n']}_{i}")

            for r in range(2):
                n = 2 * h + r
                ptkeys = [f"PT{r}:{s}" for s in range(NB)]
                def mmo(e, r=r):
                    rr = []
                    for s in range(NB):
                        c0s = s * 128
                        for (c, w) in spans(c0s, T):
                            slast = min(4 * (c // 512) + 3, NB - 1)
                            rr.append(e.matmul(psA[:, c:c + w], vs[:, s * 128:(s + 1) * 128],
                                               PT_p[r][s][:, c - c0s:c - c0s + w],
                                               start=(s == 0), stop=(s == slast)))
                    return rr
                emit("tensor", mmo, reads=[kvs] + ptkeys, writes=pk("A", 0, 2048),
                     tag=f"o{n}")
                def mmcs(e, r=r):
                    rr = []
                    for s in range(NB):
                        c0s = s * 128
                        for (c, w) in spans(c0s, T):
                            slast = min(4 * (c // 512) + 3, NB - 1)
                            rr.append(e.matmul(psB[0:1, c:c + w], onescol[:],
                                               PT_p[r][s][:, c - c0s:c - c0s + w],
                                               start=(s == 0), stop=(s == slast)))
                    return rr
                emit("tensor", mmcs, reads=["ones"] + ptkeys, writes=pk("B", 0, 2048),
                     tag=f"cs{n}")
                emit("scalar", [
                    lambda e: [e.activation(psB[0:1, 0:2048], psB[0:1, 0:2048], AF.Ln)],
                    lambda e: [e.activation(crecipb, psB[0:1, 0:2048], AF.Exp, scale=-1.0)],
                ], reads=pk("B", 0, 2048), writes=["crecip"] + list(pk("B", 0, 2048)),
                    tag=f"orc{n}", seq=True)
                def mmrb2(e):
                    return [e.matmul(psB[:, c:c + 512], onesrow[0:1, :],
                                     crecipb[:, c:c + 512], start=True, stop=True)
                            for c in range(0, T, 512)]
                emit("tensor", mmrb2, reads=["crecip", "ones"], writes=pk("B", 0, 2048),
                     tag=f"orb{n}")
                emit("vector", lambda e: [e.tensor_copy(orbc, psB[:, 0:2048])],
                     reads=pk("B", 0, 2048), writes=["orbc"], tag=f"orbe{n}")
                emit("vector", lambda e: [
                    e.tensor_tensor(obf, psA[:, 0:2048], orbc, ALU.mult)],
                    reads=pk("A", 0, 2048) + ("orbc",), writes=["obf"], tag=f"onrm{n}")
                dma("sdma", o_scr[n * 128:(n + 1) * 128, :], obf, ["obf"], ["o_scr"],
                    tag=f"sto{n}")

        for h in range(n_pairs):
            head_pair(h)
        P.barrier()

        AR.reset()
        xnT2 = AR.alloc(NB * 128 * 8, BF16)
        xnT2_keys = tuple(f"xnT2:{j}" for j in range(8))
        for j in range(8):
            dma("gdma", xnT2[:, j * 2048:(j + 1) * 2048], xnT_scr[j * 128:(j + 1) * 128, :],
                ["xnT_scr"], [f"xnT2:{j}"], tag=f"ldxnT{j}")
        wcw_b = [AR.alloc(2 * HID, BF16) for _ in range(2)]
        bcs = AR.alloc(2 * NEXP, F32)
        wb_b = [AR.alloc(T, BF16) for _ in range(2)]
        gout_b = [AR.alloc(T, BF16) for _ in range(2)]
        wcout_b = [AR.alloc(T, BF16) for _ in range(4)]
        wtsTr = AR.alloc(T, BF16, parts=16)
        wtb = AR.alloc(NB * 16, BF16)
        dma("sdma", bcs, bcolsp[:, :], ["in"], ["bcs"], tag="ldbcols")
        emit("vector", lambda e: [e.tensor_copy(wtb, wts_tm[:, :])],
             reads=["wts_tm"], writes=["wtb"], tag="wtsbf")
        for half in range(2):
            which = "A" if half == 0 else "B"
            psx = psA if half == 0 else psB
            def trw3(e, half=half, psx=psx):
                return [e.matmul(psx[0:16, jj * 128:(jj + 1) * 128],
                                 wtb[:, (half * 8 + jj) * 16:(half * 8 + jj + 1) * 16],
                                 identbf[:], start=True, stop=True) for jj in range(8)]
            emit("tensor", trw3, reads=["wtb", "identbf"], writes=pk(which, 0, 1024),
                 tag=f"trwts{half}")
            emit("vector", lambda e, half=half, psx=psx: [
                e.tensor_copy(wtsTr[:, half * 1024:(half + 1) * 1024], psx[0:16, 0:1024])],
                reads=pk(which, 0, 1024), writes=["wtsTr"], tag=f"evwts{half}")
        for c in range(NEXP):
            p = c % 2
            wcw = wcw_b[p]
            dma("gdma", wcw.rearrange("p (j m) -> p j m", j=8),
                Wcols[c].rearrange("(j p) m -> p j m", p=128),
                ["in"], [f"wcw{p}"], tag=f"ldwc{c}")
            which = "A" if p == 0 else "B"
            ps = psA if p == 0 else psB
            def mmwb(e, c=c, ps=ps):
                return [e.matmul(ps[:, cc:cc + 512], selcs[:, c * 128:(c + 1) * 128],
                                 wtsTr[:, cc:cc + 512], start=True, stop=True)
                        for cc in range(0, T, 512)]
            emit("tensor", mmwb, reads=["wtsTr", "selcs"], writes=pk(which, 0, 2048),
                 tag=f"mmwb{c}")
            emit("vector", lambda e, wb=wb_b[p], ps=ps: [e.tensor_copy(wb, ps[:, 0:2048])],
                 reads=pk(which, 0, 2048), writes=[f"wb{p}"], tag=f"evwb{c}")
            for nt in range(2):
                which2 = "B" if p == 0 else "A"
                ps2 = psB if p == 0 else psA
                def mmcols(e, wcw=wcw, nt=nt, ps2=ps2):
                    r = []
                    for cc in range(0, T, 512):
                        for j in range(8):
                            r.append(e.matmul(
                                ps2[:, cc:cc + 512],
                                wcw[:, j * 256 + nt * 128:j * 256 + (nt + 1) * 128],
                                xnT2[:, j * 2048 + cc:j * 2048 + cc + 512],
                                start=(j == 0), stop=(j == 7)))
                    return r
                emit("tensor", mmcols, reads=(f"wcw{p}",) + xnT2_keys,
                     writes=pk(which2, 0, 2048), tag=f"mmcol{c}_{nt}")
                gout = gout_b[nt]
                emit("scalar", lambda e, gout=gout, c=c, nt=nt, ps2=ps2: [
                    e.activation(gout, ps2[:, 0:2048], AF.Gelu,
                                 bias=bcs[:, c * 2 + nt:c * 2 + nt + 1])],
                    reads=pk(which2, 0, 2048) + ("bcs",), writes=[f"gout{nt}"],
                    tag=f"gel{c}_{nt}")
                wco = wcout_b[2 * p + nt]
                emit("vector", lambda e, gout=gout, wco=wco, wb=wb_b[p]: [
                    e.tensor_tensor(wco, gout, wb, ALU.mult)],
                    reads=[f"gout{nt}", f"wb{p}"], writes=[f"wco{2 * p + nt}"],
                    tag=f"wm{c}_{nt}")
                dma("sdma", wc_scr[(c * 2 + nt) * 128:(c * 2 + nt + 1) * 128, :], wco,
                    [f"wco{2 * p + nt}"], ["wc_scr"], tag=f"stwc{c}_{nt}")
        P.barrier()

        AR.reset()
        pws = AR.alloc(32 * HID, BF16)
        wos = AR.alloc(16 * HID, BF16)
        og_b = [AR.alloc(16 * 128, BF16) for _ in range(2)]
        ao_b = [AR.alloc(HID, BF16) for _ in range(2)]
        pwkeys = tuple(f"pws{j}" for j in range(4))
        dma("gdma", wos.rearrange("p (j m) -> p j m", j=16),
            WoT.rearrange("(j p) m -> p j m", p=128), ["in"], ["wos"], tag="ldWo")
        for j in range(4):
            dma("gdma", pws[:, j * 8192:(j + 1) * 8192].rearrange("p (jj m) -> p jj m", jj=8),
                projw.rearrange("(jj p) m -> p jj m", p=128)[:, j * 8:(j + 1) * 8, :],
                ["in"], [f"pws{j}"], tag=f"ldpw{j}")
        lngbc = AR.alloc(HID, F32); lnbbc = AR.alloc(HID, F32); pbbc = AR.alloc(HID, F32)
        mrow = AR.mark()
        rowf = AR.alloc(HID, F32, parts=1)
        rowh = AR.alloc(HID, BF16, parts=1)
        rowl = AR.alloc(HID, BF16, parts=1)
        rows = AR.alloc(HID, F32, parts=1)
        AR.release(mrow)
        wcg_b = [AR.alloc(32 * 128, BF16) for _ in range(2)]
        xng_b = [AR.alloc(HID, F32) for _ in range(2)]
        xg_b = [AR.alloc(HID, F32) for _ in range(2)]
        yb_b = [AR.alloc(HID, F32) for _ in range(2)]
        yn_b = [AR.alloc(HID, F32) for _ in range(2)]
        ob_b = [AR.alloc(HID, F32) for _ in range(2)]

        def bc_row(srcap, kin, dst, kout, whichps):
            ps = psA if whichps == "A" else psB
            dma("sdma", rowf, srcap, ["in", "rowf"], ["rowf"], tag=f"ld{kout}")
            emit("vector", [
                lambda e: [e.tensor_copy(rowh, rowf)],
                lambda e: [e.tensor_copy(rows, rowh)],
                lambda e: [e.tensor_tensor(rows, rowf, rows, ALU.subtract)],
                lambda e: [e.tensor_copy(rowl, rows)],
            ], reads=["rowf"], writes=["rowhl", "rowf"], tag=f"hl{kout}", seq=True)
            def mm(e):
                r = []
                for cc in range(0, HID, 512):
                    r.append(e.matmul(ps[:, cc:cc + 512], onesrow[0:1, :],
                                      rowh[:, cc:cc + 512], start=True, stop=False))
                    r.append(e.matmul(ps[:, cc:cc + 512], onesrow[0:1, :],
                                      rowl[:, cc:cc + 512], start=False, stop=True))
                return r
            emit("tensor", mm, reads=["rowhl", "ones"], writes=pk(whichps, 0, 1024),
                 tag=f"bc{kout}")
            emit("vector", lambda e, dst=dst: [e.tensor_copy(dst, ps[:, 0:1024])],
                 reads=pk(whichps, 0, 1024), writes=[kout], tag=f"ev{kout}")

        bc_row(projb[0:1, :], "projbr", pbbc, "pbbc", "A")
        bc_row(lng[0:1, :], "lngr", lngbc, "lngbc", "B")
        bc_row(lnb[0:1, :], "lnbr", lnbbc, "lnbbc", "A")
        P.barrier()
        if "bc_scr" in debug_taps:
            dma("sdma", bc_scr[0:128, :], pbbc, ["pbbc"], ["bc_scr"], tag="dbg_bc0")
            dma("sdma", bc_scr[128:256, :], lngbc, ["lngbc"], ["bc_scr"], tag="dbg_bc1")
            dma("sdma", bc_scr[256:384, :], lnbbc, ["lnbbc"], ["bc_scr"], tag="dbg_bc2")

        for i in range(NB):
            p = i % 2
            which = "A" if p == 0 else "B"
            ps = psA if p == 0 else psB
            wcg, xng, xg = wcg_b[p], xng_b[p], xg_b[p]
            yb, yn, ob = yb_b[p], yn_b[p], ob_b[p]
            dma("sdma", wcg,
                wc_scr.rearrange("(kk p) t -> p kk t", p=128)[:, :, i * 128:(i + 1) * 128],
                ["wc_scr"], [f"wcg{p}"], tag=f"ldwcg{i}")
            dma("sdma", xng, xn_scr[i * 128:(i + 1) * 128, :], ["xn_scr"], [f"xng{p}"],
                tag=f"ldxng{i}")
            dma("gdma", xg, x_in[i * 128:(i + 1) * 128, :], ["in"], [f"xg{p}"],
                tag=f"ldxg{i}")
            og = og_b[p]; ao = ao_b[p]
            dma("sdma", og,
                o_scr.rearrange("(nn pp) t -> pp nn t", pp=128)[:, :, i * 128:(i + 1) * 128],
                ["o_scr"], [f"og{p}"], tag=f"ldog{i}")
            def mmy(e, wcg=wcg, ps=ps):
                r = []
                for cc in range(0, HID, 512):
                    for j in range(32):
                        r.append(e.matmul(ps[:, cc:cc + 512], wcg[:, j * 128:(j + 1) * 128],
                                          pws[:, j * 1024 + cc:j * 1024 + cc + 512],
                                          start=(j == 0), stop=(j == 31)))
                return r
            emit("tensor", mmy, reads=(f"wcg{p}",) + pwkeys, writes=pk(which, 0, 1024),
                 tag=f"mmy{i}")
            def mmwo(e, og=og, ps=ps):
                r = []
                for cc in range(0, HID, 512):
                    for j in range(NH):
                        r.append(e.matmul(ps[:, 1024 + cc:1024 + cc + 512],
                                          og[:, j * 128:(j + 1) * 128],
                                          wos[:, j * 1024 + cc:j * 1024 + cc + 512],
                                          start=(j == 0), stop=(j == NH - 1)))
                return r
            emit("tensor", mmwo, reads=[f"og{p}", "wos"], writes=pk(which, 1024, 2048),
                 tag=f"mmwo{i}")
            emit("scalar", lambda e, ao=ao, ps=ps: [
                e.activation(ao, ps[:, 1024:2048], AF.Identity)],
                reads=pk(which, 1024, 2048), writes=[f"ao{p}"], tag=f"evwo{i}")
            emit("vector", [
                lambda e, yb=yb, ps=ps: [e.scalar_tensor_tensor(
                    yb, ps[:, 0:1024], 1.0, pbbc, ALU.mult, ALU.add)],
                lambda e, yb=yb, xng=xng: [e.tensor_tensor(yb, yb, xng, ALU.add)],
            ], reads=pk(which, 0, 1024) + (f"xng{p}", "pbbc"), writes=[f"yb{p}"],
                tag=f"yb{i}", seq=True)
            if "y_scr" in debug_taps:
                dma("sdma", y_scr[i * 128:(i + 1) * 128, :], yb, [f"yb{p}"], ["y_scr"],
                    tag=f"dbg_y{i}")
            R1 = red1[:, p:p + 1]
            R2 = red2[:, p:p + 1]
            kr1, kr2 = f"red1{p}", f"red2{p}"
            emit("vector", [
                lambda e, yb=yb, R1=R1: [e.tensor_reduce(R1, yb, AX.X, ALU.add)],
                lambda e, R1=R1: [e.tensor_scalar(R1, R1, 1.0 / HID, None, ALU.mult)],
                lambda e, yb=yb, yn=yn, R1=R1: [e.tensor_scalar(yn, yb, R1, None, ALU.subtract)],
            ], reads=[f"yb{p}"], writes=[f"yn{p}", kr1], tag=f"ylm{i}", seq=True)
            emit("scalar", lambda e, yn=yn, ob=ob: [e.activation(ob, yn, AF.Square)],
                 reads=[f"yn{p}"], writes=[f"ob{p}"], tag=f"yls{i}")
            emit("vector", [
                lambda e, ob=ob, R2=R2: [e.tensor_reduce(R2, ob, AX.X, ALU.add)],
                lambda e, R2=R2: [e.tensor_scalar(R2, R2, 1.0 / HID, None, ALU.mult)],
            ], reads=[f"ob{p}"], writes=[kr2], tag=f"ylv{i}", seq=True)
            emit("scalar", [
                lambda e, R2=R2: [e.activation(R2, R2, AF.Ln, bias=eps_ln[:])],
                lambda e, R2=R2: [e.activation(R2, R2, AF.Exp, scale=-0.5)],
            ], reads=[kr2, "eps"], writes=[kr2], tag=f"ylr{i}", seq=True)
            emit("vector", [
                lambda e, yn=yn, R2=R2: [e.tensor_scalar(yn, yn, R2, None, ALU.mult)],
                lambda e, yn=yn: [e.tensor_tensor(yn, yn, lngbc, ALU.mult)],
                lambda e, yn=yn, ob=ob: [e.tensor_tensor(ob, yn, lnbbc, ALU.add)],
                lambda e, ob=ob, xg=xg: [e.tensor_tensor(ob, ob, xg, ALU.add)],
                lambda e, ob=ob, ao=ao: [e.tensor_tensor(ob, ob, ao, ALU.add)],
            ], reads=[f"yn{p}", kr2, "lngbc", "lnbbc", f"xg{p}", f"ao{p}"],
                writes=[f"ob{p}", f"yn{p}"], tag=f"yf{i}", seq=True)
            dma("sdma", out[i * 128:(i + 1) * 128, :], ob, [f"ob{p}"], ["out"],
                tag=f"stout{i}")

        def body(ename):
            def run(eng):
                for waits, fn, tag, drain, seq, sk, tv in P.groups[ename]:
                    for dk, val in waits:
                        eng.wait_ge(sems[dk], val)
                    if drain:
                        eng.drain()
                    if seq:
                        instrs = None
                        for ix, sub in enumerate(fn):
                            if ix:
                                eng.drain()
                            instrs = sub(eng)
                    else:
                        instrs = fn(eng)
                    assert instrs, f"empty group {tag} on {ename}"
                    if ename in ("tensor", "vector", "scalar", "pool"):
                        instrs[-1].then_inc(sems[sk], 1)
                    else:
                        instrs[-1].then_inc(sems[sk], 16)
                if ename == "sdma":
                    for qe in ("sdma", "gdma"):
                        for j in range(Prog.NSEM):
                            k2 = f"{qe}{j}"
                            u = P.sem_uses.get(k2, 0)
                            if u:
                                eng.wait_ge(sems[k2], u * 16)
            return run

        block.tensor(body("tensor"))
        block.vector(body("vector"))
        block.scalar(body("scalar"))
        block.gpsimd(body("gdma"))
        block.sync(body("sdma"))

    return nc


def host_inputs(inp):
    f = np.ascontiguousarray
    d = {}
    d["WqT"] = f(inp["L0_attn_Wq"].T)
    d["WkT"] = f(inp["L0_attn_Wk"].T)
    d["WvT"] = f(inp["L0_attn_Wv"].T)
    d["WoT"] = f(inp["L0_attn_Wo"].T)
    d["wA1T"] = f(inp["L0_attn_wA1"].T)
    d["wA2T"] = f(inp["L0_attn_wA2"].T)
    d["wB1T"] = f(inp["L0_attn_wB1"].T)
    d["wB2T"] = f(inp["L0_attn_wB2"].T)
    d["convAp"] = f(inp["L0_attn_convA"].reshape(16, 128, 3).transpose(1, 0, 2).reshape(128, 48))
    d["convBp"] = f(inp["L0_attn_convB"].reshape(16, 128, 3).transpose(1, 0, 2).reshape(128, 48))
    d["btwT"] = f(inp["L0_attn_bt_w"].T)
    d["btb"] = f(inp["L0_attn_bt_b"].reshape(16, 1))
    d["phi"] = f(inp["L0_attn_phi"].reshape(2, 1))
    d["omr"] = f(inp["L0_attn_omega_raw"].reshape(2, 1))
    d["colemb"] = f(inp["L0_ffn_col_emb"])
    d["gw1"] = f(inp["L0_ffn_gate_w1"])
    d["gb1p"] = f(inp["L0_ffn_gate_b1"].reshape(4, 128).T)
    d["gw2"] = f(inp["L0_ffn_gate_w2"])
    d["gb2"] = f(inp["L0_ffn_gate_b2"].reshape(1, 16))
    d["Wcols"] = f(inp["L0_ffn_W_cols"])
    d["bcolsp"] = f(inp["L0_ffn_b_cols"].reshape(16, 2, 128).transpose(2, 0, 1).reshape(128, 32))
    d["projw"] = f(inp["L0_ffn_proj_w"])
    d["projb"] = f(inp["L0_ffn_proj_b"].reshape(1, HID))
    d["lng"] = f(inp["L0_ffn_ln_g"].reshape(1, HID))
    d["lnb"] = f(inp["L0_ffn_ln_b"].reshape(1, HID))
    d["identin"] = np.eye(128, dtype=np.float32)
    d["maskSin"] = np.tril(np.ones((128, 128), np.float32), -1)
    d["maskKin"] = np.triu(np.ones((128, 128), np.float32))
    d["pos4in"] = np.broadcast_to(np.arange(T, dtype=np.float32), (4, T)).copy()
    sel = np.zeros((16, 16 * 128), np.float32)
    for c in range(16):
        sel[c, c * 128:(c + 1) * 128] = 1.0
    d["selcin"] = sel
    sel4 = np.zeros((4, 4 * 128), np.float32)
    for c in range(4):
        sel4[c, c * 128:(c + 1) * 128] = 1.0
    d["sel4in"] = sel4
    d["dup24in"] = np.array([[1, 1, 0, 0], [0, 0, 1, 1]], np.float32)
    mb = np.zeros((128, 128), np.float32)
    mb[np.triu_indices(128, 1)] = -1e6
    d["mbigin"] = mb
    hp = float(np.float32(0.5 * math.pi))
    d["phsh4in"] = np.array([[hp], [0.0], [hp], [0.0]], np.float32)
    return d


def kernel(**inputs):
    shared = host_inputs(inputs)
    x = np.ascontiguousarray(inputs["L0_attn_hidden_states"].astype(np.float32))
    nc = build_nc()
    in_maps = []
    for b in range(8):
        m = dict(shared)
        m["x"] = np.ascontiguousarray(x[b])
        in_maps.append(m)
    res = run_bass_kernel_spmd(nc, in_maps, core_ids=list(range(8)))
    out = np.stack([res.results[b]["out"] for b in range(8)], axis=0)
    return out.astype(np.float32)




# version 13
# speedup vs baseline: 1.0544x; 1.0206x over previous
import math
import numpy as np
import concourse.bass as bass
import concourse.mybir as mybir
from concourse.bass_utils import run_bass_kernel_spmd

F32 = mybir.dt.float32
BF16 = mybir.dt.bfloat16
I32 = mybir.dt.int32
U8 = mybir.dt.uint8
AF = mybir.ActivationFunctionType
ALU = mybir.AluOpType
AX = mybir.AxisListType

T = 2048
HID = 1024
H, R, D = 8, 2, 128
NH = H * R
NB = T // 128
NEXP = 16
OMEGA_SCALE = 2.0 * math.pi / (512 * 16.0)
ISQD = 1.0 / math.sqrt(D)
TWO_PI = float(np.float32(2.0 * math.pi))
INV_2PI = float(np.float32(1.0 / (2.0 * math.pi)))

N_PAIRS = 8
ARENA_B = 177 * 1024


def spans(c0, c1, bank=512):
    out = []
    c = c0
    while c < c1:
        e = min(c1, (c // bank + 1) * bank)
        out.append((c, e - c))
        c = e
    return out


class Prog:
    CENG = ("tensor", "vector", "scalar", "pool")
    DENG = ("sdma", "gdma")
    ENGS = CENG + DENG
    NSEM = 6

    def __init__(self):
        self.groups = {e: [] for e in self.ENGS}
        self.count = {e: 0 for e in self.ENGS}
        self.sem_uses = {}
        self.writer = {}
        self.readers = {}
        self.vclock = {e: {} for e in self.ENGS}
        self.drained = {}
        self.barrier_snap = None

    def _token_for(self, eng):
        if eng in self.CENG:
            return (eng, eng, self.count[eng] + 1)
        j = self.count[eng] % self.NSEM
        sk = f"{eng}{j}"
        return (eng, sk, (self.sem_uses.get(sk, 0) + 1) * 16)

    def emit(self, eng, fn, reads=(), writes=(), tag="", seq=False):
        deps = {}
        own_dep = [0]
        def add(tok):
            de, sk, dv = tok
            if de == eng and de in self.CENG:
                if de == "tensor":
                    return
                own_dep[0] = max(own_dep[0], dv)
                return
            if deps.get(sk, 0) < dv:
                deps[sk] = dv
        if self.barrier_snap:
            for tok in self.barrier_snap:
                add(tok)
        for k in reads:
            w = self.writer.get(k)
            if w:
                add(w)
        for k in writes:
            w = self.writer.get(k)
            if w:
                add(w)
            for rdep in self.readers.get(k, ()):
                add(rdep)
        tok = self._token_for(eng)
        if eng in self.DENG:
            sk = tok[1]
            prev = self.sem_uses.get(sk, 0)
            if prev > 0:
                if deps.get(sk, 0) < prev * 16:
                    deps[sk] = prev * 16
            self.sem_uses[sk] = prev + 1
        vc = self.vclock[eng]
        waits = []
        for sk, dv in deps.items():
            if vc.get(sk, 0) >= dv:
                continue
            vc[sk] = dv
            waits.append((sk, dv))
        drain = False
        if own_dep[0] > self.drained.get(eng, 0):
            drain = True
            self.drained[eng] = self.count[eng]
        self.count[eng] += 1
        self.groups[eng].append((waits, fn, tag, drain, seq, tok[1], tok[2]))
        me = tok
        for k in writes:
            self.writer[k] = me
            self.readers[k] = []
        for k in reads:
            self.readers.setdefault(k, []).append(me)
        return me

    def barrier(self):
        snap = []
        for e in self.CENG:
            if self.count[e]:
                snap.append((e, e, self.count[e]))
        for e in self.DENG:
            for j in range(self.NSEM):
                sk = f"{e}{j}"
                u = self.sem_uses.get(sk, 0)
                if u:
                    snap.append((e, sk, u * 16))
        self.barrier_snap = snap


class Arena:
    def __init__(self, ap_u8, nbytes):
        self.ap = ap_u8
        self.nbytes = nbytes
        self.off = 0

    def mark(self):
        return self.off

    def reset(self):
        self.off = 0

    def release(self, m):
        self.off = m

    def alloc(self, cols, dtype, parts=128):
        esz = 2 if dtype in (BF16,) else 4
        start = (self.off + 31) // 32 * 32
        self.off = start + cols * esz
        assert self.off <= self.nbytes, f"arena overflow {self.off}"
        view = self.ap.bitcast(dtype)
        v = view[:, start // esz:start // esz + cols]
        if parts != 128:
            v = v[0:parts, :]
        return v


def build_nc(debug_taps=(), n_pairs=N_PAIRS):
    nc = bass.Bass("TRN2", target_bir_lowering=False, debug=False)

    def din(name, shape, dtype=F32):
        return nc.dram_tensor(name, list(shape), dtype, kind="ExternalInput").ap()

    def dscr(name, shape, dtype):
        kind = "ExternalOutput" if name in debug_taps else "Internal"
        return nc.dram_tensor(name, list(shape), dtype, kind=kind).ap()

    x_in = din("x", [T, HID])
    WqT = din("WqT", [HID, HID]); WkT = din("WkT", [HID, HID]); WvT = din("WvT", [HID, HID])
    WoT = din("WoT", [2 * HID, HID])
    wA1T = din("wA1T", [HID, 32]); wA2T = din("wA2T", [32, 2 * HID])
    wB1T = din("wB1T", [HID, 32]); wB2T = din("wB2T", [32, 2 * HID])
    convAp = din("convAp", [128, NB * 3]); convBp = din("convBp", [128, NB * 3])
    btwT = din("btwT", [HID, 16]); btb = din("btb", [16, 1])
    phi_in = din("phi", [2, 1]); omr_in = din("omr", [2, 1])
    colemb = din("colemb", [16, HID])
    gw1 = din("gw1", [HID, 512]); gb1p = din("gb1p", [128, 4])
    gw2 = din("gw2", [512, 16]); gb2 = din("gb2", [1, 16])
    Wcols = din("Wcols", [NEXP, HID, 256])
    bcolsp = din("bcolsp", [128, 2 * NEXP])
    projw = din("projw", [4 * HID, HID])
    projb = din("projb", [1, HID])
    lng = din("lng", [1, HID]); lnb = din("lnb", [1, HID])
    identin = din("identin", [128, 128])
    maskSin = din("maskSin", [128, 128])
    maskKin = din("maskKin", [128, 128])
    pos4in = din("pos4in", [4, T])
    selcin = din("selcin", [16, 16 * 128])
    sel4in = din("sel4in", [4, 4 * 128])
    mbigin = din("mbigin", [128, 128])
    dup24in = din("dup24in", [2, 4])
    phsh4in = din("phsh4in", [4, 1])

    out = nc.dram_tensor("out", [T, HID], F32, kind="ExternalOutput").ap()

    xn_scr = dscr("xn_scr", [T, HID], F32)
    xnT_scr = dscr("xnT_scr", [HID, T], BF16)
    qT_scr = dscr("qT_scr", [HID, T], BF16)
    kT_scr = dscr("kT_scr", [HID, T], BF16)
    v_scr = dscr("v_scr", [T, HID], BF16)
    W_scr = dscr("W_scr", [2 * HID, T], BF16)
    o_scr = dscr("o_scr", [NH * 128, T], BF16)
    attn_scr = dscr("attn_scr", [T, HID], F32)
    wc_scr = dscr("wc_scr", [4 * HID, T], BF16)
    beta_scr = dscr("beta_scr", [128, NB * 16], F32)
    wts_scr = dscr("wts_scr", [128, NB * 16], F32)
    y_scr = dscr("y_scr", [T, HID], F32)
    bc_scr = dscr("bc_scr", [3 * 128, HID], F32)

    P = Prog()

    from contextlib import ExitStack
    with ExitStack() as ST:
        def sb(name, shape, dtype):
            return ST.enter_context(nc.sbuf_tensor(name, shape, dtype))
        arena_t = sb("arena", [128, ARENA_B], U8)
        identbf = sb("identbf", [128, 128], BF16)
        maskSs = sb("maskSs", [128, 128], F32)
        maskKs = sb("maskKs", [128, 128], F32)
        onescol = sb("onescol", [128, 1], BF16)
        onesrow = sb("onesrow", [1, 128], BF16)
        selcs = sb("selcs", [16, 16 * 128], BF16)
        sel4s = sb("sel4s", [4, 4 * 128], BF16)
        mbigs = sb("mbigs", [128, 128], BF16)
        dup24s = sb("dup24s", [2, 4], F32)
        phsh4s = sb("phsh4s", [4, 1], F32)
        sm2 = sb("sm2", [2, 8], F32)
        sm4 = sb("sm4", [4, 8], F32)
        btbs = sb("btbs", [16, 1], F32)
        beta_tm = sb("beta_tm", [128, NB * 16], F32)
        betaneg_tm = sb("betaneg_tm", [128, NB * 16], F32)
        rnorm_tm = sb("rnorm_tm", [128, NB], F32)
        svar = sb("svar", [128, NB], F32)
        srstd = sb("srstd", [128, NB], F32)
        wts_tm = sb("wts_tm", [128, NB * 16], F32)
        eps_ln = sb("eps_ln", [128, 1], F32)
        eps24 = sb("eps24", [128, 1], F32)
        gb2bc = sb("gb2bc", [128, 16], F32)
        red1 = sb("red1", [128, 2], F32)
        red2 = sb("red2", [128, 2], F32)
        red3 = sb("red3", [128, 2], F32)
        t16a = sb("t16a", [128, 16], F32)
        t16b = sb("t16b", [128, 16], F32)
        t16c = sb("t16c", [128, 16], F32)
        psA = ST.enter_context(nc.psum_tensor("psA", [128, 2048], F32))
        psB = ST.enter_context(nc.psum_tensor("psB", [128, 2048], F32))
        smt = ST.enter_context(nc.semaphore("smt"))
        smv = ST.enter_context(nc.semaphore("smv"))
        sms = ST.enter_context(nc.semaphore("sms"))
        smp = ST.enter_context(nc.semaphore("smp"))
        dma_sems = {}
        for qe in ("sdma", "gdma"):
            for j in range(Prog.NSEM):
                dma_sems[f"{qe}{j}"] = ST.enter_context(nc.semaphore(f"{qe}{j}"))
        block = ST.enter_context(nc.Block())

        AR = Arena(arena_t[:, :], ARENA_B)
        sems = {"tensor": smt, "vector": smv, "scalar": sms, "pool": smp}
        sems.update(dma_sems)

        def pk(which, c0, c1):
            return tuple(f"ps{which}:{b}" for b in range(c0 // 512, (c1 + 511) // 512))

        emit = P.emit

        def dma(eng, dst, src, reads, writes, tag=""):
            emit(eng, lambda e, d=dst, s=src: [e.dma_start(out=d, in_=s)],
                 reads=reads, writes=writes, tag=tag)

        dma("gdma", identbf[:], identin[:, :], ["in"], ["identbf"])
        dma("sdma", maskSs[:], maskSin[:, :], ["in"], ["maskSs"])
        dma("sdma", maskKs[:], maskKin[:, :], ["in"], ["maskKs"])
        dma("gdma", selcs[:], selcin[:, :], ["in"], ["selcs"])
        dma("gdma", sel4s[:], sel4in[:, :], ["in"], ["sel4s"])
        dma("gdma", mbigs[:], mbigin[:, :], ["in"], ["mbigs"])
        dma("sdma", dup24s[:], dup24in[:, :], ["in"], ["dup24s"])
        dma("sdma", phsh4s[:], phsh4in[:, :], ["in"], ["phsh4s"])
        dma("sdma", btbs[:], btb[:, :], ["in"], ["btbs"])
        dma("sdma", sm2[0:2, 0:1], phi_in[:, :], ["in"], ["phi"])
        dma("sdma", sm2[0:2, 1:2], omr_in[:, :], ["in"], ["omr"])
        emit("vector", lambda e: [
            e.memset(onescol[:], 1.0), e.memset(onesrow[:], 1.0),
            e.memset(eps_ln[:], 1e-5), e.memset(eps24[:], 1e-24),
        ], writes=["ones", "eps"], tag="consts")

        xnT_sb = AR.alloc(NB * 128 * 8, BF16)
        xnT_keys = tuple(f"xnT:{i}" for i in range(NB))
        xnT_view = xnT_sb.rearrange("p (j t) -> p j t", j=8)

        m0 = AR.mark()
        xt_b = [AR.alloc(HID, F32) for _ in range(2)]
        xc_b = [AR.alloc(HID, F32) for _ in range(2)]
        xnf_b = [AR.alloc(HID, F32) for _ in range(2)]
        xnb_b = [AR.alloc(HID, BF16) for _ in range(2)]
        for i in range(NB):
            p = i % 2
            xt, xc, xnf, xnb = xt_b[p], xc_b[p], xnf_b[p], xnb_b[p]
            kx, kc, kn, knb = f"xt{p}", f"xc{p}", f"xnf{p}", f"xnb{p}"
            dma("sdma", xt, x_in[i * 128:(i + 1) * 128, :], ["in"], [kx], tag=f"ldx{i}")
            R1 = red1[:, p:p + 1]
            R2 = red2[:, p:p + 1]
            R3 = red3[:, p:p + 1]
            kr1, kr2, kr3 = f"red1{p}", f"red2{p}", f"red3{p}"
            emit("vector", [
                lambda e, xt=xt, R1=R1: [e.tensor_reduce(R1, xt, AX.X, ALU.add)],
                lambda e, R1=R1: [e.tensor_scalar(R1, R1, 1.0 / HID, None, ALU.mult)],
                lambda e, xt=xt, xc=xc, R1=R1: [e.tensor_scalar(xc, xt, R1, None, ALU.subtract)],
            ], reads=[kx], writes=[kc, kr1], tag=f"lnm{i}", seq=True)
            emit("scalar", lambda e, xc=xc, xnf=xnf: [e.activation(xnf, xc, AF.Square)],
                 reads=[kc], writes=[kn], tag=f"lnsq{i}")
            emit("vector", [
                lambda e, xnf=xnf, i=i: [e.tensor_reduce(svar[:, i:i + 1], xnf, AX.X, ALU.add)],
                lambda e, i=i: [e.tensor_scalar(svar[:, i:i + 1], svar[:, i:i + 1],
                                                1.0 / HID, None, ALU.mult)],
            ], reads=[kn], writes=[kr2, "svar"], tag=f"lnv{i}", seq=True)
            emit("scalar", [
                lambda e, i=i, R3=R3: [e.activation(R3, svar[:, i:i + 1], AF.Ln,
                                                    bias=eps_ln[:])],
                lambda e, i=i, R3=R3: [e.activation(srstd[:, i:i + 1], R3, AF.Exp,
                                                    scale=-0.5)],
            ], reads=[kr2, "eps", "svar"], writes=[kr3, "srstd"], tag=f"lnr{i}", seq=True)
            emit("vector", lambda e, xc=xc, xnf=xnf, i=i: [
                e.tensor_scalar(xnf, xc, srstd[:, i:i + 1], None, ALU.mult)],
                reads=[kc, "srstd"], writes=[kn], tag=f"lnx{i}")
            emit("scalar", lambda e, xnf=xnf, xnb=xnb: [e.activation(xnb, xnf, AF.Identity)],
                 reads=[kn], writes=[knb], tag=f"lnxb{i}")
            dma("sdma", xn_scr[i * 128:(i + 1) * 128, :], xnf, [kn], ["xn_scr"], tag=f"stxn{i}")
            which = "A" if p == 0 else "B"
            psx = psA if p == 0 else psB
            def trx(e, xnb=xnb, psx=psx):
                return [e.matmul(psx[:, j * 128:(j + 1) * 128],
                                 xnb[:, j * 128:(j + 1) * 128], identbf[:],
                                 start=True, stop=True)
                        for j in range(8)]
            emit("tensor", trx, reads=[knb, "identbf"], writes=pk(which, 0, 1024),
                 tag=f"trx{i}")
            emit("vector", lambda e, psx=psx, i=i: [
                e.tensor_copy(xnT_view[:, :, i * 128:(i + 1) * 128],
                              psx[:, 0:1024].rearrange("p (j c) -> p j c", j=8))],
                 reads=pk(which, 0, 1024), writes=[f"xnT:{i}"], tag=f"evx{i}")
        for j in range(8):
            dma("sdma", xnT_scr[j * 128:(j + 1) * 128, :],
                xnT_sb[:, j * 2048:(j + 1) * 2048], list(xnT_keys), ["xnT_scr"],
                tag=f"stxnT{j}")

        emit("vector", [
            lambda e: [e.tensor_tensor(rnorm_tm[:, :], srstd[:, :], srstd[:, :], ALU.mult)],
            lambda e: [e.tensor_tensor(rnorm_tm[:, :], rnorm_tm[:, :], svar[:, :], ALU.mult)],
            lambda e: [e.tensor_scalar(rnorm_tm[:, :], rnorm_tm[:, :], float(HID), None, ALU.mult)],
            lambda e: [e.tensor_scalar(rnorm_tm[:, :], rnorm_tm[:, :], eps24[:], None, ALU.max)],
        ], reads=["svar", "srstd", "eps"], writes=["rnorm"], tag="rnb", seq=True)
        emit("scalar", [
            lambda e: [e.activation(rnorm_tm[:, :], rnorm_tm[:, :], AF.Ln)],
            lambda e: [e.activation(rnorm_tm[:, :], rnorm_tm[:, :], AF.Exp, scale=-0.5)],
        ], reads=["rnorm"], writes=["rnorm"], tag="rnb2", seq=True)

        btwTs = AR.alloc(8 * 16, BF16)
        betaT_f = AR.alloc(T, F32, parts=16)
        betaT_bf = AR.alloc(T, BF16, parts=16)
        dma("gdma", btwTs.rearrange("p (j m) -> p j m", j=8), btwT.rearrange("(j p) m -> p j m", p=128), ["in"], ["btwTs"])
        def mmbeta(e):
            r = []
            for c in range(0, T, 512):
                for j in range(8):
                    r.append(e.matmul(psA[0:16, c:c + 512], btwTs[:, j * 16:(j + 1) * 16],
                                      xnT_sb[:, j * 2048 + c:j * 2048 + c + 512],
                                      start=(j == 0), stop=(j == 7)))
            return r
        emit("tensor", mmbeta, reads=("btwTs",) + xnT_keys, writes=pk("A", 0, 2048),
             tag="mmbeta")
        emit("scalar", lambda e: [
            e.activation(betaT_f, psA[0:16, 0:2048], AF.Sigmoid, bias=btbs[:])],
            reads=pk("A", 0, 2048) + ("btbs",), writes=["betaTf"], tag="betasig")
        emit("vector", lambda e: [
            e.tensor_scalar(betaT_bf, betaT_f, 2.0, None, ALU.mult)],
            reads=["betaTf"], writes=["betaTbf"], tag="beta2")
        for half in range(2):
            which = "A" if half == 0 else "B"
            psx = psA if half == 0 else psB
            def trb2(e, half=half, psx=psx):
                r = []
                for jj in range(8):
                    i = half * 8 + jj
                    r.append(e.matmul(psx[:, jj * 16:(jj + 1) * 16],
                                      betaT_bf[:, i * 128:(i + 1) * 128],
                                      identbf[0:16, 0:16], start=True, stop=True))
                return r
            emit("tensor", trb2, reads=["betaTbf", "identbf"], writes=pk(which, 0, 128),
                 tag=f"trbeta{half}")
            def evb(e, half=half, psx=psx):
                r = []
                for jj in range(8):
                    i = half * 8 + jj
                    r.append(e.tensor_copy(beta_tm[:, i * 16:(i + 1) * 16],
                                           psx[:, jj * 16:(jj + 1) * 16]))
                return r
            emit("vector", evb, reads=pk(which, 0, 128), writes=["beta_tm"],
                 tag=f"evbeta{half}")
        emit("vector", lambda e: [
            e.tensor_scalar(betaneg_tm[:, :], beta_tm[:, :], -1.0, None, ALU.mult)],
            reads=["beta_tm"], writes=["betaneg_tm"], tag="betaneg")
        dma("sdma", beta_scr[:, :], beta_tm[:, :], ["beta_tm"], ["beta_scr"], tag="stbeta")
        AR.release(m0)
        P.barrier()

        cbc = [AR.alloc(T, BF16) for _ in range(2)]
        sbc = [AR.alloc(T, BF16) for _ in range(2)]
        m1 = AR.mark()
        pos4 = AR.alloc(T, F32, parts=4)
        th4 = AR.alloc(T, F32, parts=4)
        sc4 = AR.alloc(T, F32, parts=4)
        si4 = AR.alloc(T, I32, parts=4)
        csrow = AR.alloc(T, BF16, parts=4)
        dma("sdma", pos4, pos4in[:, :], ["in"], ["pos4"], tag="ldpos")
        emit("scalar", lambda e: [
            e.activation(sm2[0:2, 2:3], sm2[0:2, 1:2], AF.Abs)],
            reads=["omr"], writes=["sp1"], tag="spa")
        emit("scalar", lambda e: [
            e.activation(sm2[0:2, 3:4], sm2[0:2, 2:3], AF.Exp, scale=-1.0)],
            reads=["sp1"], writes=["sp2"], tag="spe")
        emit("vector", lambda e: [
            e.tensor_scalar(sm2[0:2, 4:5], sm2[0:2, 3:4], 1.0, None, ALU.add),
            e.tensor_scalar(sm2[0:2, 5:6], sm2[0:2, 1:2], 0.0, None, ALU.max),
        ], reads=["sp2", "omr"], writes=["sp3"], tag="spv")
        emit("scalar", lambda e: [
            e.activation(sm2[0:2, 6:7], sm2[0:2, 4:5], AF.Ln)],
            reads=["sp3"], writes=["sp4"], tag="spl")
        emit("vector", [
            lambda e: [e.tensor_tensor(sm2[0:2, 7:8], sm2[0:2, 5:6], sm2[0:2, 6:7], ALU.add)],
            lambda e: [e.tensor_scalar(sm2[0:2, 7:8], sm2[0:2, 7:8], OMEGA_SCALE, None, ALU.mult)],
        ], reads=["sp4", "sp3"], writes=["omega2"], tag="spf", seq=True)
        emit("tensor", lambda e: [
            e.matmul(psA[0:4, 0:1], dup24s[:, :], sm2[0:2, 0:1], start=True, stop=True),
            e.matmul(psA[0:4, 1:2], dup24s[:, :], sm2[0:2, 7:8], start=True, stop=True),
        ], reads=["phi", "omega2", "dup24s"], writes=pk("A", 0, 2), tag="dupmm")
        emit("vector", lambda e: [
            e.tensor_tensor(sm4[0:4, 0:1], psA[0:4, 0:1], phsh4s[:, :], ALU.add),
            e.tensor_copy(sm4[0:4, 1:2], psA[0:4, 1:2]),
        ], reads=pk("A", 0, 2) + ("phsh4s",), writes=["pom4"], tag="dupev")
        emit("vector", lambda e: [
            e.tensor_scalar(th4, pos4, sm4[0:4, 1:2], sm4[0:4, 0:1], ALU.mult, ALU.add),
        ], reads=["pos4", "pom4"], writes=["th4"], tag="th4")
        emit("vector", [
            lambda e: [e.tensor_scalar(sc4, th4, INV_2PI, None, ALU.mult)],
            lambda e: [e.tensor_copy(si4, sc4)],
            lambda e: [e.tensor_copy(th4, si4)],
            lambda e: [e.tensor_tensor(sc4, sc4, th4, ALU.subtract)],
        ], reads=["th4"], writes=["sc4", "th4"], tag="rr4", seq=True)
        emit("scalar", lambda e: [
            e.activation(th4, sc4, AF.Sin, scale=TWO_PI)],
            reads=["sc4"], writes=["sin4"], tag="sin4")
        emit("vector", lambda e: [e.tensor_copy(csrow, th4)],
             reads=["sin4"], writes=["csrow"], tag="csbf")
        for r in range(2):
            for ci in range(2):
                idx = 2 * r + ci
                which = "A" if idx % 2 == 0 else "B"
                ps = psA if idx % 2 == 0 else psB
                def mmbc(e, idx=idx, ps=ps):
                    return [e.matmul(ps[:, c:c + 512], sel4s[:, idx * 128:(idx + 1) * 128],
                                     csrow[:, c:c + 512], start=True, stop=True)
                            for c in range(0, T, 512)]
                emit("tensor", mmbc, reads=["csrow", "sel4s"], writes=pk(which, 0, 2048),
                     tag=f"bccs{idx}")
                dst = cbc[r] if ci == 0 else sbc[r]
                kcs = f"cs{r}{ci}"
                emit("vector", lambda e, dst=dst, ps=ps: [e.tensor_copy(dst, ps[:, 0:2048])],
                     reads=pk(which, 0, 2048), writes=[kcs], tag=f"evcs{idx}")
        AR.release(m1)
        P.barrier()

        m1 = AR.mark()
        wA1Ts = AR.alloc(8 * 32, BF16); wB1Ts = AR.alloc(8 * 32, BF16)
        wA2Ts = AR.alloc(2 * HID, BF16, parts=32); wB2Ts = AR.alloc(2 * HID, BF16, parts=32)
        convAs = AR.alloc(NB * 3, F32); convBs = AR.alloc(NB * 3, F32)
        dma("gdma", wA1Ts.rearrange("p (j m) -> p j m", j=8), wA1T.rearrange("(j p) m -> p j m", p=128), ["in"], ["wA1Ts"])
        dma("gdma", wB1Ts.rearrange("p (j m) -> p j m", j=8), wB1T.rearrange("(j p) m -> p j m", p=128), ["in"], ["wB1Ts"])
        dma("gdma", wA2Ts, wA2T[:, :], ["in"], ["wA2Ts"])
        dma("gdma", wB2Ts, wB2T[:, :], ["in"], ["wB2Ts"])
        dma("sdma", convAs, convAp[:, :], ["in"], ["convAs"])
        dma("sdma", convBs, convBp[:, :], ["in"], ["convBs"])
        uTA = AR.alloc(T, BF16, parts=32)
        uTB = AR.alloc(T, BF16, parts=32)
        wn_b = [AR.alloc(T, BF16) for _ in range(2)]
        sq_b = [AR.alloc(T, BF16) for _ in range(2)]
        cv_b = [AR.alloc(T, F32) for _ in range(2)]
        cv2_b = [AR.alloc(T, F32) for _ in range(2)]
        xa_b = [AR.alloc(T, BF16) for _ in range(2)]
        xb_b = [AR.alloc(T, BF16) for _ in range(2)]
        wfin_b = [AR.alloc(T, BF16) for _ in range(2)]
        rinvS = AR.alloc(T, F32, parts=1)
        rrowS = AR.alloc(T, BF16, parts=1)
        rbcA = AR.alloc(T, F32); rbcB = AR.alloc(T, F32)

        def mmu(e, w1s, uT):
            r = []
            for c in range(0, T, 512):
                for j in range(8):
                    r.append(e.matmul(psA[0:32, c:c + 512], w1s[:, j * 32:(j + 1) * 32],
                                      xnT_sb[:, j * 2048 + c:j * 2048 + c + 512],
                                      start=(j == 0), stop=(j == 7)))
            return r
        for (pn, w1s, w2s, k1, k2, uT) in (("A", wA1Ts, wA2Ts, "wA1Ts", "wA2Ts", uTA),
                                           ("B", wB1Ts, wB2Ts, "wB1Ts", "wB2Ts", uTB)):
            emit("tensor", lambda e, w1s=w1s, uT=uT: mmu(e, w1s, uT),
                 reads=(k1,) + xnT_keys, writes=pk("A", 0, 2048), tag=f"mmu{pn}")
            emit("vector", lambda e, uT=uT: [e.tensor_copy(uT, psA[0:32, 0:2048])],
                 reads=pk("A", 0, 2048), writes=[f"uT{pn}"], tag=f"evu{pn}")
            for j in range(NB):
                p = j % 2
                def mmw(e, j=j, w2s=w2s, uT=uT):
                    return [e.matmul(psA[:, c:c + 512], w2s[:, j * 128:(j + 1) * 128],
                                     uT[:, c:c + 512], start=True, stop=True)
                            for c in range(0, T, 512)]
                emit("tensor", mmw, reads=[k2, f"uT{pn}"], writes=pk("A", 0, 2048),
                     tag=f"p1m{pn}{j}")
                emit("scalar", lambda e, sq=sq_b[p]: [
                    e.activation(sq, psA[:, 0:2048], AF.Square)],
                    reads=pk("A", 0, 2048), writes=[f"sq{p}"], tag=f"p1s{pn}{j}")
                def mmss(e, sq=sq_b[p], j=j):
                    return [e.matmul(psB[0:1, c:c + 512], onescol[:], sq[:, c:c + 512],
                                     start=(j == 0), stop=(j == NB - 1))
                            for c in range(0, T, 512)]
                emit("tensor", mmss, reads=[f"sq{p}", "ones"], writes=pk("B", 0, 2048),
                     tag=f"p1ss{pn}{j}")
            rinv = rinvS
            rrow = rrowS
            rbc = rbcA if pn == "A" else rbcB
            emit("scalar", [
                lambda e, rinv=rinv: [e.activation(rinv, psB[0:1, 0:2048], AF.Ln)],
                lambda e, rinv=rinv: [e.activation(rinv, rinv, AF.Exp, scale=-0.5)],
            ], reads=pk("B", 0, 2048), writes=[f"rinv{pn}"], tag=f"ri{pn}", seq=True)
            emit("vector", lambda e, rinv=rinv, rrow=rrow: [e.tensor_copy(rrow, rinv)],
                 reads=[f"rinv{pn}"], writes=[f"rrow{pn}"], tag=f"rr{pn}")
            def mmrb(e, rrow=rrow):
                return [e.matmul(psB[:, c:c + 512], onesrow[0:1, :], rrow[:, c:c + 512],
                                 start=True, stop=True) for c in range(0, T, 512)]
            emit("tensor", mmrb, reads=[f"rrow{pn}", "ones"], writes=pk("B", 0, 2048),
                 tag=f"rbm{pn}")
            emit("vector", lambda e, rbc=rbc: [e.tensor_copy(rbc, psB[:, 0:2048])],
                 reads=pk("B", 0, 2048), writes=[f"rbc{pn}"], tag=f"rbe{pn}")
            emit("vector", lambda e, uT=uT, rbc=rbc: [
                e.tensor_tensor(uT, uT, rbc[0:32, :], ALU.mult)],
                reads=[f"uT{pn}", f"rbc{pn}"], writes=[f"uT{pn}"], tag=f"usc{pn}")

        for j in range(NB):
            p = j % 2
            r_ = j % 2
            for (pn, w2s, k2, uT, convs, kconv, xres, kxr, rbc, krbc) in (
                    ("A", wA2Ts, "wA2Ts", uTA, convAs, "convAs", xa_b[p], f"xa{p}", rbcA, "rbcA"),
                    ("B", wB2Ts, "wB2Ts", uTB, convBs, "convBs", xb_b[p], f"xb{p}", rbcB, "rbcB")):
                wn, cv, cv2 = wn_b[p], cv_b[p], cv2_b[p]
                def mmw2(e, j=j, w2s=w2s, uT=uT):
                    return [e.matmul(psA[:, c:c + 512], w2s[:, j * 128:(j + 1) * 128],
                                     uT[:, c:c + 512], start=True, stop=True)
                            for c in range(0, T, 512)]
                emit("tensor", mmw2, reads=[k2, f"uT{pn}"], writes=pk("A", 0, 2048),
                     tag=f"p2m{pn}{j}")
                emit("scalar", lambda e, wn=wn: [
                    e.activation(wn, psA[:, 0:2048], AF.Identity)],
                    reads=pk("A", 0, 2048), writes=[f"wn{p}"], tag=f"p2n{pn}{j}")
                emit("scalar", lambda e, wn=wn, cv=cv, j=j, convs=convs: [
                    e.activation(cv, wn, AF.Identity, scale=convs[:, j * 3 + 2:j * 3 + 3])],
                    reads=[f"wn{p}", kconv], writes=[f"cv{p}"], tag=f"cva{pn}{j}")
                emit("vector", lambda e, wn=wn, cv=cv, cv2=cv2, j=j, convs=convs: [
                    e.scalar_tensor_tensor(cv2[:, 1:T], wn[:, 0:T - 1],
                                           convs[:, j * 3 + 1:j * 3 + 2], cv[:, 1:T],
                                           ALU.mult, ALU.add),
                    e.tensor_copy(cv2[:, 0:1], cv[:, 0:1]),
                ], reads=[f"wn{p}", f"cv{p}", kconv], writes=[f"cv2{p}"], tag=f"cvb{pn}{j}")
                emit("vector", lambda e, wn=wn, cv=cv, cv2=cv2, j=j, convs=convs: [
                    e.scalar_tensor_tensor(cv[:, 2:T], wn[:, 0:T - 2],
                                           convs[:, j * 3 + 0:j * 3 + 1], cv2[:, 2:T],
                                           ALU.mult, ALU.add),
                    e.tensor_copy(cv[:, 0:2], cv2[:, 0:2]),
                ], reads=[f"wn{p}", f"cv2{p}", kconv], writes=[f"cv{p}"], tag=f"cvc{pn}{j}")
                emit("scalar", lambda e, cv=cv, xres=xres: [
                    e.activation(xres, cv, AF.Silu)],
                    reads=[f"cv{p}"], writes=[kxr], tag=f"sil{pn}{j}")
            wf = wfin_b[p]
            emit("vector", lambda e, p=p, r_=r_, wf=wf: [
                e.tensor_tensor(cv_b[p], xa_b[p], cbc[r_], ALU.mult),
                e.tensor_tensor(cv2_b[p], xb_b[p], sbc[r_], ALU.mult),
            ], reads=[f"xa{p}", f"xb{p}", f"cs{r_}0", f"cs{r_}1"],
                writes=[f"cv{p}", f"cv2{p}"], tag=f"wc1{j}")
            emit("vector", lambda e, p=p, wf=wf: [
                e.tensor_tensor(wf, cv_b[p], cv2_b[p], ALU.add)],
                reads=[f"cv{p}", f"cv2{p}"], writes=[f"wf{p}"], tag=f"wc2{j}")
            dma("sdma", W_scr[j * 128:(j + 1) * 128, :], wf, [f"wf{p}"], ["W_scr"],
                tag=f"stW{j}")
        AR.release(m1)
        P.barrier()

        m1 = AR.mark()
        wbig = AR.alloc(8 * HID, BF16)
        stg_b = [AR.alloc(T, BF16) for _ in range(2)]
        for (nameW, Wap, scr, tmaj) in (("q", WqT, qT_scr, False),
                                        ("k", WkT, kT_scr, False),
                                        ("v", WvT, v_scr, True)):
            dma("gdma", wbig.rearrange("p (j m) -> p j m", j=8),
                Wap.rearrange("(j p) m -> p j m", p=128),
                ["in", "wbig"], ["wbig"], tag=f"ldW{nameW}")
            if not tmaj:
                for mch in range(8):
                    p = mch % 2
                    which = "A" if p == 0 else "B"
                    ps = psA if p == 0 else psB
                    def mmq(e, mch=mch, ps=ps):
                        r = []
                        for c in range(0, T, 512):
                            for j in range(8):
                                r.append(e.matmul(
                                    ps[:, c:c + 512],
                                    wbig[:, j * 1024 + mch * 128:j * 1024 + (mch + 1) * 128],
                                    xnT_sb[:, j * 2048 + c:j * 2048 + c + 512],
                                    start=(j == 0), stop=(j == 7)))
                        return r
                    emit("tensor", mmq, reads=("wbig",) + xnT_keys,
                         writes=pk(which, 0, 2048), tag=f"mm{nameW}{mch}")
                    emit("vector", lambda e, ps=ps, st=stg_b[p]: [
                        e.tensor_copy(st, ps[:, 0:2048])],
                        reads=pk(which, 0, 2048), writes=[f"stg{p}"], tag=f"ev{nameW}{mch}")
                    dma("sdma", scr[mch * 128:(mch + 1) * 128, :], stg_b[p],
                        [f"stg{p}"], [f"{nameW}_scr"], tag=f"st{nameW}{mch}")
            else:
                for i in range(NB):
                    p = i % 2
                    which = "A" if p == 0 else "B"
                    ps = psA if p == 0 else psB
                    def mmv(e, i=i, ps=ps):
                        r = []
                        for c in range(0, HID, 512):
                            for j in range(8):
                                r.append(e.matmul(
                                    ps[:, c:c + 512],
                                    xnT_sb[:, j * 2048 + i * 128:j * 2048 + (i + 1) * 128],
                                    wbig[:, j * 1024 + c:j * 1024 + c + 512],
                                    start=(j == 0), stop=(j == 7)))
                        return r
                    emit("tensor", mmv, reads=("wbig",) + xnT_keys,
                         writes=pk(which, 0, 1024), tag=f"mmv{i}")
                    emit("vector", lambda e, ps=ps, st=stg_b[p]: [
                        e.tensor_copy(st[:, 0:1024], ps[:, 0:1024])],
                        reads=pk(which, 0, 1024), writes=[f"stg{p}"], tag=f"evv{i}")
                    dma("sdma", scr[i * 128:(i + 1) * 128, :], stg_b[p][:, 0:1024],
                        [f"stg{p}"], ["v_scr"], tag=f"stv{i}")
        AR.release(m1)
        P.barrier()

        m1 = AR.mark()
        gw1s = AR.alloc(4 * HID, BF16)
        gb1s = AR.alloc(4, F32)
        hT_t = [AR.alloc(T, BF16) for _ in range(4)]
        gw2s = AR.alloc(4 * 16, BF16)
        ceu = AR.alloc(HID, F32, parts=16)
        ceub = AR.alloc(HID, BF16, parts=16)
        cesq = AR.alloc(HID, F32, parts=16)
        ceuT = AR.alloc(8 * 16, BF16)
        dma("gdma", gw1s.rearrange("p (j m) -> p j m", j=8), gw1.rearrange("(j p) m -> p j m", p=128), ["in"], ["gw1s"])
        dma("sdma", gb1s, gb1p[:, :], ["in"], ["gb1s"])
        dma("gdma", gw2s.rearrange("p (j m) -> p j m", j=4), gw2.rearrange("(j p) m -> p j m", p=128), ["in"], ["gw2s"])
        dma("sdma", ceu, colemb[:, :], ["in"], ["ceu"], tag="ldce")
        dma("sdma", gb2bc[0:1, :], gb2[0:1, :], ["in"], ["gb2r"], tag="ldgb2")
        emit("vector", lambda e: [e.tensor_copy(t16a[0:1, 0:16].bitcast(BF16)[0:1, 0:16]
                                                if False else t16b[0:1, 0:16], gb2bc[0:1, 0:16])],
             reads=["gb2r"], writes=["gb2row"], tag="gb2cp") if False else None
        gb2rowb = AR.alloc(16, BF16, parts=1)
        emit("vector", lambda e: [e.tensor_copy(gb2rowb, gb2bc[0:1, 0:16])],
             reads=["gb2r"], writes=["gb2rowb"], tag="gb2cp")
        emit("tensor", lambda e: [e.matmul(psA[:, 0:16], onesrow[0:1, :], gb2rowb,
                                           start=True, stop=True)],
             reads=["gb2rowb", "ones"], writes=pk("A", 0, 16), tag="gb2mm")
        emit("vector", lambda e: [e.tensor_copy(gb2bc[:, :], psA[:, 0:16])],
             reads=pk("A", 0, 16), writes=["gb2bc"], tag="gb2ev")
        emit("scalar", lambda e: [e.activation(cesq, ceu, AF.Square)],
             reads=["ceu"], writes=["cesq"], tag="cesq")
        emit("vector", [
            lambda e: [e.tensor_reduce(t16a[0:16, 0:1], cesq, AX.X, ALU.add)],
            lambda e: [e.tensor_scalar(t16a[0:16, 0:1], t16a[0:16, 0:1], eps24[0:16, :], None, ALU.max)],
        ], reads=["cesq", "eps"], writes=["cen2"], tag="cered", seq=True)
        emit("scalar", [
            lambda e: [e.activation(t16a[0:16, 0:1], t16a[0:16, 0:1], AF.Ln)],
            lambda e: [e.activation(t16a[0:16, 0:1], t16a[0:16, 0:1], AF.Exp, scale=-0.5)],
        ], reads=["cen2"], writes=["cen2"], tag="ceri", seq=True)
        emit("vector", lambda e: [
            e.tensor_scalar(ceub, ceu, t16a[0:16, 0:1], None, ALU.mult)],
            reads=["ceu", "cen2"], writes=["ceub"], tag="ceun")
        def cetr(e):
            return [e.matmul(psB[:, j * 16:(j + 1) * 16],
                             ceub[:, j * 128:(j + 1) * 128], identbf[0:16, 0:16],
                             start=True, stop=True)
                    for j in range(8)]
        emit("tensor", cetr, reads=["ceub", "identbf"], writes=pk("B", 0, 128), tag="cetr")
        emit("vector", lambda e: [e.tensor_copy(ceuT, psB[:, 0:128])],
             reads=pk("B", 0, 128), writes=["ceuT"], tag="ceuTev")
        for mch in range(4):
            which = "A" if mch % 2 == 0 else "B"
            ps = psA if mch % 2 == 0 else psB
            def mmh(e, mch=mch, ps=ps):
                r = []
                for c in range(0, T, 512):
                    for j in range(8):
                        r.append(e.matmul(ps[:, c:c + 512],
                                          gw1s[:, j * 512 + mch * 128:j * 512 + (mch + 1) * 128],
                                          xnT_sb[:, j * 2048 + c:j * 2048 + c + 512],
                                          start=(j == 0), stop=(j == 7)))
                return r
            emit("tensor", mmh, reads=("gw1s",) + xnT_keys, writes=pk(which, 0, 2048),
                 tag=f"mmh{mch}")
            emit("scalar", lambda e, ps=ps, mch=mch: [
                e.activation(hT_t[mch], ps[:, 0:2048], AF.Gelu, bias=gb1s[:, mch:mch + 1])],
                reads=pk(which, 0, 2048) + ("gb1s",), writes=[f"hT{mch}"], tag=f"gelh{mch}")
        for i in range(NB):
            which = "A" if i % 2 == 0 else "B"
            ps = psA if i % 2 == 0 else psB
            def mml(e, i=i, ps=ps):
                r = []
                for j in range(8):
                    r.append(e.matmul(ps[:, 0:16],
                                      xnT_sb[:, j * 2048 + i * 128:j * 2048 + (i + 1) * 128],
                                      ceuT[:, j * 16:(j + 1) * 16],
                                      start=(j == 0), stop=(j == 7)))
                for j in range(4):
                    r.append(e.matmul(ps[:, 16:32], hT_t[j][:, i * 128:(i + 1) * 128],
                                      gw2s[:, j * 16:(j + 1) * 16],
                                      start=(j == 0), stop=(j == 3)))
                return r
            emit("tensor", mml,
                 reads=tuple(f"hT{m}" for m in range(4)) + ("ceuT", "gw2s") + xnT_keys,
                 writes=pk(which, 0, 32), tag=f"mml{i}")
            emit("vector", lambda e, ps=ps: [
                e.tensor_tensor(t16a[:, 0:16], ps[:, 16:32], gb2bc[:, 0:16], ALU.add)],
                reads=pk(which, 0, 32) + ("gb2bc",), writes=["t16a"], tag=f"lga{i}")
            emit("scalar", lambda e: [
                e.activation(t16a[:, 0:16], t16a[:, 0:16], AF.Sigmoid)],
                reads=["t16a"], writes=["t16a"], tag=f"lgs{i}")
            emit("vector", [
                lambda e, ps=ps, i=i: [e.scalar_tensor_tensor(
                    t16c[:, 0:16], ps[:, 0:16], rnorm_tm[:, i:i + 1],
                    t16a[:, 0:16], ALU.mult, ALU.add)],
                lambda e: [e.tensor_reduce(red1[:, 0:1], t16c[:, 0:16], AX.X, ALU.max)],
                lambda e: [e.tensor_scalar(t16b[:, 0:16], t16c[:, 0:16], red1[:, 0:1], None, ALU.is_equal)],
            ], reads=pk(which, 0, 32) + ("t16a", "rnorm"),
                writes=["lgt", "msk", "red10"], tag=f"lgm{i}", seq=True)
            emit("scalar", lambda e: [
                e.activation(t16c[:, 0:16], t16c[:, 0:16], AF.Exp)],
                reads=["lgt"], writes=["lgt"], tag=f"lge{i}")
            emit("vector", [
                lambda e: [e.tensor_reduce(red2[:, 0:1], t16c[:, 0:16], AX.X, ALU.add)],
                lambda e: [e.reciprocal(red2[:, 0:1], red2[:, 0:1])],
                lambda e: [e.tensor_scalar(t16c[:, 0:16], t16c[:, 0:16], red2[:, 0:1], None, ALU.mult)],
                lambda e, i=i: [e.tensor_tensor(wts_tm[:, i * 16:(i + 1) * 16], t16c[:, 0:16],
                                                t16b[:, 0:16], ALU.mult)],
            ], reads=["lgt", "msk"], writes=["wts_tm", "red20"], tag=f"lgw{i}", seq=True)
        dma("sdma", wts_scr[:, :], wts_tm[:, :], ["wts_tm"], ["wts_scr"], tag="stwts")
        AR.release(m1)
        P.barrier()

        AR.reset()
        qs_b = [AR.alloc(T, BF16) for _ in range(2)]
        ks_b = [AR.alloc(T, BF16) for _ in range(2)]
        vs_b = [AR.alloc(T, BF16) for _ in range(2)]
        ws_pb = [[AR.alloc(T, BF16) for _ in range(2)] for _ in range(2)]
        wN_p = [AR.alloc(T, BF16) for _ in range(2)]
        Ninv_p = [AR.alloc(NB * 128, BF16) for _ in range(2)]
        Lb_p = [AR.alloc(512, BF16) for _ in range(2)]
        Lt_p = [AR.alloc(512, BF16) for _ in range(2)]
        Xt_p = [[AR.alloc(512, BF16) for _ in range(2)] for _ in range(2)]
        En_p = [[AR.alloc(512, BF16) for _ in range(2)] for _ in range(2)]
        Et_p = [[AR.alloc(512, BF16) for _ in range(2)] for _ in range(2)]
        Vsb_p = [AR.alloc(T, BF16) for _ in range(2)]
        Zsb_p = [AR.alloc(T, BF16) for _ in range(2)]
        Csb_p = [AR.alloc(T, BF16) for _ in range(2)]
        Mt_p = [AR.alloc(128, BF16) for _ in range(2)]
        PT_p = [[AR.alloc((NB - s) * 128, BF16) for s in range(NB)] for _ in range(2)]
        crecipb = AR.alloc(T, BF16, parts=1)
        orbc = AR.alloc(T, BF16)
        obf = AR.alloc(T, BF16)

        def head_pair(h):
            hp = h % 2
            qs, ks, vs = qs_b[hp], ks_b[hp], vs_b[hp]
            ws_p = ws_pb[hp]
            kqs, kks, kvs = f"qs{hp}", f"ks{hp}", f"vs{hp}"
            dma("gdma", qs, qT_scr[h * 128:(h + 1) * 128, :], ["q_scr"], [kqs], tag=f"ldq{h}")
            dma("gdma", ks, kT_scr[h * 128:(h + 1) * 128, :], ["k_scr"], [kks], tag=f"ldk{h}")
            dma("sdma", vs,
                v_scr.rearrange("(s p) (hh d) -> p s hh d", p=128, d=128)[:, :, h, :],
                ["v_scr"], [kvs], tag=f"ldv{h}")
            for r in range(2):
                n = 2 * h + r
                dma("gdma", ws_p[r], W_scr[n * 128:(n + 1) * 128, :], ["W_scr"],
                    [f"ws{r}{hp}"], tag=f"ldw{n}")

            ctx = []
            for r in range(2):
                ctx.append(dict(
                    n=2 * h + r, ws=ws_p[r], kw=f"ws{r}{hp}", wN=wN_p[r], kwN=f"wN{r}",
                    which=("A" if r == 0 else "B"), psx=(psA if r == 0 else psB),
                    Lb=Lb_p[r], Lt=Lt_p[r], Xt=Xt_p[r], En=En_p[r], Et=Et_p[r],
                    kL=f"L{r}", kLt=f"Lt{r}",
                    kXt=[f"Xt{r}0", f"Xt{r}1"], kEn=[f"En{r}0", f"En{r}1"],
                    kEt=[f"Et{r}0", f"Et{r}1"]))
            for half in range(2):
                for r in range(2):
                    d = ctx[r]
                    def trw(e, half=half, ws=d["ws"], psx=d["psx"]):
                        return [e.matmul(psx[:, jj * 128:(jj + 1) * 128],
                                         ws[:, (half * 8 + jj) * 128:(half * 8 + jj + 1) * 128],
                                         identbf[:], start=True, stop=True)
                                for jj in range(8)]
                    emit("tensor", trw, reads=[d["kw"], "identbf"],
                         writes=pk(d["which"], 0, 1024), tag=f"trw{d['n']}{half}")
                for r in range(2):
                    d = ctx[r]
                    emit("scalar", lambda e, half=half, wN=d["wN"], psx=d["psx"]: [
                        e.activation(wN[:, half * 1024:(half + 1) * 1024], psx[:, 0:1024],
                                     AF.Identity)],
                        reads=pk(d["which"], 0, 1024), writes=[d["kwN"]],
                        tag=f"evw{d['n']}{half}")
            for g in range(4):
                for r in range(2):
                    d = ctx[r]
                    def mmG(e, g=g, ws=d["ws"], psx=d["psx"]):
                        return [e.matmul(psx[:, 1024 + jj * 128:1024 + (jj + 1) * 128],
                                         ws[:, (g * 4 + jj) * 128:(g * 4 + jj + 1) * 128],
                                         ws[:, (g * 4 + jj) * 128:(g * 4 + jj + 1) * 128],
                                         start=True, stop=True) for jj in range(4)]
                    emit("tensor", mmG, reads=[d["kw"]], writes=pk(d["which"], 1024, 1536),
                         tag=f"G{d['n']}{g}")
                for r in range(2):
                    d = ctx[r]
                    def evL(e, g=g, n=d["n"], psx=d["psx"], Lb=d["Lb"]):
                        return [e.scalar_tensor_tensor(
                            Lb[:, jj * 128:(jj + 1) * 128],
                            psx[:, 1024 + jj * 128:1024 + (jj + 1) * 128],
                            beta_tm[:, (g * 4 + jj) * 16 + n:(g * 4 + jj) * 16 + n + 1],
                            maskSs[:], ALU.mult, ALU.mult) for jj in range(4)]
                    emit("vector", evL, reads=pk(d["which"], 1024, 1536) + ("beta_tm", "maskSs"),
                         writes=[d["kL"]], tag=f"evL{d['n']}{g}")
                for r in range(2):
                    d = ctx[r]
                    def trL(e, Lb=d["Lb"], psx=d["psx"]):
                        return [e.matmul(psx[:, 1536 + jj * 128:1536 + (jj + 1) * 128],
                                         Lb[:, jj * 128:(jj + 1) * 128], identbf[:],
                                         start=True, stop=True)
                                for jj in range(4)]
                    emit("tensor", trL, reads=[d["kL"], "identbf"],
                         writes=pk(d["which"], 1536, 2048), tag=f"trL{d['n']}{g}")
                for r in range(2):
                    d = ctx[r]
                    emit("vector", lambda e, Lt=d["Lt"], psx=d["psx"]: [
                        e.tensor_copy(Lt, psx[:, 1536:2048])],
                        reads=pk(d["which"], 1536, 2048), writes=[d["kLt"]],
                        tag=f"evLt{d['n']}{g}")
                for r in range(2):
                    d = ctx[r]
                    emit("vector", lambda e, Lt=d["Lt"], Xt=d["Xt"]: [
                        e.tensor_tensor(Xt[0][:, jj * 128:(jj + 1) * 128], identbf[:],
                                        Lt[:, jj * 128:(jj + 1) * 128], ALU.subtract)
                        for jj in range(4)],
                        reads=[d["kLt"], "identbf"], writes=[d["kXt"][0]],
                        tag=f"x0{d['n']}{g}")
                for r in range(2):
                    d = ctx[r]
                    def e0(e, Lb=d["Lb"], Lt=d["Lt"], psx=d["psx"]):
                        r2 = []
                        for jj in range(4):
                            s = slice(jj * 128, (jj + 1) * 128)
                            r2.append(e.matmul(psx[:, 1024 + jj * 128:1024 + (jj + 1) * 128],
                                               Lt[:, s], Lb[:, s], start=True, stop=True))
                            r2.append(e.matmul(psx[:, 1536 + jj * 128:1536 + (jj + 1) * 128],
                                               Lb[:, s], Lt[:, s], start=True, stop=True))
                        return r2
                    emit("tensor", e0, reads=[d["kL"], d["kLt"]],
                         writes=pk(d["which"], 1024, 2048), tag=f"e0{d['n']}{g}")
                for r in range(2):
                    d = ctx[r]
                    emit("vector", lambda e, En=d["En"], Et=d["Et"], psx=d["psx"]: [
                        e.tensor_copy(En[0], psx[:, 1024:1536]),
                        e.tensor_copy(Et[0], psx[:, 1536:2048]),
                    ], reads=pk(d["which"], 1024, 2048), writes=[d["kEn"][0], d["kEt"][0]],
                        tag=f"eve0{d['n']}{g}")
                for it in range(6):
                    pa, pb = it % 2, 1 - it % 2
                    for r in range(2):
                        d = ctx[r]
                        def xup(e, En=d["En"], Xt=d["Xt"], pa=pa, psx=d["psx"]):
                            return [e.matmul(psx[:, 1024 + jj * 128:1024 + (jj + 1) * 128],
                                             En[pa][:, jj * 128:(jj + 1) * 128],
                                             Xt[pa][:, jj * 128:(jj + 1) * 128],
                                             start=True, stop=True) for jj in range(4)]
                        emit("tensor", xup, reads=[d["kEn"][pa], d["kXt"][pa]],
                             writes=pk(d["which"], 1024, 1536), tag=f"xu{d['n']}{g}{it}")
                    for r in range(2):
                        d = ctx[r]
                        emit("vector", lambda e, Xt=d["Xt"], pa=pa, pb=pb, psx=d["psx"]: [
                            e.tensor_tensor(Xt[pb], psx[:, 1024:1536], Xt[pa], ALU.add)],
                            reads=pk(d["which"], 1024, 1536) + (d["kXt"][pa],),
                            writes=[d["kXt"][pb]], tag=f"xa{d['n']}{g}{it}")
                    if it < 5:
                        for r in range(2):
                            d = ctx[r]
                            def esq(e, En=d["En"], Et=d["Et"], pa=pa, psx=d["psx"]):
                                return [e.matmul(psx[:, 1536 + jj * 128:1536 + (jj + 1) * 128],
                                                 Et[pa][:, jj * 128:(jj + 1) * 128],
                                                 En[pa][:, jj * 128:(jj + 1) * 128],
                                                 start=True, stop=True) for jj in range(4)]
                            emit("tensor", esq, reads=[d["kEt"][pa], d["kEn"][pa]],
                                 writes=pk(d["which"], 1536, 2048), tag=f"en{d['n']}{g}{it}")
                        for r in range(2):
                            d = ctx[r]
                            emit("scalar", lambda e, En=d["En"], pb=pb, psx=d["psx"]: [
                                e.activation(En[pb], psx[:, 1536:2048], AF.Identity)],
                                reads=pk(d["which"], 1536, 2048), writes=[d["kEn"][pb]],
                                tag=f"evn{d['n']}{g}{it}")
                        for r in range(2):
                            d = ctx[r]
                            def etq(e, En=d["En"], Et=d["Et"], pa=pa, psx=d["psx"]):
                                return [e.matmul(psx[:, 1536 + jj * 128:1536 + (jj + 1) * 128],
                                                 En[pa][:, jj * 128:(jj + 1) * 128],
                                                 Et[pa][:, jj * 128:(jj + 1) * 128],
                                                 start=True, stop=True) for jj in range(4)]
                            emit("tensor", etq, reads=[d["kEn"][pa], d["kEt"][pa]],
                                 writes=pk(d["which"], 1536, 2048), tag=f"et{d['n']}{g}{it}")
                        for r in range(2):
                            d = ctx[r]
                            emit("vector", lambda e, Et=d["Et"], pb=pb, psx=d["psx"]: [
                                e.tensor_copy(Et[pb], psx[:, 1536:2048])],
                                reads=pk(d["which"], 1536, 2048), writes=[d["kEt"][pb]],
                                tag=f"evt{d['n']}{g}{it}")
                for r in range(2):
                    d = ctx[r]
                    def trN(e, Xt=d["Xt"], psx=d["psx"]):
                        return [e.matmul(psx[:, 1024 + jj * 128:1024 + (jj + 1) * 128],
                                         Xt[0][:, jj * 128:(jj + 1) * 128], identbf[:],
                                         start=True, stop=True)
                                for jj in range(4)]
                    emit("tensor", trN, reads=[d["kXt"][0], "identbf"],
                         writes=pk(d["which"], 1024, 1536), tag=f"trN{d['n']}{g}")
                for r in range(2):
                    d = ctx[r]
                    emit("vector", lambda e, g=g, r=r, psx=d["psx"]: [
                        e.tensor_copy(Ninv_p[r][:, g * 512:(g + 1) * 512],
                                      psx[:, 1024:1536])],
                        reads=pk(d["which"], 1024, 1536), writes=[f"Ninv{r}:{g}"],
                        tag=f"evN{d['n']}{g}")

            for i in range(NB - 1, -1, -1):
                c0 = i * 128
                pr = []
                for r in range(2):
                    pr.append(dict(
                        n=2 * h + r, which=("A" if r == 0 else "B"),
                        ps=(psA if r == 0 else psB), ws=ws_p[r], kw=f"ws{r}{hp}",
                        wN=wN_p[r], kwN=f"wN{r}", Vsb=Vsb_p[r], Zsb=Zsb_p[r],
                        Ccur=Csb_p[r], Cnxt=Csb_p[r],
                        kC=f"C{r}", kCn=f"C{r}"))
                for r in range(2):
                    d = pr[r]
                    def mmV(e, i=i, c0=c0, ws=d["ws"], Ccur=d["Ccur"], ps=d["ps"]):
                        rr = []
                        for (c, w) in spans(c0, T):
                            rr.append(e.matmul(ps[:, c:c + w], ws[:, c0:c0 + 128],
                                               qs[:, c:c + w], start=True,
                                               stop=(i == NB - 1 or c + w <= c0 + 128)))
                            if i < NB - 1 and c + w > c0 + 128:
                                cc = max(c, c0 + 128)
                                rr.append(e.matmul(ps[:, cc:c + w], ws[:, c0:c0 + 128],
                                                   Ccur[:, cc:c + w], start=False, stop=True))
                        return rr
                    emit("tensor", mmV, reads=[d["kw"], kqs, d["kC"]],
                         writes=pk(d["which"], c0, T), tag=f"V{d['n']}_{i}")
                for r in range(2):
                    d = pr[r]
                    which, ps, Vsb, n = d["which"], d["ps"], d["Vsb"], d["n"]
                    B0 = min(T, ((c0 + 128 + 511) // 512) * 512)
                    Bm = min(T, max(B0, ((c0 + (T - c0) // 2 + 511) // 512) * 512))
                    emit("vector", lambda e, Vsb=Vsb, c0=c0, ps=ps, B0=B0, Bm=Bm: [
                        e.scalar_tensor_tensor(Vsb[:, c0:c0 + 128], ps[:, c0:c0 + 128],
                                               1.0, maskKs[:], ALU.mult, ALU.mult)] +
                        ([e.tensor_copy(Vsb[:, c0 + 128:Bm], ps[:, c0 + 128:Bm])]
                         if c0 + 128 < Bm else []),
                        reads=pk(which, c0, Bm) + ("maskKs",),
                        writes=[f"Vsbd{r}"], tag=f"evVd{n}_{i}")
                    if Bm < T:
                        emit("scalar", lambda e, Vsb=Vsb, ps=ps, Bm=Bm: [
                            e.activation(Vsb[:, Bm:T], ps[:, Bm:T], AF.Identity)],
                            reads=pk(which, Bm, T), writes=[f"Vsb{r}"],
                            tag=f"evV{n}_{i}")
                for r in range(2):
                    d = pr[r]
                    vkeys = [f"Vsb{r}", f"Vsbd{r}"]
                    def mmZ(e, i=i, c0=c0, Vsb=d["Vsb"], ps=d["ps"], r=r):
                        return [e.matmul(ps[:, c:c + w],
                                         Ninv_p[r][:, i * 128:(i + 1) * 128],
                                         Vsb[:, c:c + w], start=True, stop=True)
                                for (c, w) in spans(c0, T)]
                    emit("tensor", mmZ, reads=vkeys + [f"Ninv{r}:{i // 4}"],
                         writes=pk(d["which"], c0, T), tag=f"Z{d['n']}_{i}")
                for r in range(2):
                    d = pr[r]
                    which, ps, Zsb, n = d["which"], d["ps"], d["Zsb"], d["n"]
                    Bm = min(T, ((c0 + (T - c0) // 2 + 511) // 512) * 512)
                    emit("vector", lambda e, Zsb=Zsb, c0=c0, i=i, n=n, ps=ps, Bm=Bm: [
                        e.tensor_scalar(Zsb[:, c0:Bm], ps[:, c0:Bm],
                                        betaneg_tm[:, i * 16 + n:i * 16 + n + 1],
                                        None, ALU.mult)],
                        reads=pk(which, c0, Bm) + ("betaneg_tm",), writes=[f"Zsbd{r}"],
                        tag=f"evZd{n}_{i}")
                    if Bm < T:
                        emit("scalar", lambda e, Zsb=Zsb, i=i, n=n, ps=ps, Bm=Bm: [
                            e.activation(Zsb[:, Bm:T], ps[:, Bm:T], AF.Identity,
                                         scale=betaneg_tm[:, i * 16 + n:i * 16 + n + 1])],
                            reads=pk(which, Bm, T) + ("betaneg_tm",), writes=[f"Zsb{r}"],
                            tag=f"evZ{n}_{i}")
                for r in range(2):
                    d = pr[r]
                    def mmM(e, c0=c0, ws=d["ws"], ps=d["ps"]):
                        return [e.matmul(ps[:, c0:c0 + 128], ws[:, c0:c0 + 128],
                                         ks[:, c0:c0 + 128], start=True, stop=True)]
                    emit("tensor", mmM, reads=[d["kw"], kks],
                         writes=pk(d["which"], c0, c0 + 128), tag=f"M{d['n']}_{i}")
                for r in range(2):
                    d = pr[r]
                    emit("vector", lambda e, r=r, c0=c0, ps=d["ps"]: [
                        e.tensor_tensor(Mt_p[r], ps[:, c0:c0 + 128], maskSs[:], ALU.mult)],
                        reads=pk(d["which"], c0, c0 + 128) + ("maskSs",), writes=[f"Mt{r}"],
                        tag=f"evM{d['n']}_{i}")
                for r in range(2):
                    d = pr[r]
                    def mmS(e, i=i, c0=c0, Zsb=d["Zsb"], Ccur=d["Ccur"], r=r, ps=d["ps"]):
                        rr = []
                        first = True
                        for (c, w) in spans(c0, T):
                            rr.append(e.matmul(ps[:, c:c + w], ks[:, c0:c0 + 128],
                                               qs[:, c:c + w], start=True, stop=False))
                            if first:
                                rr.append(e.matmul(ps[:, c0:c0 + 128], mbigs[:],
                                                   identbf[:], start=False, stop=False))
                                first = False
                            haveC = (i < NB - 1 and c + w > c0 + 128)
                            rr.append(e.matmul(ps[:, c:c + w], Mt_p[r],
                                               Zsb[:, c:c + w], start=False,
                                               stop=(not haveC)))
                            if haveC:
                                cc = max(c, c0 + 128)
                                rr.append(e.matmul(ps[:, cc:c + w], ks[:, c0:c0 + 128],
                                                   Ccur[:, cc:c + w], start=False, stop=True))
                        return rr
                    emit("tensor", mmS, reads=[kks, kqs, f"Mt{r}", f"Zsb{r}", f"Zsbd{r}",
                                               d["kC"], "mbigs", "identbf"],
                         writes=pk(d["which"], c0, T), tag=f"S{d['n']}_{i}")
                for r in range(2):
                    d = pr[r]
                    PTt = PT_p[r][i]
                    emit("scalar", lambda e, PTt=PTt, c0=c0, ps=d["ps"]: [
                        e.activation(PTt[:, 0:T - c0], ps[:, c0:T], AF.Exp, scale=ISQD)],
                        reads=pk(d["which"], c0, T), writes=[f"PT{r}:{i}"],
                        tag=f"exp{d['n']}_{i}")

                for r in range(2):
                    d = pr[r]
                    def mmC(e, c0=c0, wN=d["wN"], Zsb=d["Zsb"], ps=d["ps"]):
                        return [e.matmul(ps[:, c:c + w], wN[:, c0:c0 + 128],
                                         Zsb[:, c:c + w], start=True, stop=True)
                                for (c, w) in spans(c0, T)]
                    emit("tensor", mmC, reads=[d["kwN"], f"Zsb{r}", f"Zsbd{r}"],
                         writes=pk(d["which"], c0, T), tag=f"Cm{d['n']}_{i}")
                for r in range(2):
                    d = pr[r]
                    which, ps = d["which"], d["ps"]
                    Cnxt, Ccur, kC, kCn = d["Cnxt"], d["Ccur"], d["kC"], d["kCn"]
                    if i == NB - 1:
                        emit("vector", lambda e, Cnxt=Cnxt, c0=c0, ps=ps: [
                            e.tensor_copy(Cnxt[:, c0:T], ps[:, c0:T])],
                            reads=pk(which, c0, T), writes=[kCn], tag=f"Cc{d['n']}_{i}")
                    else:
                        emit("vector", lambda e, Cnxt=Cnxt, Ccur=Ccur, c0=c0, ps=ps: [
                            e.tensor_copy(Cnxt[:, c0:c0 + 128], ps[:, c0:c0 + 128]),
                            e.tensor_tensor(Cnxt[:, c0 + 128:T], ps[:, c0 + 128:T],
                                            Ccur[:, c0 + 128:T], ALU.add)],
                            reads=pk(which, c0, T) + (kC,), writes=[kCn],
                            tag=f"Cc{d['n']}_{i}")

            for r in range(2):
                n = 2 * h + r
                ptkeys = [f"PT{r}:{s}" for s in range(NB)]
                def mmo(e, r=r):
                    rr = []
                    for s in range(NB):
                        c0s = s * 128
                        for (c, w) in spans(c0s, T):
                            slast = min(4 * (c // 512) + 3, NB - 1)
                            rr.append(e.matmul(psA[:, c:c + w], vs[:, s * 128:(s + 1) * 128],
                                               PT_p[r][s][:, c - c0s:c - c0s + w],
                                               start=(s == 0), stop=(s == slast)))
                    return rr
                emit("tensor", mmo, reads=[kvs] + ptkeys, writes=pk("A", 0, 2048),
                     tag=f"o{n}")
                def mmcs(e, r=r):
                    rr = []
                    for s in range(NB):
                        c0s = s * 128
                        for (c, w) in spans(c0s, T):
                            slast = min(4 * (c // 512) + 3, NB - 1)
                            rr.append(e.matmul(psB[0:1, c:c + w], onescol[:],
                                               PT_p[r][s][:, c - c0s:c - c0s + w],
                                               start=(s == 0), stop=(s == slast)))
                    return rr
                emit("tensor", mmcs, reads=["ones"] + ptkeys, writes=pk("B", 0, 2048),
                     tag=f"cs{n}")
                emit("scalar", [
                    lambda e: [e.activation(psB[0:1, 0:2048], psB[0:1, 0:2048], AF.Ln)],
                    lambda e: [e.activation(crecipb, psB[0:1, 0:2048], AF.Exp, scale=-1.0)],
                ], reads=pk("B", 0, 2048), writes=["crecip"] + list(pk("B", 0, 2048)),
                    tag=f"orc{n}", seq=True)
                def mmrb2(e):
                    return [e.matmul(psB[:, c:c + 512], onesrow[0:1, :],
                                     crecipb[:, c:c + 512], start=True, stop=True)
                            for c in range(0, T, 512)]
                emit("tensor", mmrb2, reads=["crecip", "ones"], writes=pk("B", 0, 2048),
                     tag=f"orb{n}")
                emit("vector", lambda e: [e.tensor_copy(orbc, psB[:, 0:2048])],
                     reads=pk("B", 0, 2048), writes=["orbc"], tag=f"orbe{n}")
                emit("vector", lambda e: [
                    e.tensor_tensor(obf, psA[:, 0:2048], orbc, ALU.mult)],
                    reads=pk("A", 0, 2048) + ("orbc",), writes=["obf"], tag=f"onrm{n}")
                dma("sdma", o_scr[n * 128:(n + 1) * 128, :], obf, ["obf"], ["o_scr"],
                    tag=f"sto{n}")

        for h in range(n_pairs):
            head_pair(h)
        P.barrier()

        AR.reset()
        xnT2 = AR.alloc(NB * 128 * 8, BF16)
        xnT2_keys = tuple(f"xnT2:{j}" for j in range(8))
        for j in range(8):
            dma("gdma", xnT2[:, j * 2048:(j + 1) * 2048], xnT_scr[j * 128:(j + 1) * 128, :],
                ["xnT_scr"], [f"xnT2:{j}"], tag=f"ldxnT{j}")
        wcw_b = [AR.alloc(2 * HID, BF16) for _ in range(2)]
        bcs = AR.alloc(2 * NEXP, F32)
        wb_b = [AR.alloc(T, BF16) for _ in range(2)]
        gout_b = [AR.alloc(T, BF16) for _ in range(2)]
        wcout_b = [AR.alloc(T, BF16) for _ in range(4)]
        wtsTr = AR.alloc(T, BF16, parts=16)
        wtb = AR.alloc(NB * 16, BF16)
        dma("sdma", bcs, bcolsp[:, :], ["in"], ["bcs"], tag="ldbcols")
        emit("vector", lambda e: [e.tensor_copy(wtb, wts_tm[:, :])],
             reads=["wts_tm"], writes=["wtb"], tag="wtsbf")
        for half in range(2):
            which = "A" if half == 0 else "B"
            psx = psA if half == 0 else psB
            def trw3(e, half=half, psx=psx):
                return [e.matmul(psx[0:16, jj * 128:(jj + 1) * 128],
                                 wtb[:, (half * 8 + jj) * 16:(half * 8 + jj + 1) * 16],
                                 identbf[:], start=True, stop=True) for jj in range(8)]
            emit("tensor", trw3, reads=["wtb", "identbf"], writes=pk(which, 0, 1024),
                 tag=f"trwts{half}")
            emit("vector", lambda e, half=half, psx=psx: [
                e.tensor_copy(wtsTr[:, half * 1024:(half + 1) * 1024], psx[0:16, 0:1024])],
                reads=pk(which, 0, 1024), writes=["wtsTr"], tag=f"evwts{half}")
        for c in range(NEXP):
            p = c % 2
            wcw = wcw_b[p]
            dma("gdma", wcw.rearrange("p (j m) -> p j m", j=8),
                Wcols[c].rearrange("(j p) m -> p j m", p=128),
                ["in"], [f"wcw{p}"], tag=f"ldwc{c}")
            which = "A" if p == 0 else "B"
            ps = psA if p == 0 else psB
            def mmwb(e, c=c, ps=ps):
                return [e.matmul(ps[:, cc:cc + 512], selcs[:, c * 128:(c + 1) * 128],
                                 wtsTr[:, cc:cc + 512], start=True, stop=True)
                        for cc in range(0, T, 512)]
            emit("tensor", mmwb, reads=["wtsTr", "selcs"], writes=pk(which, 0, 2048),
                 tag=f"mmwb{c}")
            emit("vector", lambda e, wb=wb_b[p], ps=ps: [e.tensor_copy(wb, ps[:, 0:2048])],
                 reads=pk(which, 0, 2048), writes=[f"wb{p}"], tag=f"evwb{c}")
            for nt in range(2):
                which2 = "B" if p == 0 else "A"
                ps2 = psB if p == 0 else psA
                def mmcols(e, wcw=wcw, nt=nt, ps2=ps2):
                    r = []
                    for cc in range(0, T, 512):
                        for j in range(8):
                            r.append(e.matmul(
                                ps2[:, cc:cc + 512],
                                wcw[:, j * 256 + nt * 128:j * 256 + (nt + 1) * 128],
                                xnT2[:, j * 2048 + cc:j * 2048 + cc + 512],
                                start=(j == 0), stop=(j == 7)))
                    return r
                emit("tensor", mmcols, reads=(f"wcw{p}",) + xnT2_keys,
                     writes=pk(which2, 0, 2048), tag=f"mmcol{c}_{nt}")
                gout = gout_b[nt]
                emit("scalar", lambda e, gout=gout, c=c, nt=nt, ps2=ps2: [
                    e.activation(gout, ps2[:, 0:2048], AF.Gelu,
                                 bias=bcs[:, c * 2 + nt:c * 2 + nt + 1])],
                    reads=pk(which2, 0, 2048) + ("bcs",), writes=[f"gout{nt}"],
                    tag=f"gel{c}_{nt}")
                wco = wcout_b[2 * p + nt]
                emit("vector", lambda e, gout=gout, wco=wco, wb=wb_b[p]: [
                    e.tensor_tensor(wco, gout, wb, ALU.mult)],
                    reads=[f"gout{nt}", f"wb{p}"], writes=[f"wco{2 * p + nt}"],
                    tag=f"wm{c}_{nt}")
                dma("sdma", wc_scr[(c * 2 + nt) * 128:(c * 2 + nt + 1) * 128, :], wco,
                    [f"wco{2 * p + nt}"], ["wc_scr"], tag=f"stwc{c}_{nt}")
        P.barrier()

        AR.reset()
        pws = AR.alloc(32 * HID, BF16)
        wos = AR.alloc(16 * HID, BF16)
        og_b = [AR.alloc(16 * 128, BF16) for _ in range(2)]
        ao_b = [AR.alloc(HID, BF16) for _ in range(2)]
        pwkeys = tuple(f"pws{j}" for j in range(4))
        dma("gdma", wos.rearrange("p (j m) -> p j m", j=16),
            WoT.rearrange("(j p) m -> p j m", p=128), ["in"], ["wos"], tag="ldWo")
        for j in range(4):
            dma("gdma", pws[:, j * 8192:(j + 1) * 8192].rearrange("p (jj m) -> p jj m", jj=8),
                projw.rearrange("(jj p) m -> p jj m", p=128)[:, j * 8:(j + 1) * 8, :],
                ["in"], [f"pws{j}"], tag=f"ldpw{j}")
        lngbc = AR.alloc(HID, F32); lnbbc = AR.alloc(HID, F32); pbbc = AR.alloc(HID, F32)
        mrow = AR.mark()
        rowf = AR.alloc(HID, F32, parts=1)
        rowh = AR.alloc(HID, BF16, parts=1)
        rowl = AR.alloc(HID, BF16, parts=1)
        rows = AR.alloc(HID, F32, parts=1)
        AR.release(mrow)
        wcg_b = [AR.alloc(32 * 128, BF16) for _ in range(2)]
        xng_b = [AR.alloc(HID, F32) for _ in range(2)]
        xg_b = [AR.alloc(HID, F32) for _ in range(2)]
        yb_b = [AR.alloc(HID, F32) for _ in range(2)]
        yn_b = [AR.alloc(HID, F32) for _ in range(2)]
        ob_b = [AR.alloc(HID, F32) for _ in range(2)]

        def bc_row(srcap, kin, dst, kout, whichps):
            ps = psA if whichps == "A" else psB
            dma("sdma", rowf, srcap, ["in", "rowf"], ["rowf"], tag=f"ld{kout}")
            emit("vector", [
                lambda e: [e.tensor_copy(rowh, rowf)],
                lambda e: [e.tensor_copy(rows, rowh)],
                lambda e: [e.tensor_tensor(rows, rowf, rows, ALU.subtract)],
                lambda e: [e.tensor_copy(rowl, rows)],
            ], reads=["rowf"], writes=["rowhl", "rowf"], tag=f"hl{kout}", seq=True)
            def mm(e):
                r = []
                for cc in range(0, HID, 512):
                    r.append(e.matmul(ps[:, cc:cc + 512], onesrow[0:1, :],
                                      rowh[:, cc:cc + 512], start=True, stop=False))
                    r.append(e.matmul(ps[:, cc:cc + 512], onesrow[0:1, :],
                                      rowl[:, cc:cc + 512], start=False, stop=True))
                return r
            emit("tensor", mm, reads=["rowhl", "ones"], writes=pk(whichps, 0, 1024),
                 tag=f"bc{kout}")
            emit("vector", lambda e, dst=dst: [e.tensor_copy(dst, ps[:, 0:1024])],
                 reads=pk(whichps, 0, 1024), writes=[kout], tag=f"ev{kout}")

        bc_row(projb[0:1, :], "projbr", pbbc, "pbbc", "A")
        bc_row(lng[0:1, :], "lngr", lngbc, "lngbc", "B")
        bc_row(lnb[0:1, :], "lnbr", lnbbc, "lnbbc", "A")
        P.barrier()
        if "bc_scr" in debug_taps:
            dma("sdma", bc_scr[0:128, :], pbbc, ["pbbc"], ["bc_scr"], tag="dbg_bc0")
            dma("sdma", bc_scr[128:256, :], lngbc, ["lngbc"], ["bc_scr"], tag="dbg_bc1")
            dma("sdma", bc_scr[256:384, :], lnbbc, ["lnbbc"], ["bc_scr"], tag="dbg_bc2")

        for i in range(NB):
            p = i % 2
            which = "A" if p == 0 else "B"
            ps = psA if p == 0 else psB
            wcg, xng, xg = wcg_b[p], xng_b[p], xg_b[p]
            yb, yn, ob = yb_b[p], yn_b[p], ob_b[p]
            dma("sdma", wcg,
                wc_scr.rearrange("(kk p) t -> p kk t", p=128)[:, :, i * 128:(i + 1) * 128],
                ["wc_scr"], [f"wcg{p}"], tag=f"ldwcg{i}")
            dma("sdma", xng, xn_scr[i * 128:(i + 1) * 128, :], ["xn_scr"], [f"xng{p}"],
                tag=f"ldxng{i}")
            dma("gdma", xg, x_in[i * 128:(i + 1) * 128, :], ["in"], [f"xg{p}"],
                tag=f"ldxg{i}")
            og = og_b[p]; ao = ao_b[p]
            dma("sdma", og,
                o_scr.rearrange("(nn pp) t -> pp nn t", pp=128)[:, :, i * 128:(i + 1) * 128],
                ["o_scr"], [f"og{p}"], tag=f"ldog{i}")
            def mmy(e, wcg=wcg, ps=ps):
                r = []
                for cc in range(0, HID, 512):
                    for j in range(32):
                        r.append(e.matmul(ps[:, cc:cc + 512], wcg[:, j * 128:(j + 1) * 128],
                                          pws[:, j * 1024 + cc:j * 1024 + cc + 512],
                                          start=(j == 0), stop=(j == 31)))
                return r
            emit("tensor", mmy, reads=(f"wcg{p}",) + pwkeys, writes=pk(which, 0, 1024),
                 tag=f"mmy{i}")
            def mmwo(e, og=og, ps=ps):
                r = []
                for cc in range(0, HID, 512):
                    for j in range(NH):
                        r.append(e.matmul(ps[:, 1024 + cc:1024 + cc + 512],
                                          og[:, j * 128:(j + 1) * 128],
                                          wos[:, j * 1024 + cc:j * 1024 + cc + 512],
                                          start=(j == 0), stop=(j == NH - 1)))
                return r
            emit("tensor", mmwo, reads=[f"og{p}", "wos"], writes=pk(which, 1024, 2048),
                 tag=f"mmwo{i}")
            emit("scalar", lambda e, ao=ao, ps=ps: [
                e.activation(ao, ps[:, 1024:2048], AF.Identity)],
                reads=pk(which, 1024, 2048), writes=[f"ao{p}"], tag=f"evwo{i}")
            emit("vector", [
                lambda e, yb=yb, ps=ps: [e.scalar_tensor_tensor(
                    yb, ps[:, 0:1024], 1.0, pbbc, ALU.mult, ALU.add)],
                lambda e, yb=yb, xng=xng: [e.tensor_tensor(yb, yb, xng, ALU.add)],
            ], reads=pk(which, 0, 1024) + (f"xng{p}", "pbbc"), writes=[f"yb{p}"],
                tag=f"yb{i}", seq=True)
            if "y_scr" in debug_taps:
                dma("sdma", y_scr[i * 128:(i + 1) * 128, :], yb, [f"yb{p}"], ["y_scr"],
                    tag=f"dbg_y{i}")
            R1 = red1[:, p:p + 1]
            R2 = red2[:, p:p + 1]
            kr1, kr2 = f"red1{p}", f"red2{p}"
            emit("vector", [
                lambda e, yb=yb, R1=R1: [e.tensor_reduce(R1, yb, AX.X, ALU.add)],
                lambda e, R1=R1: [e.tensor_scalar(R1, R1, 1.0 / HID, None, ALU.mult)],
                lambda e, yb=yb, yn=yn, R1=R1: [e.tensor_scalar(yn, yb, R1, None, ALU.subtract)],
            ], reads=[f"yb{p}"], writes=[f"yn{p}", kr1], tag=f"ylm{i}", seq=True)
            emit("scalar", lambda e, yn=yn, ob=ob: [e.activation(ob, yn, AF.Square)],
                 reads=[f"yn{p}"], writes=[f"ob{p}"], tag=f"yls{i}")
            emit("vector", [
                lambda e, ob=ob, R2=R2: [e.tensor_reduce(R2, ob, AX.X, ALU.add)],
                lambda e, R2=R2: [e.tensor_scalar(R2, R2, 1.0 / HID, None, ALU.mult)],
            ], reads=[f"ob{p}"], writes=[kr2], tag=f"ylv{i}", seq=True)
            emit("scalar", [
                lambda e, R2=R2: [e.activation(R2, R2, AF.Ln, bias=eps_ln[:])],
                lambda e, R2=R2: [e.activation(R2, R2, AF.Exp, scale=-0.5)],
            ], reads=[kr2, "eps"], writes=[kr2], tag=f"ylr{i}", seq=True)
            emit("vector", [
                lambda e, yn=yn, R2=R2: [e.tensor_scalar(yn, yn, R2, None, ALU.mult)],
                lambda e, yn=yn: [e.tensor_tensor(yn, yn, lngbc, ALU.mult)],
                lambda e, yn=yn, ob=ob: [e.tensor_tensor(ob, yn, lnbbc, ALU.add)],
                lambda e, ob=ob, xg=xg: [e.tensor_tensor(ob, ob, xg, ALU.add)],
                lambda e, ob=ob, ao=ao: [e.tensor_tensor(ob, ob, ao, ALU.add)],
            ], reads=[f"yn{p}", kr2, "lngbc", "lnbbc", f"xg{p}", f"ao{p}"],
                writes=[f"ob{p}", f"yn{p}"], tag=f"yf{i}", seq=True)
            dma("sdma", out[i * 128:(i + 1) * 128, :], ob, [f"ob{p}"], ["out"],
                tag=f"stout{i}")

        def body(ename):
            def run(eng):
                for waits, fn, tag, drain, seq, sk, tv in P.groups[ename]:
                    for dk, val in waits:
                        eng.wait_ge(sems[dk], val)
                    if drain:
                        eng.drain()
                    if seq:
                        instrs = None
                        for ix, sub in enumerate(fn):
                            if ix:
                                eng.drain()
                            instrs = sub(eng)
                    else:
                        instrs = fn(eng)
                    assert instrs, f"empty group {tag} on {ename}"
                    if ename in ("tensor", "vector", "scalar", "pool"):
                        instrs[-1].then_inc(sems[sk], 1)
                    else:
                        instrs[-1].then_inc(sems[sk], 16)
                if ename == "sdma":
                    for qe in ("sdma", "gdma"):
                        for j in range(Prog.NSEM):
                            k2 = f"{qe}{j}"
                            u = P.sem_uses.get(k2, 0)
                            if u:
                                eng.wait_ge(sems[k2], u * 16)
            return run

        block.tensor(body("tensor"))
        block.vector(body("vector"))
        block.scalar(body("scalar"))
        block.gpsimd(body("gdma"))
        block.sync(body("sdma"))

    return nc


def host_inputs(inp):
    f = np.ascontiguousarray
    d = {}
    d["WqT"] = f(inp["L0_attn_Wq"].T)
    d["WkT"] = f(inp["L0_attn_Wk"].T)
    d["WvT"] = f(inp["L0_attn_Wv"].T)
    d["WoT"] = f(inp["L0_attn_Wo"].T)
    d["wA1T"] = f(inp["L0_attn_wA1"].T)
    d["wA2T"] = f(inp["L0_attn_wA2"].T)
    d["wB1T"] = f(inp["L0_attn_wB1"].T)
    d["wB2T"] = f(inp["L0_attn_wB2"].T)
    d["convAp"] = f(inp["L0_attn_convA"].reshape(16, 128, 3).transpose(1, 0, 2).reshape(128, 48))
    d["convBp"] = f(inp["L0_attn_convB"].reshape(16, 128, 3).transpose(1, 0, 2).reshape(128, 48))
    d["btwT"] = f(inp["L0_attn_bt_w"].T)
    d["btb"] = f(inp["L0_attn_bt_b"].reshape(16, 1))
    d["phi"] = f(inp["L0_attn_phi"].reshape(2, 1))
    d["omr"] = f(inp["L0_attn_omega_raw"].reshape(2, 1))
    d["colemb"] = f(inp["L0_ffn_col_emb"])
    d["gw1"] = f(inp["L0_ffn_gate_w1"])
    d["gb1p"] = f(inp["L0_ffn_gate_b1"].reshape(4, 128).T)
    d["gw2"] = f(inp["L0_ffn_gate_w2"])
    d["gb2"] = f(inp["L0_ffn_gate_b2"].reshape(1, 16))
    d["Wcols"] = f(inp["L0_ffn_W_cols"])
    d["bcolsp"] = f(inp["L0_ffn_b_cols"].reshape(16, 2, 128).transpose(2, 0, 1).reshape(128, 32))
    d["projw"] = f(inp["L0_ffn_proj_w"])
    d["projb"] = f(inp["L0_ffn_proj_b"].reshape(1, HID))
    d["lng"] = f(inp["L0_ffn_ln_g"].reshape(1, HID))
    d["lnb"] = f(inp["L0_ffn_ln_b"].reshape(1, HID))
    d["identin"] = np.eye(128, dtype=np.float32)
    d["maskSin"] = np.tril(np.ones((128, 128), np.float32), -1)
    d["maskKin"] = np.triu(np.ones((128, 128), np.float32))
    d["pos4in"] = np.broadcast_to(np.arange(T, dtype=np.float32), (4, T)).copy()
    sel = np.zeros((16, 16 * 128), np.float32)
    for c in range(16):
        sel[c, c * 128:(c + 1) * 128] = 1.0
    d["selcin"] = sel
    sel4 = np.zeros((4, 4 * 128), np.float32)
    for c in range(4):
        sel4[c, c * 128:(c + 1) * 128] = 1.0
    d["sel4in"] = sel4
    d["dup24in"] = np.array([[1, 1, 0, 0], [0, 0, 1, 1]], np.float32)
    mb = np.zeros((128, 128), np.float32)
    mb[np.triu_indices(128, 1)] = -1e6
    d["mbigin"] = mb
    hp = float(np.float32(0.5 * math.pi))
    d["phsh4in"] = np.array([[hp], [0.0], [hp], [0.0]], np.float32)
    return d


def kernel(**inputs):
    shared = host_inputs(inputs)
    x = np.ascontiguousarray(inputs["L0_attn_hidden_states"].astype(np.float32))
    nc = build_nc()
    in_maps = []
    for b in range(8):
        m = dict(shared)
        m["x"] = np.ascontiguousarray(x[b])
        in_maps.append(m)
    res = run_bass_kernel_spmd(nc, in_maps, core_ids=list(range(8)))
    out = np.stack([res.results[b]["out"] for b in range(8)], axis=0)
    return out.astype(np.float32)




# version 14 (final)
# speedup vs baseline: 1.0689x; 1.0138x over previous
import math
import numpy as np
import concourse.bass as bass
import concourse.mybir as mybir
from concourse.bass_utils import run_bass_kernel_spmd

F32 = mybir.dt.float32
BF16 = mybir.dt.bfloat16
I32 = mybir.dt.int32
U8 = mybir.dt.uint8
AF = mybir.ActivationFunctionType
ALU = mybir.AluOpType
AX = mybir.AxisListType

T = 2048
HID = 1024
H, R, D = 8, 2, 128
NH = H * R
NB = T // 128
NEXP = 16
OMEGA_SCALE = 2.0 * math.pi / (512 * 16.0)
ISQD = 1.0 / math.sqrt(D)
TWO_PI = float(np.float32(2.0 * math.pi))
INV_2PI = float(np.float32(1.0 / (2.0 * math.pi)))

N_PAIRS = 8
ARENA_B = 177 * 1024

def spans(c0, c1, bank=512):
    out = []
    c = c0
    while c < c1:
        e = min(c1, (c // bank + 1) * bank)
        out.append((c, e - c))
        c = e
    return out

class Prog:
    CENG = ("tensor", "vector", "scalar", "pool")
    DENG = ("sdma", "gdma")
    ENGS = CENG + DENG
    NSEM = 6

    def __init__(self):
        self.groups = {e: [] for e in self.ENGS}
        self.count = {e: 0 for e in self.ENGS}
        self.sem_uses = {}
        self.writer = {}
        self.readers = {}
        self.vclock = {e: {} for e in self.ENGS}
        self.drained = {}
        self.barrier_snap = None

    def _token_for(self, eng):
        if eng in self.CENG:
            return (eng, eng, self.count[eng] + 1)
        j = self.count[eng] % self.NSEM
        sk = f"{eng}{j}"
        return (eng, sk, (self.sem_uses.get(sk, 0) + 1) * 16)

    def emit(self, eng, fn, reads=(), writes=(), tag="", seq=False):
        deps = {}
        own_dep = [0]
        def add(tok):
            de, sk, dv = tok
            if de == eng and de in self.CENG:
                if de == "tensor":
                    return
                own_dep[0] = max(own_dep[0], dv)
                return
            if deps.get(sk, 0) < dv:
                deps[sk] = dv
        if self.barrier_snap:
            for tok in self.barrier_snap:
                add(tok)
        for k in reads:
            w = self.writer.get(k)
            if w:
                add(w)
        for k in writes:
            w = self.writer.get(k)
            if w:
                add(w)
            for rdep in self.readers.get(k, ()):
                add(rdep)
        tok = self._token_for(eng)
        if eng in self.DENG:
            sk = tok[1]
            prev = self.sem_uses.get(sk, 0)
            if prev > 0:
                if deps.get(sk, 0) < prev * 16:
                    deps[sk] = prev * 16
            self.sem_uses[sk] = prev + 1
        vc = self.vclock[eng]
        waits = []
        for sk, dv in deps.items():
            if vc.get(sk, 0) >= dv:
                continue
            vc[sk] = dv
            waits.append((sk, dv))
        drain = False
        if own_dep[0] > self.drained.get(eng, 0):
            drain = True
            self.drained[eng] = self.count[eng]
        self.count[eng] += 1
        self.groups[eng].append((waits, fn, tag, drain, seq, tok[1], tok[2]))
        me = tok
        for k in writes:
            self.writer[k] = me
            self.readers[k] = []
        for k in reads:
            self.readers.setdefault(k, []).append(me)
        return me

    def barrier(self):
        snap = []
        for e in self.CENG:
            if self.count[e]:
                snap.append((e, e, self.count[e]))
        for e in self.DENG:
            for j in range(self.NSEM):
                sk = f"{e}{j}"
                u = self.sem_uses.get(sk, 0)
                if u:
                    snap.append((e, sk, u * 16))
        self.barrier_snap = snap

class Arena:
    def __init__(self, ap_u8, nbytes):
        self.ap = ap_u8
        self.nbytes = nbytes
        self.off = 0

    def mark(self):
        return self.off

    def reset(self):
        self.off = 0

    def release(self, m):
        self.off = m

    def alloc(self, cols, dtype, parts=128):
        esz = 2 if dtype in (BF16,) else 4
        start = (self.off + 31) // 32 * 32
        self.off = start + cols * esz
        assert self.off <= self.nbytes, f"arena overflow {self.off}"
        view = self.ap.bitcast(dtype)
        v = view[:, start // esz:start // esz + cols]
        if parts != 128:
            v = v[0:parts, :]
        return v

def build_nc(debug_taps=(), n_pairs=N_PAIRS):
    nc = bass.Bass("TRN2", target_bir_lowering=False, debug=False)

    def din(name, shape, dtype=F32):
        return nc.dram_tensor(name, list(shape), dtype, kind="ExternalInput").ap()

    def dscr(name, shape, dtype):
        kind = "ExternalOutput" if name in debug_taps else "Internal"
        return nc.dram_tensor(name, list(shape), dtype, kind=kind).ap()

    x_in = din("x", [T, HID])
    WqT = din("WqT", [HID, HID]); WkT = din("WkT", [HID, HID]); WvT = din("WvT", [HID, HID])
    WoT = din("WoT", [2 * HID, HID])
    wA1T = din("wA1T", [HID, 32]); wA2T = din("wA2T", [32, 2 * HID])
    wB1T = din("wB1T", [HID, 32]); wB2T = din("wB2T", [32, 2 * HID])
    convAp = din("convAp", [128, NB * 3]); convBp = din("convBp", [128, NB * 3])
    btwT = din("btwT", [HID, 16]); btb = din("btb", [16, 1])
    phi_in = din("phi", [2, 1]); omr_in = din("omr", [2, 1])
    colemb = din("colemb", [16, HID])
    gw1 = din("gw1", [HID, 512]); gb1p = din("gb1p", [128, 4])
    gw2 = din("gw2", [512, 16]); gb2 = din("gb2", [1, 16])
    Wcols = din("Wcols", [NEXP, HID, 256])
    bcolsp = din("bcolsp", [128, 2 * NEXP])
    projw = din("projw", [4 * HID, HID])
    projb = din("projb", [1, HID])
    lng = din("lng", [1, HID]); lnb = din("lnb", [1, HID])
    identin = din("identin", [128, 128])
    maskSin = din("maskSin", [128, 128])
    maskKin = din("maskKin", [128, 128])
    pos4in = din("pos4in", [4, T])
    selcin = din("selcin", [16, 16 * 128])
    sel4in = din("sel4in", [4, 4 * 128])
    mbigin = din("mbigin", [128, 128])
    dup24in = din("dup24in", [2, 4])
    phsh4in = din("phsh4in", [4, 1])

    out = nc.dram_tensor("out", [T, HID], F32, kind="ExternalOutput").ap()

    xn_scr = dscr("xn_scr", [T, HID], F32)
    xnT_scr = dscr("xnT_scr", [HID, T], BF16)
    qT_scr = dscr("qT_scr", [HID, T], BF16)
    kT_scr = dscr("kT_scr", [HID, T], BF16)
    v_scr = dscr("v_scr", [T, HID], BF16)
    W_scr = dscr("W_scr", [2 * HID, T], BF16)
    o_scr = dscr("o_scr", [NH * 128, T], BF16)
    attn_scr = dscr("attn_scr", [T, HID], F32)
    wc_scr = dscr("wc_scr", [4 * HID, T], BF16)
    beta_scr = dscr("beta_scr", [128, NB * 16], F32)
    wts_scr = dscr("wts_scr", [128, NB * 16], F32)
    y_scr = dscr("y_scr", [T, HID], F32)
    bc_scr = dscr("bc_scr", [3 * 128, HID], F32)

    P = Prog()

    from contextlib import ExitStack
    with ExitStack() as ST:
        def sb(name, shape, dtype):
            return ST.enter_context(nc.sbuf_tensor(name, shape, dtype))
        arena_t = sb("arena", [128, ARENA_B], U8)
        identbf = sb("identbf", [128, 128], BF16)
        maskSs = sb("maskSs", [128, 128], F32)
        maskKs = sb("maskKs", [128, 128], F32)
        onescol = sb("onescol", [128, 1], BF16)
        onesrow = sb("onesrow", [1, 128], BF16)
        selcs = sb("selcs", [16, 16 * 128], BF16)
        sel4s = sb("sel4s", [4, 4 * 128], BF16)
        mbigs = sb("mbigs", [128, 128], BF16)
        dup24s = sb("dup24s", [2, 4], F32)
        phsh4s = sb("phsh4s", [4, 1], F32)
        sm2 = sb("sm2", [2, 8], F32)
        sm4 = sb("sm4", [4, 8], F32)
        btbs = sb("btbs", [16, 1], F32)
        beta_tm = sb("beta_tm", [128, NB * 16], F32)
        betaneg_tm = sb("betaneg_tm", [128, NB * 16], F32)
        rnorm_tm = sb("rnorm_tm", [128, NB], F32)
        svar = sb("svar", [128, NB], F32)
        srstd = sb("srstd", [128, NB], F32)
        wts_tm = sb("wts_tm", [128, NB * 16], F32)
        eps_ln = sb("eps_ln", [128, 1], F32)
        eps24 = sb("eps24", [128, 1], F32)
        gb2bc = sb("gb2bc", [128, 16], F32)
        red1 = sb("red1", [128, 2], F32)
        red2 = sb("red2", [128, 2], F32)
        red3 = sb("red3", [128, 2], F32)
        t16a = sb("t16a", [128, 16], F32)
        t16b = sb("t16b", [128, 16], F32)
        t16c = sb("t16c", [128, 16], F32)
        psA = ST.enter_context(nc.psum_tensor("psA", [128, 2048], F32))
        psB = ST.enter_context(nc.psum_tensor("psB", [128, 2048], F32))
        smt = ST.enter_context(nc.semaphore("smt"))
        smv = ST.enter_context(nc.semaphore("smv"))
        sms = ST.enter_context(nc.semaphore("sms"))
        smp = ST.enter_context(nc.semaphore("smp"))
        dma_sems = {}
        for qe in ("sdma", "gdma"):
            for j in range(Prog.NSEM):
                dma_sems[f"{qe}{j}"] = ST.enter_context(nc.semaphore(f"{qe}{j}"))
        block = ST.enter_context(nc.Block())

        AR = Arena(arena_t[:, :], ARENA_B)
        sems = {"tensor": smt, "vector": smv, "scalar": sms, "pool": smp}
        sems.update(dma_sems)

        def pk(which, c0, c1):
            return tuple(f"ps{which}:{b}" for b in range(c0 // 512, (c1 + 511) // 512))

        emit = P.emit

        def dma(eng, dst, src, reads, writes, tag=""):
            emit(eng, lambda e, d=dst, s=src: [e.dma_start(out=d, in_=s)],
                 reads=reads, writes=writes, tag=tag)

        dma("gdma", identbf[:], identin[:, :], ["in"], ["identbf"])
        dma("sdma", maskSs[:], maskSin[:, :], ["in"], ["maskSs"])
        dma("sdma", maskKs[:], maskKin[:, :], ["in"], ["maskKs"])
        dma("gdma", selcs[:], selcin[:, :], ["in"], ["selcs"])
        dma("gdma", sel4s[:], sel4in[:, :], ["in"], ["sel4s"])
        dma("gdma", mbigs[:], mbigin[:, :], ["in"], ["mbigs"])
        dma("sdma", dup24s[:], dup24in[:, :], ["in"], ["dup24s"])
        dma("sdma", phsh4s[:], phsh4in[:, :], ["in"], ["phsh4s"])
        dma("sdma", btbs[:], btb[:, :], ["in"], ["btbs"])
        dma("sdma", sm2[0:2, 0:1], phi_in[:, :], ["in"], ["phi"])
        dma("sdma", sm2[0:2, 1:2], omr_in[:, :], ["in"], ["omr"])
        emit("vector", lambda e: [
            e.memset(onescol[:], 1.0), e.memset(onesrow[:], 1.0),
            e.memset(eps_ln[:], 1e-5), e.memset(eps24[:], 1e-24),
        ], writes=["ones", "eps"], tag="consts")

        xnT_sb = AR.alloc(NB * 128 * 8, BF16)
        xnT_keys = tuple(f"xnT:{i}" for i in range(NB))
        xnT_view = xnT_sb.rearrange("p (j t) -> p j t", j=8)

        m0 = AR.mark()
        xt_b = [AR.alloc(HID, F32) for _ in range(2)]
        xc_b = [AR.alloc(HID, F32) for _ in range(2)]
        xnf_b = [AR.alloc(HID, F32) for _ in range(2)]
        xnb_b = [AR.alloc(HID, BF16) for _ in range(2)]
        for i in range(NB):
            p = i % 2
            xt, xc, xnf, xnb = xt_b[p], xc_b[p], xnf_b[p], xnb_b[p]
            kx, kc, kn, knb = f"xt{p}", f"xc{p}", f"xnf{p}", f"xnb{p}"
            dma("sdma", xt, x_in[i * 128:(i + 1) * 128, :], ["in"], [kx], tag=f"ldx{i}")
            R1 = red1[:, p:p + 1]
            R2 = red2[:, p:p + 1]
            R3 = red3[:, p:p + 1]
            kr1, kr2, kr3 = f"red1{p}", f"red2{p}", f"red3{p}"
            emit("vector", [
                lambda e, xt=xt, R1=R1: [e.tensor_reduce(R1, xt, AX.X, ALU.add)],
                lambda e, R1=R1: [e.tensor_scalar(R1, R1, 1.0 / HID, None, ALU.mult)],
                lambda e, xt=xt, xc=xc, R1=R1: [e.tensor_scalar(xc, xt, R1, None, ALU.subtract)],
            ], reads=[kx], writes=[kc, kr1], tag=f"lnm{i}", seq=True)
            emit("scalar", lambda e, xc=xc, xnf=xnf: [e.activation(xnf, xc, AF.Square)],
                 reads=[kc], writes=[kn], tag=f"lnsq{i}")
            emit("vector", [
                lambda e, xnf=xnf, i=i: [e.tensor_reduce(svar[:, i:i + 1], xnf, AX.X, ALU.add)],
                lambda e, i=i: [e.tensor_scalar(svar[:, i:i + 1], svar[:, i:i + 1],
                                                1.0 / HID, None, ALU.mult)],
            ], reads=[kn], writes=[kr2, "svar"], tag=f"lnv{i}", seq=True)
            emit("scalar", [
                lambda e, i=i, R3=R3: [e.activation(R3, svar[:, i:i + 1], AF.Ln,
                                                    bias=eps_ln[:])],
                lambda e, i=i, R3=R3: [e.activation(srstd[:, i:i + 1], R3, AF.Exp,
                                                    scale=-0.5)],
            ], reads=[kr2, "eps", "svar"], writes=[kr3, "srstd"], tag=f"lnr{i}", seq=True)
            emit("vector", lambda e, xc=xc, xnf=xnf, i=i: [
                e.tensor_scalar(xnf, xc, srstd[:, i:i + 1], None, ALU.mult)],
                reads=[kc, "srstd"], writes=[kn], tag=f"lnx{i}")
            emit("scalar", lambda e, xnf=xnf, xnb=xnb: [e.activation(xnb, xnf, AF.Identity)],
                 reads=[kn], writes=[knb], tag=f"lnxb{i}")
            dma("sdma", xn_scr[i * 128:(i + 1) * 128, :], xnf, [kn], ["xn_scr"], tag=f"stxn{i}")
            which = "A" if p == 0 else "B"
            psx = psA if p == 0 else psB
            def trx(e, xnb=xnb, psx=psx):
                return [e.matmul(psx[:, j * 128:(j + 1) * 128],
                                 xnb[:, j * 128:(j + 1) * 128], identbf[:],
                                 start=True, stop=True)
                        for j in range(8)]
            emit("tensor", trx, reads=[knb, "identbf"], writes=pk(which, 0, 1024),
                 tag=f"trx{i}")
            emit("vector", lambda e, psx=psx, i=i: [
                e.tensor_copy(xnT_view[:, :, i * 128:(i + 1) * 128],
                              psx[:, 0:1024].rearrange("p (j c) -> p j c", j=8))],
                 reads=pk(which, 0, 1024), writes=[f"xnT:{i}"], tag=f"evx{i}")
        for j in range(8):
            dma("sdma", xnT_scr[j * 128:(j + 1) * 128, :],
                xnT_sb[:, j * 2048:(j + 1) * 2048], list(xnT_keys), ["xnT_scr"],
                tag=f"stxnT{j}")

        emit("vector", [
            lambda e: [e.tensor_tensor(rnorm_tm[:, :], srstd[:, :], srstd[:, :], ALU.mult)],
            lambda e: [e.tensor_tensor(rnorm_tm[:, :], rnorm_tm[:, :], svar[:, :], ALU.mult)],
            lambda e: [e.tensor_scalar(rnorm_tm[:, :], rnorm_tm[:, :], float(HID), None, ALU.mult)],
            lambda e: [e.tensor_scalar(rnorm_tm[:, :], rnorm_tm[:, :], eps24[:], None, ALU.max)],
        ], reads=["svar", "srstd", "eps"], writes=["rnorm"], tag="rnb", seq=True)
        emit("scalar", [
            lambda e: [e.activation(rnorm_tm[:, :], rnorm_tm[:, :], AF.Ln)],
            lambda e: [e.activation(rnorm_tm[:, :], rnorm_tm[:, :], AF.Exp, scale=-0.5)],
        ], reads=["rnorm"], writes=["rnorm"], tag="rnb2", seq=True)

        btwTs = AR.alloc(8 * 16, BF16)
        betaT_f = AR.alloc(T, F32, parts=16)
        betaT_bf = AR.alloc(T, BF16, parts=16)
        dma("gdma", btwTs.rearrange("p (j m) -> p j m", j=8), btwT.rearrange("(j p) m -> p j m", p=128), ["in"], ["btwTs"])
        def mmbeta(e):
            r = []
            for c in range(0, T, 512):
                for j in range(8):
                    r.append(e.matmul(psA[0:16, c:c + 512], btwTs[:, j * 16:(j + 1) * 16],
                                      xnT_sb[:, j * 2048 + c:j * 2048 + c + 512],
                                      start=(j == 0), stop=(j == 7)))
            return r
        emit("tensor", mmbeta, reads=("btwTs",) + xnT_keys, writes=pk("A", 0, 2048),
             tag="mmbeta")
        emit("scalar", lambda e: [
            e.activation(betaT_f, psA[0:16, 0:2048], AF.Sigmoid, bias=btbs[:])],
            reads=pk("A", 0, 2048) + ("btbs",), writes=["betaTf"], tag="betasig")
        emit("vector", lambda e: [
            e.tensor_scalar(betaT_bf, betaT_f, 2.0, None, ALU.mult)],
            reads=["betaTf"], writes=["betaTbf"], tag="beta2")
        for half in range(2):
            which = "A" if half == 0 else "B"
            psx = psA if half == 0 else psB
            def trb2(e, half=half, psx=psx):
                r = []
                for jj in range(8):
                    i = half * 8 + jj
                    r.append(e.matmul(psx[:, jj * 16:(jj + 1) * 16],
                                      betaT_bf[:, i * 128:(i + 1) * 128],
                                      identbf[0:16, 0:16], start=True, stop=True))
                return r
            emit("tensor", trb2, reads=["betaTbf", "identbf"], writes=pk(which, 0, 128),
                 tag=f"trbeta{half}")
            def evb(e, half=half, psx=psx):
                r = []
                for jj in range(8):
                    i = half * 8 + jj
                    r.append(e.tensor_copy(beta_tm[:, i * 16:(i + 1) * 16],
                                           psx[:, jj * 16:(jj + 1) * 16]))
                return r
            emit("vector", evb, reads=pk(which, 0, 128), writes=["beta_tm"],
                 tag=f"evbeta{half}")
        emit("vector", lambda e: [
            e.tensor_scalar(betaneg_tm[:, :], beta_tm[:, :], -1.0, None, ALU.mult)],
            reads=["beta_tm"], writes=["betaneg_tm"], tag="betaneg")
        dma("sdma", beta_scr[:, :], beta_tm[:, :], ["beta_tm"], ["beta_scr"], tag="stbeta")
        AR.release(m0)
        P.barrier()

        cbc = [AR.alloc(T, BF16) for _ in range(2)]
        sbc = [AR.alloc(T, BF16) for _ in range(2)]
        m1 = AR.mark()
        pos4 = AR.alloc(T, F32, parts=4)
        th4 = AR.alloc(T, F32, parts=4)
        sc4 = AR.alloc(T, F32, parts=4)
        si4 = AR.alloc(T, I32, parts=4)
        csrow = AR.alloc(T, BF16, parts=4)
        dma("sdma", pos4, pos4in[:, :], ["in"], ["pos4"], tag="ldpos")
        emit("scalar", lambda e: [
            e.activation(sm2[0:2, 2:3], sm2[0:2, 1:2], AF.Abs)],
            reads=["omr"], writes=["sp1"], tag="spa")
        emit("scalar", lambda e: [
            e.activation(sm2[0:2, 3:4], sm2[0:2, 2:3], AF.Exp, scale=-1.0)],
            reads=["sp1"], writes=["sp2"], tag="spe")
        emit("vector", lambda e: [
            e.tensor_scalar(sm2[0:2, 4:5], sm2[0:2, 3:4], 1.0, None, ALU.add),
            e.tensor_scalar(sm2[0:2, 5:6], sm2[0:2, 1:2], 0.0, None, ALU.max),
        ], reads=["sp2", "omr"], writes=["sp3"], tag="spv")
        emit("scalar", lambda e: [
            e.activation(sm2[0:2, 6:7], sm2[0:2, 4:5], AF.Ln)],
            reads=["sp3"], writes=["sp4"], tag="spl")
        emit("vector", [
            lambda e: [e.tensor_tensor(sm2[0:2, 7:8], sm2[0:2, 5:6], sm2[0:2, 6:7], ALU.add)],
            lambda e: [e.tensor_scalar(sm2[0:2, 7:8], sm2[0:2, 7:8], OMEGA_SCALE, None, ALU.mult)],
        ], reads=["sp4", "sp3"], writes=["omega2"], tag="spf", seq=True)
        emit("tensor", lambda e: [
            e.matmul(psA[0:4, 0:1], dup24s[:, :], sm2[0:2, 0:1], start=True, stop=True),
            e.matmul(psA[0:4, 1:2], dup24s[:, :], sm2[0:2, 7:8], start=True, stop=True),
        ], reads=["phi", "omega2", "dup24s"], writes=pk("A", 0, 2), tag="dupmm")
        emit("vector", lambda e: [
            e.tensor_tensor(sm4[0:4, 0:1], psA[0:4, 0:1], phsh4s[:, :], ALU.add),
            e.tensor_copy(sm4[0:4, 1:2], psA[0:4, 1:2]),
        ], reads=pk("A", 0, 2) + ("phsh4s",), writes=["pom4"], tag="dupev")
        emit("vector", lambda e: [
            e.tensor_scalar(th4, pos4, sm4[0:4, 1:2], sm4[0:4, 0:1], ALU.mult, ALU.add),
        ], reads=["pos4", "pom4"], writes=["th4"], tag="th4")
        emit("vector", [
            lambda e: [e.tensor_scalar(sc4, th4, INV_2PI, None, ALU.mult)],
            lambda e: [e.tensor_copy(si4, sc4)],
            lambda e: [e.tensor_copy(th4, si4)],
            lambda e: [e.tensor_tensor(sc4, sc4, th4, ALU.subtract)],
        ], reads=["th4"], writes=["sc4", "th4"], tag="rr4", seq=True)
        emit("scalar", lambda e: [
            e.activation(th4, sc4, AF.Sin, scale=TWO_PI)],
            reads=["sc4"], writes=["sin4"], tag="sin4")
        emit("vector", lambda e: [e.tensor_copy(csrow, th4)],
             reads=["sin4"], writes=["csrow"], tag="csbf")
        for r in range(2):
            for ci in range(2):
                idx = 2 * r + ci
                which = "A" if idx % 2 == 0 else "B"
                ps = psA if idx % 2 == 0 else psB
                def mmbc(e, idx=idx, ps=ps):
                    return [e.matmul(ps[:, c:c + 512], sel4s[:, idx * 128:(idx + 1) * 128],
                                     csrow[:, c:c + 512], start=True, stop=True)
                            for c in range(0, T, 512)]
                emit("tensor", mmbc, reads=["csrow", "sel4s"], writes=pk(which, 0, 2048),
                     tag=f"bccs{idx}")
                dst = cbc[r] if ci == 0 else sbc[r]
                kcs = f"cs{r}{ci}"
                emit("vector", lambda e, dst=dst, ps=ps: [e.tensor_copy(dst, ps[:, 0:2048])],
                     reads=pk(which, 0, 2048), writes=[kcs], tag=f"evcs{idx}")
        AR.release(m1)
        P.barrier()

        m1 = AR.mark()
        wA1Ts = AR.alloc(8 * 32, BF16); wB1Ts = AR.alloc(8 * 32, BF16)
        wA2Ts = AR.alloc(2 * HID, BF16, parts=32); wB2Ts = AR.alloc(2 * HID, BF16, parts=32)
        convAs = AR.alloc(NB * 3, F32); convBs = AR.alloc(NB * 3, F32)
        dma("gdma", wA1Ts.rearrange("p (j m) -> p j m", j=8), wA1T.rearrange("(j p) m -> p j m", p=128), ["in"], ["wA1Ts"])
        dma("gdma", wB1Ts.rearrange("p (j m) -> p j m", j=8), wB1T.rearrange("(j p) m -> p j m", p=128), ["in"], ["wB1Ts"])
        dma("gdma", wA2Ts, wA2T[:, :], ["in"], ["wA2Ts"])
        dma("gdma", wB2Ts, wB2T[:, :], ["in"], ["wB2Ts"])
        dma("sdma", convAs, convAp[:, :], ["in"], ["convAs"])
        dma("sdma", convBs, convBp[:, :], ["in"], ["convBs"])
        uTA = AR.alloc(T, BF16, parts=32)
        uTB = AR.alloc(T, BF16, parts=32)
        wn_b = [AR.alloc(T, BF16) for _ in range(2)]
        sq_b = [AR.alloc(T, BF16) for _ in range(2)]
        cv_b = [AR.alloc(T, F32) for _ in range(2)]
        cv2_b = [AR.alloc(T, F32) for _ in range(2)]
        xa_b = [AR.alloc(T, BF16) for _ in range(2)]
        xb_b = [AR.alloc(T, BF16) for _ in range(2)]
        wfin_b = [AR.alloc(T, BF16) for _ in range(2)]
        rinvS = AR.alloc(T, F32, parts=1)
        rrowS = AR.alloc(T, BF16, parts=1)
        rbcA = AR.alloc(T, F32); rbcB = AR.alloc(T, F32)

        def mmu(e, w1s, uT):
            r = []
            for c in range(0, T, 512):
                for j in range(8):
                    r.append(e.matmul(psA[0:32, c:c + 512], w1s[:, j * 32:(j + 1) * 32],
                                      xnT_sb[:, j * 2048 + c:j * 2048 + c + 512],
                                      start=(j == 0), stop=(j == 7)))
            return r
        for (pn, w1s, w2s, k1, k2, uT) in (("A", wA1Ts, wA2Ts, "wA1Ts", "wA2Ts", uTA),
                                           ("B", wB1Ts, wB2Ts, "wB1Ts", "wB2Ts", uTB)):
            emit("tensor", lambda e, w1s=w1s, uT=uT: mmu(e, w1s, uT),
                 reads=(k1,) + xnT_keys, writes=pk("A", 0, 2048), tag=f"mmu{pn}")
            emit("vector", lambda e, uT=uT: [e.tensor_copy(uT, psA[0:32, 0:2048])],
                 reads=pk("A", 0, 2048), writes=[f"uT{pn}"], tag=f"evu{pn}")
            for j in range(NB):
                p = j % 2
                def mmw(e, j=j, w2s=w2s, uT=uT):
                    return [e.matmul(psA[:, c:c + 512], w2s[:, j * 128:(j + 1) * 128],
                                     uT[:, c:c + 512], start=True, stop=True)
                            for c in range(0, T, 512)]
                emit("tensor", mmw, reads=[k2, f"uT{pn}"], writes=pk("A", 0, 2048),
                     tag=f"p1m{pn}{j}")
                emit("scalar", lambda e, sq=sq_b[p]: [
                    e.activation(sq, psA[:, 0:2048], AF.Square)],
                    reads=pk("A", 0, 2048), writes=[f"sq{p}"], tag=f"p1s{pn}{j}")
                def mmss(e, sq=sq_b[p], j=j):
                    return [e.matmul(psB[0:1, c:c + 512], onescol[:], sq[:, c:c + 512],
                                     start=(j == 0), stop=(j == NB - 1))
                            for c in range(0, T, 512)]
                emit("tensor", mmss, reads=[f"sq{p}", "ones"], writes=pk("B", 0, 2048),
                     tag=f"p1ss{pn}{j}")
            rinv = rinvS
            rrow = rrowS
            rbc = rbcA if pn == "A" else rbcB
            emit("scalar", [
                lambda e, rinv=rinv: [e.activation(rinv, psB[0:1, 0:2048], AF.Ln)],
                lambda e, rinv=rinv: [e.activation(rinv, rinv, AF.Exp, scale=-0.5)],
            ], reads=pk("B", 0, 2048), writes=[f"rinv{pn}"], tag=f"ri{pn}", seq=True)
            emit("vector", lambda e, rinv=rinv, rrow=rrow: [e.tensor_copy(rrow, rinv)],
                 reads=[f"rinv{pn}"], writes=[f"rrow{pn}"], tag=f"rr{pn}")
            def mmrb(e, rrow=rrow):
                return [e.matmul(psB[:, c:c + 512], onesrow[0:1, :], rrow[:, c:c + 512],
                                 start=True, stop=True) for c in range(0, T, 512)]
            emit("tensor", mmrb, reads=[f"rrow{pn}", "ones"], writes=pk("B", 0, 2048),
                 tag=f"rbm{pn}")
            emit("vector", lambda e, rbc=rbc: [e.tensor_copy(rbc, psB[:, 0:2048])],
                 reads=pk("B", 0, 2048), writes=[f"rbc{pn}"], tag=f"rbe{pn}")
            emit("vector", lambda e, uT=uT, rbc=rbc: [
                e.tensor_tensor(uT, uT, rbc[0:32, :], ALU.mult)],
                reads=[f"uT{pn}", f"rbc{pn}"], writes=[f"uT{pn}"], tag=f"usc{pn}")

        for j in range(NB):
            p = j % 2
            r_ = j % 2
            for (pn, w2s, k2, uT, convs, kconv, xres, kxr, rbc, krbc) in (
                    ("A", wA2Ts, "wA2Ts", uTA, convAs, "convAs", xa_b[p], f"xa{p}", rbcA, "rbcA"),
                    ("B", wB2Ts, "wB2Ts", uTB, convBs, "convBs", xb_b[p], f"xb{p}", rbcB, "rbcB")):
                wn, cv, cv2 = wn_b[p], cv_b[p], cv2_b[p]
                def mmw2(e, j=j, w2s=w2s, uT=uT):
                    return [e.matmul(psA[:, c:c + 512], w2s[:, j * 128:(j + 1) * 128],
                                     uT[:, c:c + 512], start=True, stop=True)
                            for c in range(0, T, 512)]
                emit("tensor", mmw2, reads=[k2, f"uT{pn}"], writes=pk("A", 0, 2048),
                     tag=f"p2m{pn}{j}")
                emit("scalar", lambda e, wn=wn: [
                    e.activation(wn, psA[:, 0:2048], AF.Identity)],
                    reads=pk("A", 0, 2048), writes=[f"wn{p}"], tag=f"p2n{pn}{j}")
                emit("scalar", lambda e, wn=wn, cv=cv, j=j, convs=convs: [
                    e.activation(cv, wn, AF.Identity, scale=convs[:, j * 3 + 2:j * 3 + 3])],
                    reads=[f"wn{p}", kconv], writes=[f"cv{p}"], tag=f"cva{pn}{j}")
                emit("vector", lambda e, wn=wn, cv=cv, cv2=cv2, j=j, convs=convs: [
                    e.scalar_tensor_tensor(cv2[:, 1:T], wn[:, 0:T - 1],
                                           convs[:, j * 3 + 1:j * 3 + 2], cv[:, 1:T],
                                           ALU.mult, ALU.add),
                    e.tensor_copy(cv2[:, 0:1], cv[:, 0:1]),
                ], reads=[f"wn{p}", f"cv{p}", kconv], writes=[f"cv2{p}"], tag=f"cvb{pn}{j}")
                emit("vector", lambda e, wn=wn, cv=cv, cv2=cv2, j=j, convs=convs: [
                    e.scalar_tensor_tensor(cv[:, 2:T], wn[:, 0:T - 2],
                                           convs[:, j * 3 + 0:j * 3 + 1], cv2[:, 2:T],
                                           ALU.mult, ALU.add),
                    e.tensor_copy(cv[:, 0:2], cv2[:, 0:2]),
                ], reads=[f"wn{p}", f"cv2{p}", kconv], writes=[f"cv{p}"], tag=f"cvc{pn}{j}")
                emit("scalar", lambda e, cv=cv, xres=xres: [
                    e.activation(xres, cv, AF.Silu)],
                    reads=[f"cv{p}"], writes=[kxr], tag=f"sil{pn}{j}")
            wf = wfin_b[p]
            emit("vector", lambda e, p=p, r_=r_, wf=wf: [
                e.tensor_tensor(cv_b[p], xa_b[p], cbc[r_], ALU.mult),
                e.tensor_tensor(cv2_b[p], xb_b[p], sbc[r_], ALU.mult),
            ], reads=[f"xa{p}", f"xb{p}", f"cs{r_}0", f"cs{r_}1"],
                writes=[f"cv{p}", f"cv2{p}"], tag=f"wc1{j}")
            emit("vector", lambda e, p=p, wf=wf: [
                e.tensor_tensor(wf, cv_b[p], cv2_b[p], ALU.add)],
                reads=[f"cv{p}", f"cv2{p}"], writes=[f"wf{p}"], tag=f"wc2{j}")
            dma("sdma", W_scr[j * 128:(j + 1) * 128, :], wf, [f"wf{p}"], ["W_scr"],
                tag=f"stW{j}")
        AR.release(m1)
        P.barrier()

        m1 = AR.mark()
        wbig = AR.alloc(8 * HID, BF16)
        stg_b = [AR.alloc(T, BF16) for _ in range(2)]
        for (nameW, Wap, scr, tmaj) in (("q", WqT, qT_scr, False),
                                        ("k", WkT, kT_scr, False),
                                        ("v", WvT, v_scr, True)):
            dma("gdma", wbig.rearrange("p (j m) -> p j m", j=8),
                Wap.rearrange("(j p) m -> p j m", p=128),
                ["in", "wbig"], ["wbig"], tag=f"ldW{nameW}")
            if not tmaj:
                for mch in range(8):
                    p = mch % 2
                    which = "A" if p == 0 else "B"
                    ps = psA if p == 0 else psB
                    def mmq(e, mch=mch, ps=ps):
                        r = []
                        for c in range(0, T, 512):
                            for j in range(8):
                                r.append(e.matmul(
                                    ps[:, c:c + 512],
                                    wbig[:, j * 1024 + mch * 128:j * 1024 + (mch + 1) * 128],
                                    xnT_sb[:, j * 2048 + c:j * 2048 + c + 512],
                                    start=(j == 0), stop=(j == 7)))
                        return r
                    emit("tensor", mmq, reads=("wbig",) + xnT_keys,
                         writes=pk(which, 0, 2048), tag=f"mm{nameW}{mch}")
                    emit("vector", lambda e, ps=ps, st=stg_b[p]: [
                        e.tensor_copy(st, ps[:, 0:2048])],
                        reads=pk(which, 0, 2048), writes=[f"stg{p}"], tag=f"ev{nameW}{mch}")
                    dma("sdma", scr[mch * 128:(mch + 1) * 128, :], stg_b[p],
                        [f"stg{p}"], [f"{nameW}_scr"], tag=f"st{nameW}{mch}")
            else:
                for i in range(NB):
                    p = i % 2
                    which = "A" if p == 0 else "B"
                    ps = psA if p == 0 else psB
                    def mmv(e, i=i, ps=ps):
                        r = []
                        for c in range(0, HID, 512):
                            for j in range(8):
                                r.append(e.matmul(
                                    ps[:, c:c + 512],
                                    xnT_sb[:, j * 2048 + i * 128:j * 2048 + (i + 1) * 128],
                                    wbig[:, j * 1024 + c:j * 1024 + c + 512],
                                    start=(j == 0), stop=(j == 7)))
                        return r
                    emit("tensor", mmv, reads=("wbig",) + xnT_keys,
                         writes=pk(which, 0, 1024), tag=f"mmv{i}")
                    emit("vector", lambda e, ps=ps, st=stg_b[p]: [
                        e.tensor_copy(st[:, 0:1024], ps[:, 0:1024])],
                        reads=pk(which, 0, 1024), writes=[f"stg{p}"], tag=f"evv{i}")
                    dma("sdma", scr[i * 128:(i + 1) * 128, :], stg_b[p][:, 0:1024],
                        [f"stg{p}"], ["v_scr"], tag=f"stv{i}")
        AR.release(m1)
        P.barrier()

        m1 = AR.mark()
        gw1s = AR.alloc(4 * HID, BF16)
        gb1s = AR.alloc(4, F32)
        hT_t = [AR.alloc(T, BF16) for _ in range(4)]
        gw2s = AR.alloc(4 * 16, BF16)
        ceu = AR.alloc(HID, F32, parts=16)
        ceub = AR.alloc(HID, BF16, parts=16)
        cesq = AR.alloc(HID, F32, parts=16)
        ceuT = AR.alloc(8 * 16, BF16)
        dma("gdma", gw1s.rearrange("p (j m) -> p j m", j=8), gw1.rearrange("(j p) m -> p j m", p=128), ["in"], ["gw1s"])
        dma("sdma", gb1s, gb1p[:, :], ["in"], ["gb1s"])
        dma("gdma", gw2s.rearrange("p (j m) -> p j m", j=4), gw2.rearrange("(j p) m -> p j m", p=128), ["in"], ["gw2s"])
        dma("sdma", ceu, colemb[:, :], ["in"], ["ceu"], tag="ldce")
        dma("sdma", gb2bc[0:1, :], gb2[0:1, :], ["in"], ["gb2r"], tag="ldgb2")
        emit("vector", lambda e: [e.tensor_copy(t16a[0:1, 0:16].bitcast(BF16)[0:1, 0:16]
                                                if False else t16b[0:1, 0:16], gb2bc[0:1, 0:16])],
             reads=["gb2r"], writes=["gb2row"], tag="gb2cp") if False else None
        gb2rowb = AR.alloc(16, BF16, parts=1)
        emit("vector", lambda e: [e.tensor_copy(gb2rowb, gb2bc[0:1, 0:16])],
             reads=["gb2r"], writes=["gb2rowb"], tag="gb2cp")
        emit("tensor", lambda e: [e.matmul(psA[:, 0:16], onesrow[0:1, :], gb2rowb,
                                           start=True, stop=True)],
             reads=["gb2rowb", "ones"], writes=pk("A", 0, 16), tag="gb2mm")
        emit("vector", lambda e: [e.tensor_copy(gb2bc[:, :], psA[:, 0:16])],
             reads=pk("A", 0, 16), writes=["gb2bc"], tag="gb2ev")
        emit("scalar", lambda e: [e.activation(cesq, ceu, AF.Square)],
             reads=["ceu"], writes=["cesq"], tag="cesq")
        emit("vector", [
            lambda e: [e.tensor_reduce(t16a[0:16, 0:1], cesq, AX.X, ALU.add)],
            lambda e: [e.tensor_scalar(t16a[0:16, 0:1], t16a[0:16, 0:1], eps24[0:16, :], None, ALU.max)],
        ], reads=["cesq", "eps"], writes=["cen2"], tag="cered", seq=True)
        emit("scalar", [
            lambda e: [e.activation(t16a[0:16, 0:1], t16a[0:16, 0:1], AF.Ln)],
            lambda e: [e.activation(t16a[0:16, 0:1], t16a[0:16, 0:1], AF.Exp, scale=-0.5)],
        ], reads=["cen2"], writes=["cen2"], tag="ceri", seq=True)
        emit("vector", lambda e: [
            e.tensor_scalar(ceub, ceu, t16a[0:16, 0:1], None, ALU.mult)],
            reads=["ceu", "cen2"], writes=["ceub"], tag="ceun")
        def cetr(e):
            return [e.matmul(psB[:, j * 16:(j + 1) * 16],
                             ceub[:, j * 128:(j + 1) * 128], identbf[0:16, 0:16],
                             start=True, stop=True)
                    for j in range(8)]
        emit("tensor", cetr, reads=["ceub", "identbf"], writes=pk("B", 0, 128), tag="cetr")
        emit("vector", lambda e: [e.tensor_copy(ceuT, psB[:, 0:128])],
             reads=pk("B", 0, 128), writes=["ceuT"], tag="ceuTev")
        for mch in range(4):
            which = "A" if mch % 2 == 0 else "B"
            ps = psA if mch % 2 == 0 else psB
            def mmh(e, mch=mch, ps=ps):
                r = []
                for c in range(0, T, 512):
                    for j in range(8):
                        r.append(e.matmul(ps[:, c:c + 512],
                                          gw1s[:, j * 512 + mch * 128:j * 512 + (mch + 1) * 128],
                                          xnT_sb[:, j * 2048 + c:j * 2048 + c + 512],
                                          start=(j == 0), stop=(j == 7)))
                return r
            emit("tensor", mmh, reads=("gw1s",) + xnT_keys, writes=pk(which, 0, 2048),
                 tag=f"mmh{mch}")
            emit("scalar", lambda e, ps=ps, mch=mch: [
                e.activation(hT_t[mch], ps[:, 0:2048], AF.Gelu, bias=gb1s[:, mch:mch + 1])],
                reads=pk(which, 0, 2048) + ("gb1s",), writes=[f"hT{mch}"], tag=f"gelh{mch}")
        for i in range(NB):
            which = "A" if i % 2 == 0 else "B"
            ps = psA if i % 2 == 0 else psB
            def mml(e, i=i, ps=ps):
                r = []
                for j in range(8):
                    r.append(e.matmul(ps[:, 0:16],
                                      xnT_sb[:, j * 2048 + i * 128:j * 2048 + (i + 1) * 128],
                                      ceuT[:, j * 16:(j + 1) * 16],
                                      start=(j == 0), stop=(j == 7)))
                for j in range(4):
                    r.append(e.matmul(ps[:, 16:32], hT_t[j][:, i * 128:(i + 1) * 128],
                                      gw2s[:, j * 16:(j + 1) * 16],
                                      start=(j == 0), stop=(j == 3)))
                return r
            emit("tensor", mml,
                 reads=tuple(f"hT{m}" for m in range(4)) + ("ceuT", "gw2s") + xnT_keys,
                 writes=pk(which, 0, 32), tag=f"mml{i}")
            emit("vector", lambda e, ps=ps: [
                e.tensor_tensor(t16a[:, 0:16], ps[:, 16:32], gb2bc[:, 0:16], ALU.add)],
                reads=pk(which, 0, 32) + ("gb2bc",), writes=["t16a"], tag=f"lga{i}")
            emit("scalar", lambda e: [
                e.activation(t16a[:, 0:16], t16a[:, 0:16], AF.Sigmoid)],
                reads=["t16a"], writes=["t16a"], tag=f"lgs{i}")
            emit("vector", [
                lambda e, ps=ps, i=i: [e.scalar_tensor_tensor(
                    t16c[:, 0:16], ps[:, 0:16], rnorm_tm[:, i:i + 1],
                    t16a[:, 0:16], ALU.mult, ALU.add)],
                lambda e: [e.tensor_reduce(red1[:, 0:1], t16c[:, 0:16], AX.X, ALU.max)],
                lambda e: [e.tensor_scalar(t16b[:, 0:16], t16c[:, 0:16], red1[:, 0:1], None, ALU.is_equal)],
            ], reads=pk(which, 0, 32) + ("t16a", "rnorm"),
                writes=["lgt", "msk", "red10"], tag=f"lgm{i}", seq=True)
            emit("scalar", lambda e: [
                e.activation(t16c[:, 0:16], t16c[:, 0:16], AF.Exp)],
                reads=["lgt"], writes=["lgt"], tag=f"lge{i}")
            emit("vector", [
                lambda e: [e.tensor_reduce(red2[:, 0:1], t16c[:, 0:16], AX.X, ALU.add)],
                lambda e: [e.reciprocal(red2[:, 0:1], red2[:, 0:1])],
                lambda e: [e.tensor_scalar(t16c[:, 0:16], t16c[:, 0:16], red2[:, 0:1], None, ALU.mult)],
                lambda e, i=i: [e.tensor_tensor(wts_tm[:, i * 16:(i + 1) * 16], t16c[:, 0:16],
                                                t16b[:, 0:16], ALU.mult)],
            ], reads=["lgt", "msk"], writes=["wts_tm", "red20"], tag=f"lgw{i}", seq=True)
        dma("sdma", wts_scr[:, :], wts_tm[:, :], ["wts_tm"], ["wts_scr"], tag="stwts")
        AR.release(m1)
        P.barrier()

        AR.reset()
        qs_b = [AR.alloc(T, BF16) for _ in range(2)]
        ks_b = [AR.alloc(T, BF16) for _ in range(2)]
        vs_b = [AR.alloc(T, BF16) for _ in range(2)]
        ws_pb = [[AR.alloc(T, BF16) for _ in range(2)] for _ in range(2)]
        wN_p = [AR.alloc(T, BF16) for _ in range(2)]
        Ninv_p = [AR.alloc(NB * 128, BF16) for _ in range(2)]
        Lb_p = [AR.alloc(512, BF16) for _ in range(2)]
        Lt_p = [AR.alloc(512, BF16) for _ in range(2)]
        Xt_p = [[AR.alloc(512, BF16) for _ in range(2)] for _ in range(2)]
        En_p = [[AR.alloc(512, BF16) for _ in range(2)] for _ in range(2)]
        Et_p = [[AR.alloc(512, BF16) for _ in range(2)] for _ in range(2)]
        Vsb_p = [AR.alloc(T, BF16) for _ in range(2)]
        Zsb_p = [AR.alloc(T, BF16) for _ in range(2)]
        Csb_p = [AR.alloc(T, BF16) for _ in range(2)]
        Mt_p = [AR.alloc(128, BF16) for _ in range(2)]
        PT_p = [[AR.alloc((NB - s) * 128, BF16) for s in range(NB)] for _ in range(2)]
        crecipb = AR.alloc(T, BF16, parts=1)
        orbc = AR.alloc(T, BF16)
        obf = AR.alloc(T, BF16)

        def head_pair(h):
            hp = h % 2
            qs, ks, vs = qs_b[hp], ks_b[hp], vs_b[hp]
            ws_p = ws_pb[hp]
            kqs, kks, kvs = f"qs{hp}", f"ks{hp}", f"vs{hp}"
            dma("gdma", qs, qT_scr[h * 128:(h + 1) * 128, :], ["q_scr"], [kqs], tag=f"ldq{h}")
            dma("gdma", ks, kT_scr[h * 128:(h + 1) * 128, :], ["k_scr"], [kks], tag=f"ldk{h}")
            dma("sdma", vs,
                v_scr.rearrange("(s p) (hh d) -> p s hh d", p=128, d=128)[:, :, h, :],
                ["v_scr"], [kvs], tag=f"ldv{h}")
            for r in range(2):
                n = 2 * h + r
                dma("gdma", ws_p[r], W_scr[n * 128:(n + 1) * 128, :], ["W_scr"],
                    [f"ws{r}{hp}"], tag=f"ldw{n}")

            ctx = []
            for r in range(2):
                ctx.append(dict(
                    n=2 * h + r, ws=ws_p[r], kw=f"ws{r}{hp}", wN=wN_p[r], kwN=f"wN{r}",
                    which=("A" if r == 0 else "B"), psx=(psA if r == 0 else psB),
                    Lb=Lb_p[r], Lt=Lt_p[r], Xt=Xt_p[r], En=En_p[r], Et=Et_p[r],
                    kL=f"L{r}", kLt=f"Lt{r}",
                    kXt=[f"Xt{r}0", f"Xt{r}1"], kEn=[f"En{r}0", f"En{r}1"],
                    kEt=[f"Et{r}0", f"Et{r}1"]))
            for half in range(2):
                for r in range(2):
                    d = ctx[r]
                    def trw(e, half=half, ws=d["ws"], psx=d["psx"]):
                        return [e.matmul(psx[:, jj * 128:(jj + 1) * 128],
                                         ws[:, (half * 8 + jj) * 128:(half * 8 + jj + 1) * 128],
                                         identbf[:], start=True, stop=True)
                                for jj in range(8)]
                    emit("tensor", trw, reads=[d["kw"], "identbf"],
                         writes=pk(d["which"], 0, 1024), tag=f"trw{d['n']}{half}")
                for r in range(2):
                    d = ctx[r]
                    emit("scalar", lambda e, half=half, wN=d["wN"], psx=d["psx"]: [
                        e.activation(wN[:, half * 1024:(half + 1) * 1024], psx[:, 0:1024],
                                     AF.Identity)],
                        reads=pk(d["which"], 0, 1024), writes=[d["kwN"]],
                        tag=f"evw{d['n']}{half}")
            for g in range(4):
                for r in range(2):
                    d = ctx[r]
                    def mmG(e, g=g, ws=d["ws"], psx=d["psx"]):
                        return [e.matmul(psx[:, 1024 + jj * 128:1024 + (jj + 1) * 128],
                                         ws[:, (g * 4 + jj) * 128:(g * 4 + jj + 1) * 128],
                                         ws[:, (g * 4 + jj) * 128:(g * 4 + jj + 1) * 128],
                                         start=True, stop=True) for jj in range(4)]
                    emit("tensor", mmG, reads=[d["kw"]], writes=pk(d["which"], 1024, 1536),
                         tag=f"G{d['n']}{g}")
                for r in range(2):
                    d = ctx[r]
                    def evL(e, g=g, n=d["n"], psx=d["psx"], Lb=d["Lb"]):
                        return [e.scalar_tensor_tensor(
                            Lb[:, jj * 128:(jj + 1) * 128],
                            psx[:, 1024 + jj * 128:1024 + (jj + 1) * 128],
                            beta_tm[:, (g * 4 + jj) * 16 + n:(g * 4 + jj) * 16 + n + 1],
                            maskSs[:], ALU.mult, ALU.mult) for jj in range(4)]
                    emit("vector", evL, reads=pk(d["which"], 1024, 1536) + ("beta_tm", "maskSs"),
                         writes=[d["kL"]], tag=f"evL{d['n']}{g}")
                for r in range(2):
                    d = ctx[r]
                    def trL(e, Lb=d["Lb"], psx=d["psx"]):
                        return [e.matmul(psx[:, 1536 + jj * 128:1536 + (jj + 1) * 128],
                                         Lb[:, jj * 128:(jj + 1) * 128], identbf[:],
                                         start=True, stop=True)
                                for jj in range(4)]
                    emit("tensor", trL, reads=[d["kL"], "identbf"],
                         writes=pk(d["which"], 1536, 2048), tag=f"trL{d['n']}{g}")
                for r in range(2):
                    d = ctx[r]
                    emit("vector", lambda e, Lt=d["Lt"], psx=d["psx"]: [
                        e.tensor_copy(Lt, psx[:, 1536:2048])],
                        reads=pk(d["which"], 1536, 2048), writes=[d["kLt"]],
                        tag=f"evLt{d['n']}{g}")
                for r in range(2):
                    d = ctx[r]
                    emit("vector", lambda e, Lt=d["Lt"], Xt=d["Xt"]: [
                        e.tensor_tensor(Xt[0][:, jj * 128:(jj + 1) * 128], identbf[:],
                                        Lt[:, jj * 128:(jj + 1) * 128], ALU.subtract)
                        for jj in range(4)],
                        reads=[d["kLt"], "identbf"], writes=[d["kXt"][0]],
                        tag=f"x0{d['n']}{g}")
                for r in range(2):
                    d = ctx[r]
                    def e0(e, Lb=d["Lb"], Lt=d["Lt"], psx=d["psx"]):
                        r2 = []
                        for jj in range(4):
                            s = slice(jj * 128, (jj + 1) * 128)
                            r2.append(e.matmul(psx[:, 1024 + jj * 128:1024 + (jj + 1) * 128],
                                               Lt[:, s], Lb[:, s], start=True, stop=True))
                            r2.append(e.matmul(psx[:, 1536 + jj * 128:1536 + (jj + 1) * 128],
                                               Lb[:, s], Lt[:, s], start=True, stop=True))
                        return r2
                    emit("tensor", e0, reads=[d["kL"], d["kLt"]],
                         writes=pk(d["which"], 1024, 2048), tag=f"e0{d['n']}{g}")
                for r in range(2):
                    d = ctx[r]
                    emit("vector", lambda e, En=d["En"], Et=d["Et"], psx=d["psx"]: [
                        e.tensor_copy(En[0], psx[:, 1024:1536]),
                        e.tensor_copy(Et[0], psx[:, 1536:2048]),
                    ], reads=pk(d["which"], 1024, 2048), writes=[d["kEn"][0], d["kEt"][0]],
                        tag=f"eve0{d['n']}{g}")
                for it in range(6):
                    pa, pb = it % 2, 1 - it % 2
                    for r in range(2):
                        d = ctx[r]
                        def xup(e, En=d["En"], Xt=d["Xt"], pa=pa, psx=d["psx"]):
                            return [e.matmul(psx[:, 1024 + jj * 128:1024 + (jj + 1) * 128],
                                             En[pa][:, jj * 128:(jj + 1) * 128],
                                             Xt[pa][:, jj * 128:(jj + 1) * 128],
                                             start=True, stop=True) for jj in range(4)]
                        emit("tensor", xup, reads=[d["kEn"][pa], d["kXt"][pa]],
                             writes=pk(d["which"], 1024, 1536), tag=f"xu{d['n']}{g}{it}")
                    for r in range(2):
                        d = ctx[r]
                        emit("vector", lambda e, Xt=d["Xt"], pa=pa, pb=pb, psx=d["psx"]: [
                            e.tensor_tensor(Xt[pb], psx[:, 1024:1536], Xt[pa], ALU.add)],
                            reads=pk(d["which"], 1024, 1536) + (d["kXt"][pa],),
                            writes=[d["kXt"][pb]], tag=f"xa{d['n']}{g}{it}")
                    if it < 5:
                        for r in range(2):
                            d = ctx[r]
                            def esq(e, En=d["En"], Et=d["Et"], pa=pa, psx=d["psx"]):
                                return [e.matmul(psx[:, 1536 + jj * 128:1536 + (jj + 1) * 128],
                                                 Et[pa][:, jj * 128:(jj + 1) * 128],
                                                 En[pa][:, jj * 128:(jj + 1) * 128],
                                                 start=True, stop=True) for jj in range(4)]
                            emit("tensor", esq, reads=[d["kEt"][pa], d["kEn"][pa]],
                                 writes=pk(d["which"], 1536, 2048), tag=f"en{d['n']}{g}{it}")
                        for r in range(2):
                            d = ctx[r]
                            emit("scalar", lambda e, En=d["En"], pb=pb, psx=d["psx"]: [
                                e.activation(En[pb], psx[:, 1536:2048], AF.Identity)],
                                reads=pk(d["which"], 1536, 2048), writes=[d["kEn"][pb]],
                                tag=f"evn{d['n']}{g}{it}")
                        for r in range(2):
                            d = ctx[r]
                            def etq(e, En=d["En"], Et=d["Et"], pa=pa, psx=d["psx"]):
                                return [e.matmul(psx[:, 1536 + jj * 128:1536 + (jj + 1) * 128],
                                                 En[pa][:, jj * 128:(jj + 1) * 128],
                                                 Et[pa][:, jj * 128:(jj + 1) * 128],
                                                 start=True, stop=True) for jj in range(4)]
                            emit("tensor", etq, reads=[d["kEn"][pa], d["kEt"][pa]],
                                 writes=pk(d["which"], 1536, 2048), tag=f"et{d['n']}{g}{it}")
                        for r in range(2):
                            d = ctx[r]
                            emit("vector", lambda e, Et=d["Et"], pb=pb, psx=d["psx"]: [
                                e.tensor_copy(Et[pb], psx[:, 1536:2048])],
                                reads=pk(d["which"], 1536, 2048), writes=[d["kEt"][pb]],
                                tag=f"evt{d['n']}{g}{it}")
                for r in range(2):
                    d = ctx[r]
                    def trN(e, Xt=d["Xt"], psx=d["psx"]):
                        return [e.matmul(psx[:, 1024 + jj * 128:1024 + (jj + 1) * 128],
                                         Xt[0][:, jj * 128:(jj + 1) * 128], identbf[:],
                                         start=True, stop=True)
                                for jj in range(4)]
                    emit("tensor", trN, reads=[d["kXt"][0], "identbf"],
                         writes=pk(d["which"], 1024, 1536), tag=f"trN{d['n']}{g}")
                for r in range(2):
                    d = ctx[r]
                    emit("vector", lambda e, g=g, r=r, psx=d["psx"]: [
                        e.tensor_copy(Ninv_p[r][:, g * 512:(g + 1) * 512],
                                      psx[:, 1024:1536])],
                        reads=pk(d["which"], 1024, 1536), writes=[f"Ninv{r}:{g}"],
                        tag=f"evN{d['n']}{g}")

            for i in range(NB - 1, -1, -1):
                c0 = i * 128
                pr = []
                for r in range(2):
                    pr.append(dict(
                        n=2 * h + r, which=("A" if r == 0 else "B"),
                        ps=(psA if r == 0 else psB), ws=ws_p[r], kw=f"ws{r}{hp}",
                        wN=wN_p[r], kwN=f"wN{r}", Vsb=Vsb_p[r], Zsb=Zsb_p[r],
                        Ccur=Csb_p[r], Cnxt=Csb_p[r],
                        kC=f"C{r}", kCn=f"C{r}"))
                for r in range(2):
                    d = pr[r]
                    def mmV(e, i=i, c0=c0, ws=d["ws"], Ccur=d["Ccur"], ps=d["ps"]):
                        rr = []
                        for (c, w) in spans(c0, T):
                            rr.append(e.matmul(ps[:, c:c + w], ws[:, c0:c0 + 128],
                                               qs[:, c:c + w], start=True,
                                               stop=(i == NB - 1 or c + w <= c0 + 128)))
                            if i < NB - 1 and c + w > c0 + 128:
                                cc = max(c, c0 + 128)
                                rr.append(e.matmul(ps[:, cc:c + w], ws[:, c0:c0 + 128],
                                                   Ccur[:, cc:c + w], start=False, stop=True))
                        return rr
                    emit("tensor", mmV, reads=[d["kw"], kqs, d["kC"]],
                         writes=pk(d["which"], c0, T), tag=f"V{d['n']}_{i}")
                for r in range(2):
                    d = pr[r]
                    which, ps, Vsb, n = d["which"], d["ps"], d["Vsb"], d["n"]
                    B0 = min(T, ((c0 + 128 + 511) // 512) * 512)
                    Bm = min(T, max(B0, ((c0 + (T - c0) // 2 + 511) // 512) * 512))
                    emit("vector", lambda e, Vsb=Vsb, c0=c0, ps=ps, B0=B0, Bm=Bm: [
                        e.scalar_tensor_tensor(Vsb[:, c0:c0 + 128], ps[:, c0:c0 + 128],
                                               1.0, maskKs[:], ALU.mult, ALU.mult)] +
                        ([e.tensor_copy(Vsb[:, c0 + 128:Bm], ps[:, c0 + 128:Bm])]
                         if c0 + 128 < Bm else []),
                        reads=pk(which, c0, Bm) + ("maskKs",),
                        writes=[f"Vsbd{r}"], tag=f"evVd{n}_{i}")
                    if Bm < T:
                        emit("scalar", lambda e, Vsb=Vsb, ps=ps, Bm=Bm: [
                            e.activation(Vsb[:, Bm:T], ps[:, Bm:T], AF.Identity)],
                            reads=pk(which, Bm, T), writes=[f"Vsb{r}"],
                            tag=f"evV{n}_{i}")
                for r in range(2):
                    d = pr[r]
                    vkeys = [f"Vsb{r}", f"Vsbd{r}"]
                    def mmZ(e, i=i, c0=c0, Vsb=d["Vsb"], ps=d["ps"], r=r):
                        return [e.matmul(ps[:, c:c + w],
                                         Ninv_p[r][:, i * 128:(i + 1) * 128],
                                         Vsb[:, c:c + w], start=True, stop=True)
                                for (c, w) in spans(c0, T)]
                    emit("tensor", mmZ, reads=vkeys + [f"Ninv{r}:{i // 4}"],
                         writes=pk(d["which"], c0, T), tag=f"Z{d['n']}_{i}")
                for r in range(2):
                    d = pr[r]
                    which, ps, Zsb, n = d["which"], d["ps"], d["Zsb"], d["n"]
                    Bm = min(T, ((c0 + (T - c0) // 2 + 511) // 512) * 512)
                    emit("vector", lambda e, Zsb=Zsb, c0=c0, i=i, n=n, ps=ps, Bm=Bm: [
                        e.tensor_scalar(Zsb[:, c0:Bm], ps[:, c0:Bm],
                                        betaneg_tm[:, i * 16 + n:i * 16 + n + 1],
                                        None, ALU.mult)],
                        reads=pk(which, c0, Bm) + ("betaneg_tm",), writes=[f"Zsbd{r}"],
                        tag=f"evZd{n}_{i}")
                    if Bm < T:
                        emit("scalar", lambda e, Zsb=Zsb, i=i, n=n, ps=ps, Bm=Bm: [
                            e.activation(Zsb[:, Bm:T], ps[:, Bm:T], AF.Identity,
                                         scale=betaneg_tm[:, i * 16 + n:i * 16 + n + 1])],
                            reads=pk(which, Bm, T) + ("betaneg_tm",), writes=[f"Zsb{r}"],
                            tag=f"evZ{n}_{i}")
                for r in range(2):
                    d = pr[r]
                    def mmM(e, c0=c0, ws=d["ws"], ps=d["ps"]):
                        return [e.matmul(ps[:, c0:c0 + 128], ws[:, c0:c0 + 128],
                                         ks[:, c0:c0 + 128], start=True, stop=True)]
                    emit("tensor", mmM, reads=[d["kw"], kks],
                         writes=pk(d["which"], c0, c0 + 128), tag=f"M{d['n']}_{i}")
                for r in range(2):
                    d = pr[r]
                    emit("vector", lambda e, r=r, c0=c0, ps=d["ps"]: [
                        e.tensor_tensor(Mt_p[r], ps[:, c0:c0 + 128], maskSs[:], ALU.mult)],
                        reads=pk(d["which"], c0, c0 + 128) + ("maskSs",), writes=[f"Mt{r}"],
                        tag=f"evM{d['n']}_{i}")
                for r in range(2):
                    d = pr[r]
                    def mmS(e, i=i, c0=c0, Zsb=d["Zsb"], Ccur=d["Ccur"], r=r, ps=d["ps"]):
                        rr = []
                        first = True
                        for (c, w) in spans(c0, T):
                            rr.append(e.matmul(ps[:, c:c + w], ks[:, c0:c0 + 128],
                                               qs[:, c:c + w], start=True, stop=False))
                            if first:
                                rr.append(e.matmul(ps[:, c0:c0 + 128], mbigs[:],
                                                   identbf[:], start=False, stop=False))
                                first = False
                            haveC = (i < NB - 1 and c + w > c0 + 128)
                            rr.append(e.matmul(ps[:, c:c + w], Mt_p[r],
                                               Zsb[:, c:c + w], start=False,
                                               stop=(not haveC)))
                            if haveC:
                                cc = max(c, c0 + 128)
                                rr.append(e.matmul(ps[:, cc:c + w], ks[:, c0:c0 + 128],
                                                   Ccur[:, cc:c + w], start=False, stop=True))
                        return rr
                    emit("tensor", mmS, reads=[kks, kqs, f"Mt{r}", f"Zsb{r}", f"Zsbd{r}",
                                               d["kC"], "mbigs", "identbf"],
                         writes=pk(d["which"], c0, T), tag=f"S{d['n']}_{i}")
                for r in range(2):
                    d = pr[r]
                    PTt = PT_p[r][i]
                    emit("scalar", lambda e, PTt=PTt, c0=c0, ps=d["ps"]: [
                        e.activation(PTt[:, 0:T - c0], ps[:, c0:T], AF.Exp, scale=ISQD)],
                        reads=pk(d["which"], c0, T), writes=[f"PT{r}:{i}"],
                        tag=f"exp{d['n']}_{i}")

                for r in range(2):
                    d = pr[r]
                    def mmC(e, c0=c0, wN=d["wN"], Zsb=d["Zsb"], ps=d["ps"]):
                        return [e.matmul(ps[:, c:c + w], wN[:, c0:c0 + 128],
                                         Zsb[:, c:c + w], start=True, stop=True)
                                for (c, w) in spans(c0, T)]
                    emit("tensor", mmC, reads=[d["kwN"], f"Zsb{r}", f"Zsbd{r}"],
                         writes=pk(d["which"], c0, T), tag=f"Cm{d['n']}_{i}")
                for r in range(2):
                    d = pr[r]
                    which, ps = d["which"], d["ps"]
                    Cnxt, Ccur, kC, kCn = d["Cnxt"], d["Ccur"], d["kC"], d["kCn"]
                    if i == NB - 1:
                        emit("vector", lambda e, Cnxt=Cnxt, c0=c0, ps=ps: [
                            e.tensor_copy(Cnxt[:, c0:T], ps[:, c0:T])],
                            reads=pk(which, c0, T), writes=[kCn], tag=f"Cc{d['n']}_{i}")
                    else:
                        emit("vector", lambda e, Cnxt=Cnxt, Ccur=Ccur, c0=c0, ps=ps: [
                            e.tensor_copy(Cnxt[:, c0:c0 + 128], ps[:, c0:c0 + 128]),
                            e.tensor_tensor(Cnxt[:, c0 + 128:T], ps[:, c0 + 128:T],
                                            Ccur[:, c0 + 128:T], ALU.add)],
                            reads=pk(which, c0, T) + (kC,), writes=[kCn],
                            tag=f"Cc{d['n']}_{i}")

            for r in range(2):
                n = 2 * h + r
                ptkeys = [f"PT{r}:{s}" for s in range(NB)]

                def mmcs(e, r=r):
                    rr = []
                    for s in range(NB):
                        c0s = s * 128
                        for (c, w) in spans(c0s, T):
                            slast = min(4 * (c // 512) + 3, NB - 1)
                            rr.append(e.matmul(psB[0:1, c:c + w], onescol[:],
                                               PT_p[r][s][:, c - c0s:c - c0s + w],
                                               start=(s == 0), stop=(s == slast)))
                    return rr
                emit("tensor", mmcs, reads=["ones"] + ptkeys, writes=pk("B", 0, 2048),
                     tag=f"cs{n}")
                emit("scalar", [
                    lambda e: [e.activation(psB[0:1, 0:2048], psB[0:1, 0:2048], AF.Ln)],
                    lambda e: [e.activation(crecipb, psB[0:1, 0:2048], AF.Exp, scale=-1.0)],
                ], reads=pk("B", 0, 2048), writes=["crecip"] + list(pk("B", 0, 2048)),
                    tag=f"orc{n}", seq=True)
                def mmo(e, r=r):
                    rr = []
                    for s in range(NB):
                        c0s = s * 128
                        for (c, w) in spans(c0s, T):
                            slast = min(4 * (c // 512) + 3, NB - 1)
                            rr.append(e.matmul(psA[:, c:c + w], vs[:, s * 128:(s + 1) * 128],
                                               PT_p[r][s][:, c - c0s:c - c0s + w],
                                               start=(s == 0), stop=(s == slast)))
                    return rr
                emit("tensor", mmo, reads=[kvs] + ptkeys, writes=pk("A", 0, 2048),
                     tag=f"o{n}")
                def mmrb2(e):
                    return [e.matmul(psB[:, c:c + 512], onesrow[0:1, :],
                                     crecipb[:, c:c + 512], start=True, stop=True)
                            for c in range(0, T, 512)]
                emit("tensor", mmrb2, reads=["crecip", "ones"], writes=pk("B", 0, 2048),
                     tag=f"orb{n}")
                emit("vector", lambda e: [e.tensor_copy(orbc, psB[:, 0:2048])],
                     reads=pk("B", 0, 2048), writes=["orbc"], tag=f"orbe{n}")
                emit("vector", lambda e: [
                    e.tensor_tensor(obf, psA[:, 0:2048], orbc, ALU.mult)],
                    reads=pk("A", 0, 2048) + ("orbc",), writes=["obf"], tag=f"onrm{n}")
                dma("sdma", o_scr[n * 128:(n + 1) * 128, :], obf, ["obf"], ["o_scr"],
                    tag=f"sto{n}")

        for h in range(n_pairs):
            head_pair(h)
        P.barrier()

        AR.reset()
        xnT2 = AR.alloc(NB * 128 * 8, BF16)
        xnT2_keys = tuple(f"xnT2:{j}" for j in range(8))
        for j in range(8):
            dma("gdma", xnT2[:, j * 2048:(j + 1) * 2048], xnT_scr[j * 128:(j + 1) * 128, :],
                ["xnT_scr"], [f"xnT2:{j}"], tag=f"ldxnT{j}")
        wcw_b = [AR.alloc(2 * HID, BF16) for _ in range(2)]
        bcs = AR.alloc(2 * NEXP, F32)
        wb_b = [AR.alloc(T, BF16) for _ in range(2)]
        gout_b = [AR.alloc(T, BF16) for _ in range(2)]
        wcout_b = [AR.alloc(T, BF16) for _ in range(4)]
        wtsTr = AR.alloc(T, BF16, parts=16)
        wtb = AR.alloc(NB * 16, BF16)
        dma("sdma", bcs, bcolsp[:, :], ["in"], ["bcs"], tag="ldbcols")
        emit("vector", lambda e: [e.tensor_copy(wtb, wts_tm[:, :])],
             reads=["wts_tm"], writes=["wtb"], tag="wtsbf")
        for half in range(2):
            which = "A" if half == 0 else "B"
            psx = psA if half == 0 else psB
            def trw3(e, half=half, psx=psx):
                return [e.matmul(psx[0:16, jj * 128:(jj + 1) * 128],
                                 wtb[:, (half * 8 + jj) * 16:(half * 8 + jj + 1) * 16],
                                 identbf[:], start=True, stop=True) for jj in range(8)]
            emit("tensor", trw3, reads=["wtb", "identbf"], writes=pk(which, 0, 1024),
                 tag=f"trwts{half}")
            emit("vector", lambda e, half=half, psx=psx: [
                e.tensor_copy(wtsTr[:, half * 1024:(half + 1) * 1024], psx[0:16, 0:1024])],
                reads=pk(which, 0, 1024), writes=["wtsTr"], tag=f"evwts{half}")
        for c in range(NEXP):
            p = c % 2
            wcw = wcw_b[p]
            dma("gdma", wcw.rearrange("p (j m) -> p j m", j=8),
                Wcols[c].rearrange("(j p) m -> p j m", p=128),
                ["in"], [f"wcw{p}"], tag=f"ldwc{c}")
            which = "A" if p == 0 else "B"
            ps = psA if p == 0 else psB
            def mmwb(e, c=c, ps=ps):
                return [e.matmul(ps[:, cc:cc + 512], selcs[:, c * 128:(c + 1) * 128],
                                 wtsTr[:, cc:cc + 512], start=True, stop=True)
                        for cc in range(0, T, 512)]
            emit("tensor", mmwb, reads=["wtsTr", "selcs"], writes=pk(which, 0, 2048),
                 tag=f"mmwb{c}")
            emit("vector", lambda e, wb=wb_b[p], ps=ps: [e.tensor_copy(wb, ps[:, 0:2048])],
                 reads=pk(which, 0, 2048), writes=[f"wb{p}"], tag=f"evwb{c}")
            for nt in range(2):
                which2 = "B" if p == 0 else "A"
                ps2 = psB if p == 0 else psA
                def mmcols(e, wcw=wcw, nt=nt, ps2=ps2):
                    r = []
                    for cc in range(0, T, 512):
                        for j in range(8):
                            r.append(e.matmul(
                                ps2[:, cc:cc + 512],
                                wcw[:, j * 256 + nt * 128:j * 256 + (nt + 1) * 128],
                                xnT2[:, j * 2048 + cc:j * 2048 + cc + 512],
                                start=(j == 0), stop=(j == 7)))
                    return r
                emit("tensor", mmcols, reads=(f"wcw{p}",) + xnT2_keys,
                     writes=pk(which2, 0, 2048), tag=f"mmcol{c}_{nt}")
                gout = gout_b[nt]
                emit("scalar", lambda e, gout=gout, c=c, nt=nt, ps2=ps2: [
                    e.activation(gout, ps2[:, 0:2048], AF.Gelu,
                                 bias=bcs[:, c * 2 + nt:c * 2 + nt + 1])],
                    reads=pk(which2, 0, 2048) + ("bcs",), writes=[f"gout{nt}"],
                    tag=f"gel{c}_{nt}")
                wco = wcout_b[2 * p + nt]
                emit("vector", lambda e, gout=gout, wco=wco, wb=wb_b[p]: [
                    e.tensor_tensor(wco, gout, wb, ALU.mult)],
                    reads=[f"gout{nt}", f"wb{p}"], writes=[f"wco{2 * p + nt}"],
                    tag=f"wm{c}_{nt}")
                dma("sdma", wc_scr[(c * 2 + nt) * 128:(c * 2 + nt + 1) * 128, :], wco,
                    [f"wco{2 * p + nt}"], ["wc_scr"], tag=f"stwc{c}_{nt}")
        P.barrier()

        AR.reset()
        pws = AR.alloc(32 * HID, BF16)
        wos = AR.alloc(16 * HID, BF16)
        og_b = [AR.alloc(16 * 128, BF16) for _ in range(2)]
        ao_b = [AR.alloc(HID, BF16) for _ in range(2)]
        pwkeys = tuple(f"pws{j}" for j in range(4))
        dma("gdma", wos.rearrange("p (j m) -> p j m", j=16),
            WoT.rearrange("(j p) m -> p j m", p=128), ["in"], ["wos"], tag="ldWo")
        for j in range(4):
            dma("gdma", pws[:, j * 8192:(j + 1) * 8192].rearrange("p (jj m) -> p jj m", jj=8),
                projw.rearrange("(jj p) m -> p jj m", p=128)[:, j * 8:(j + 1) * 8, :],
                ["in"], [f"pws{j}"], tag=f"ldpw{j}")
        lngbc = AR.alloc(HID, F32); lnbbc = AR.alloc(HID, F32); pbbc = AR.alloc(HID, F32)
        mrow = AR.mark()
        rowf = AR.alloc(HID, F32, parts=1)
        rowh = AR.alloc(HID, BF16, parts=1)
        rowl = AR.alloc(HID, BF16, parts=1)
        rows = AR.alloc(HID, F32, parts=1)
        AR.release(mrow)
        wcg_b = [AR.alloc(32 * 128, BF16) for _ in range(2)]
        xng_b = [AR.alloc(HID, F32) for _ in range(2)]
        xg_b = [AR.alloc(HID, F32) for _ in range(2)]
        yb_b = [AR.alloc(HID, F32) for _ in range(2)]
        yn_b = [AR.alloc(HID, F32) for _ in range(2)]
        ob_b = [AR.alloc(HID, F32) for _ in range(2)]

        def bc_row(srcap, kin, dst, kout, whichps):
            ps = psA if whichps == "A" else psB
            dma("sdma", rowf, srcap, ["in", "rowf"], ["rowf"], tag=f"ld{kout}")
            emit("vector", [
                lambda e: [e.tensor_copy(rowh, rowf)],
                lambda e: [e.tensor_copy(rows, rowh)],
                lambda e: [e.tensor_tensor(rows, rowf, rows, ALU.subtract)],
                lambda e: [e.tensor_copy(rowl, rows)],
            ], reads=["rowf"], writes=["rowhl", "rowf"], tag=f"hl{kout}", seq=True)
            def mm(e):
                r = []
                for cc in range(0, HID, 512):
                    r.append(e.matmul(ps[:, cc:cc + 512], onesrow[0:1, :],
                                      rowh[:, cc:cc + 512], start=True, stop=False))
                    r.append(e.matmul(ps[:, cc:cc + 512], onesrow[0:1, :],
                                      rowl[:, cc:cc + 512], start=False, stop=True))
                return r
            emit("tensor", mm, reads=["rowhl", "ones"], writes=pk(whichps, 0, 1024),
                 tag=f"bc{kout}")
            emit("vector", lambda e, dst=dst: [e.tensor_copy(dst, ps[:, 0:1024])],
                 reads=pk(whichps, 0, 1024), writes=[kout], tag=f"ev{kout}")

        bc_row(projb[0:1, :], "projbr", pbbc, "pbbc", "A")
        bc_row(lng[0:1, :], "lngr", lngbc, "lngbc", "B")
        bc_row(lnb[0:1, :], "lnbr", lnbbc, "lnbbc", "A")
        P.barrier()
        if "bc_scr" in debug_taps:
            dma("sdma", bc_scr[0:128, :], pbbc, ["pbbc"], ["bc_scr"], tag="dbg_bc0")
            dma("sdma", bc_scr[128:256, :], lngbc, ["lngbc"], ["bc_scr"], tag="dbg_bc1")
            dma("sdma", bc_scr[256:384, :], lnbbc, ["lnbbc"], ["bc_scr"], tag="dbg_bc2")

        for i in range(NB):
            p = i % 2
            which = "A" if p == 0 else "B"
            ps = psA if p == 0 else psB
            wcg, xng, xg = wcg_b[p], xng_b[p], xg_b[p]
            yb, yn, ob = yb_b[p], yn_b[p], ob_b[p]
            dma("sdma", wcg,
                wc_scr.rearrange("(kk p) t -> p kk t", p=128)[:, :, i * 128:(i + 1) * 128],
                ["wc_scr"], [f"wcg{p}"], tag=f"ldwcg{i}")
            dma("sdma", xng, xn_scr[i * 128:(i + 1) * 128, :], ["xn_scr"], [f"xng{p}"],
                tag=f"ldxng{i}")
            dma("gdma", xg, x_in[i * 128:(i + 1) * 128, :], ["in"], [f"xg{p}"],
                tag=f"ldxg{i}")
            og = og_b[p]; ao = ao_b[p]
            dma("sdma", og,
                o_scr.rearrange("(nn pp) t -> pp nn t", pp=128)[:, :, i * 128:(i + 1) * 128],
                ["o_scr"], [f"og{p}"], tag=f"ldog{i}")
            def mmy(e, wcg=wcg, ps=ps):
                r = []
                for cc in range(0, HID, 512):
                    for j in range(32):
                        r.append(e.matmul(ps[:, cc:cc + 512], wcg[:, j * 128:(j + 1) * 128],
                                          pws[:, j * 1024 + cc:j * 1024 + cc + 512],
                                          start=(j == 0), stop=(j == 31)))
                return r
            emit("tensor", mmy, reads=(f"wcg{p}",) + pwkeys, writes=pk(which, 0, 1024),
                 tag=f"mmy{i}")
            def mmwo(e, og=og, ps=ps):
                r = []
                for cc in range(0, HID, 512):
                    for j in range(NH):
                        r.append(e.matmul(ps[:, 1024 + cc:1024 + cc + 512],
                                          og[:, j * 128:(j + 1) * 128],
                                          wos[:, j * 1024 + cc:j * 1024 + cc + 512],
                                          start=(j == 0), stop=(j == NH - 1)))
                return r
            emit("tensor", mmwo, reads=[f"og{p}", "wos"], writes=pk(which, 1024, 2048),
                 tag=f"mmwo{i}")
            emit("scalar", lambda e, ao=ao, ps=ps: [
                e.activation(ao, ps[:, 1024:2048], AF.Identity)],
                reads=pk(which, 1024, 2048), writes=[f"ao{p}"], tag=f"evwo{i}")
            emit("vector", [
                lambda e, yb=yb, ps=ps: [e.scalar_tensor_tensor(
                    yb, ps[:, 0:1024], 1.0, pbbc, ALU.mult, ALU.add)],
                lambda e, yb=yb, xng=xng: [e.tensor_tensor(yb, yb, xng, ALU.add)],
            ], reads=pk(which, 0, 1024) + (f"xng{p}", "pbbc"), writes=[f"yb{p}"],
                tag=f"yb{i}", seq=True)
            if "y_scr" in debug_taps:
                dma("sdma", y_scr[i * 128:(i + 1) * 128, :], yb, [f"yb{p}"], ["y_scr"],
                    tag=f"dbg_y{i}")
            R1 = red1[:, p:p + 1]
            R2 = red2[:, p:p + 1]
            kr1, kr2 = f"red1{p}", f"red2{p}"
            emit("vector", [
                lambda e, yb=yb, R1=R1: [e.tensor_reduce(R1, yb, AX.X, ALU.add)],
                lambda e, R1=R1: [e.tensor_scalar(R1, R1, 1.0 / HID, None, ALU.mult)],
                lambda e, yb=yb, yn=yn, R1=R1: [e.tensor_scalar(yn, yb, R1, None, ALU.subtract)],
            ], reads=[f"yb{p}"], writes=[f"yn{p}", kr1], tag=f"ylm{i}", seq=True)
            emit("scalar", lambda e, yn=yn, ob=ob: [e.activation(ob, yn, AF.Square)],
                 reads=[f"yn{p}"], writes=[f"ob{p}"], tag=f"yls{i}")
            emit("vector", [
                lambda e, ob=ob, R2=R2: [e.tensor_reduce(R2, ob, AX.X, ALU.add)],
                lambda e, R2=R2: [e.tensor_scalar(R2, R2, 1.0 / HID, None, ALU.mult)],
            ], reads=[f"ob{p}"], writes=[kr2], tag=f"ylv{i}", seq=True)
            emit("scalar", [
                lambda e, R2=R2: [e.activation(R2, R2, AF.Ln, bias=eps_ln[:])],
                lambda e, R2=R2: [e.activation(R2, R2, AF.Exp, scale=-0.5)],
            ], reads=[kr2, "eps"], writes=[kr2], tag=f"ylr{i}", seq=True)
            emit("vector", [
                lambda e, yn=yn, R2=R2: [e.tensor_scalar(yn, yn, R2, None, ALU.mult)],
                lambda e, yn=yn: [e.tensor_tensor(yn, yn, lngbc, ALU.mult)],
                lambda e, yn=yn, ob=ob: [e.tensor_tensor(ob, yn, lnbbc, ALU.add)],
                lambda e, ob=ob, xg=xg: [e.tensor_tensor(ob, ob, xg, ALU.add)],
                lambda e, ob=ob, ao=ao: [e.tensor_tensor(ob, ob, ao, ALU.add)],
            ], reads=[f"yn{p}", kr2, "lngbc", "lnbbc", f"xg{p}", f"ao{p}"],
                writes=[f"ob{p}", f"yn{p}"], tag=f"yf{i}", seq=True)
            dma("sdma", out[i * 128:(i + 1) * 128, :], ob, [f"ob{p}"], ["out"],
                tag=f"stout{i}")

        def body(ename):
            def run(eng):
                for waits, fn, tag, drain, seq, sk, tv in P.groups[ename]:
                    for dk, val in waits:
                        eng.wait_ge(sems[dk], val)
                    if drain:
                        eng.drain()
                    if seq:
                        instrs = None
                        for ix, sub in enumerate(fn):
                            if ix:
                                eng.drain()
                            instrs = sub(eng)
                    else:
                        instrs = fn(eng)
                    assert instrs, f"empty group {tag} on {ename}"
                    if ename in ("tensor", "vector", "scalar", "pool"):
                        instrs[-1].then_inc(sems[sk], 1)
                    else:
                        instrs[-1].then_inc(sems[sk], 16)
                if ename == "sdma":
                    for qe in ("sdma", "gdma"):
                        for j in range(Prog.NSEM):
                            k2 = f"{qe}{j}"
                            u = P.sem_uses.get(k2, 0)
                            if u:
                                eng.wait_ge(sems[k2], u * 16)
            return run

        block.tensor(body("tensor"))
        block.vector(body("vector"))
        block.scalar(body("scalar"))
        block.gpsimd(body("gdma"))
        block.sync(body("sdma"))

    return nc

def host_inputs(inp):
    f = np.ascontiguousarray
    d = {}
    d["WqT"] = f(inp["L0_attn_Wq"].T)
    d["WkT"] = f(inp["L0_attn_Wk"].T)
    d["WvT"] = f(inp["L0_attn_Wv"].T)
    d["WoT"] = f(inp["L0_attn_Wo"].T)
    d["wA1T"] = f(inp["L0_attn_wA1"].T)
    d["wA2T"] = f(inp["L0_attn_wA2"].T)
    d["wB1T"] = f(inp["L0_attn_wB1"].T)
    d["wB2T"] = f(inp["L0_attn_wB2"].T)
    d["convAp"] = f(inp["L0_attn_convA"].reshape(16, 128, 3).transpose(1, 0, 2).reshape(128, 48))
    d["convBp"] = f(inp["L0_attn_convB"].reshape(16, 128, 3).transpose(1, 0, 2).reshape(128, 48))
    d["btwT"] = f(inp["L0_attn_bt_w"].T)
    d["btb"] = f(inp["L0_attn_bt_b"].reshape(16, 1))
    d["phi"] = f(inp["L0_attn_phi"].reshape(2, 1))
    d["omr"] = f(inp["L0_attn_omega_raw"].reshape(2, 1))
    d["colemb"] = f(inp["L0_ffn_col_emb"])
    d["gw1"] = f(inp["L0_ffn_gate_w1"])
    d["gb1p"] = f(inp["L0_ffn_gate_b1"].reshape(4, 128).T)
    d["gw2"] = f(inp["L0_ffn_gate_w2"])
    d["gb2"] = f(inp["L0_ffn_gate_b2"].reshape(1, 16))
    d["Wcols"] = f(inp["L0_ffn_W_cols"])
    d["bcolsp"] = f(inp["L0_ffn_b_cols"].reshape(16, 2, 128).transpose(2, 0, 1).reshape(128, 32))
    d["projw"] = f(inp["L0_ffn_proj_w"])
    d["projb"] = f(inp["L0_ffn_proj_b"].reshape(1, HID))
    d["lng"] = f(inp["L0_ffn_ln_g"].reshape(1, HID))
    d["lnb"] = f(inp["L0_ffn_ln_b"].reshape(1, HID))
    d["identin"] = np.eye(128, dtype=np.float32)
    d["maskSin"] = np.tril(np.ones((128, 128), np.float32), -1)
    d["maskKin"] = np.triu(np.ones((128, 128), np.float32))
    d["pos4in"] = np.broadcast_to(np.arange(T, dtype=np.float32), (4, T)).copy()
    sel = np.zeros((16, 16 * 128), np.float32)
    for c in range(16):
        sel[c, c * 128:(c + 1) * 128] = 1.0
    d["selcin"] = sel
    sel4 = np.zeros((4, 4 * 128), np.float32)
    for c in range(4):
        sel4[c, c * 128:(c + 1) * 128] = 1.0
    d["sel4in"] = sel4
    d["dup24in"] = np.array([[1, 1, 0, 0], [0, 0, 1, 1]], np.float32)
    mb = np.zeros((128, 128), np.float32)
    mb[np.triu_indices(128, 1)] = -1e6
    d["mbigin"] = mb
    hp = float(np.float32(0.5 * math.pi))
    d["phsh4in"] = np.array([[hp], [0.0], [hp], [0.0]], np.float32)
    return d

def kernel(**inputs):
    shared = host_inputs(inputs)
    x = np.ascontiguousarray(inputs["L0_attn_hidden_states"].astype(np.float32))
    nc = build_nc()
    in_maps = []
    for b in range(8):
        m = dict(shared)
        m["x"] = np.ascontiguousarray(x[b])
        in_maps.append(m)
    res = run_bass_kernel_spmd(nc, in_maps, core_ids=list(range(8)))
    out = np.stack([res.results[b]["out"] for b in range(8)], axis=0)
    return out.astype(np.float32)

